# version 07
# baseline (speedup 1.0000x reference)
import contextlib
import numpy as np
import ml_dtypes
import concourse.bass as bass
import concourse.mybir as mybir
from concourse.bass_utils import run_bass_kernel_spmd

F32 = mybir.dt.float32
BF16 = mybir.dt.bfloat16
AF = mybir.ActivationFunctionType
ALU = mybir.AluOpType
AX = mybir.AxisListType.X

T = 2048
D = 1024
NT = T // 128
ND = D // 128
NH = 16
DH = 64
NE = 8
FF = 4096
DI = 2048
NDI = DI // 128
NS = 64
KC = 4

nbf = ml_dtypes.bfloat16


class Prog:
    def __init__(self, name="prog"):
        self.nc = bass.Bass("TRN2", target_bir_lowering=False)
        self.stack = contextlib.ExitStack()
        self.scopes = [self.stack]
        self.cnt = {}
        self.sems = {}
        for s in ("L", "S", "P", "V", "A", "G", "S2", "L2", "L3", "L4",
                  "L5", "G2"):
            self.sems[s] = self.stack.enter_context(self.nc.semaphore(s))
            self.cnt[s] = 0

    prefix = ""

    def push_scope(self):
        self.scopes.append(contextlib.ExitStack())

    def pop_scope(self):
        self.scopes.pop().close()

    def sbuf(self, name, shape, dt):
        return self.scopes[-1].enter_context(
            self.nc.sbuf_tensor(self.prefix + name, shape, dt))

    def psum(self, name, shape, dt):
        return self.scopes[-1].enter_context(
            self.nc.psum_tensor(self.prefix + name, shape, dt))

    def dram(self, name, shape, dt, kind=None):
        if kind is None:
            return self.nc.dram_tensor(name, shape, dt).ap()
        return self.nc.dram_tensor(name, shape, dt, kind=kind).ap()

    def inc(self, inst, sem, n=1):
        inst.then_inc(self.sems[sem], n)
        self.cnt[sem] += n
        return inst

    def ld(self, out, in_, sem="L", **kw):
        self.inc(self.nc.sync.dma_start(out=out, in_=in_, **kw), sem, 16)
        return self.cnt[sem]

    def st(self, out, in_, sem="S", **kw):
        self.inc(self.nc.gpsimd.dma_start(out=out, in_=in_, **kw), sem, 16)
        return self.cnt[sem]

    def wait(self, eng, sem, val=None):
        if val is None:
            val = self.cnt[sem]
        if val > 0:
            eng.wait_ge(self.sems[sem], val)

    def bar(self):
        self.nc.all_engine_barrier()

    def emit_consts(self):
        c = self.sbuf("consts", [128, 8], F32)
        v = self.nc.vector
        for i, val in enumerate([-1.0, 1.0, -0.5, 1.0 / D, 1e-5, 1e-6,
                                 1.0 / DI, 0.0]):
            v.memset(c[:, i:i + 1], val)
        ident = self.sbuf("identity", [128, 128], BF16)
        g = self.nc.gpsimd
        self.inc(g.iota(ident[:, :], [[1, 128]], channel_multiplier=-1,
                        allow_small_or_imprecise_dtypes=True), "G")
        self.wait(g, "G")
        g.tensor_single_scalar(out=ident[:], in_=ident[:],
                               scalar=0.0, op=ALU.is_equal)
        self.consts = c
        self.ident = ident
        self.bar()

    def C(self, i):
        return self.consts[:, i:i + 1]

    def done(self):
        self.stack.close()
        return self.nc


def emit_weight_transpose(p, scr, src, dst, R, C, scale_sb=None):
    nc = p.nc
    assert C <= 2048 and R % 128 == 0 and C % 128 == 0
    nr, ncc = R // 128, C // 128
    stgf, otile, pst = scr["stgf"], scr["otile"], scr["pst"]
    v, t, g = nc.vector, nc.tensor, nc.gpsimd
    W = 2048
    cast_marks = [p.cnt["V"]] * 2
    evac_marks = [p.cnt["V"]] * 2
    store_marks = [p.cnt["S"], p.cnt["S2"]]
    for r in range(nr):
        b = r % 2
        lsem = "L" if b == 0 else "L2"
        p.wait(nc.sync, "V", cast_marks[b])
        p.ld(stgf[:, b * W:b * W + C], src[r * 128:(r + 1) * 128, :],
             sem=lsem)
        p.wait(v, lsem)
        p.wait(v, "S" if b == 0 else "S2", store_marks[b])
        p.inc(v.tensor_copy(out=otile[:, b * W:b * W + C],
                            in_=stgf[:, b * W:b * W + C]), "V")
        cast_marks[b] = p.cnt["V"]
        p.wait(t, "V")
        p.wait(t, "V", evac_marks[b])
        for cc in range(ncc):
            inst = t.transpose(pst[b][:, cc * 128:(cc + 1) * 128],
                               otile[:, b * W + cc * 128:b * W + (cc + 1) * 128],
                               p.ident[:, :])
        p.inc(inst, "P")
        p.wait(v, "P")
        if scale_sb is None:
            p.inc(v.tensor_copy(out=otile[:, b * W:b * W + C],
                                in_=pst[b][:, 0:C]), "V")
        else:
            for cc in range(ncc):
                inst = v.tensor_scalar(
                    out=otile[:, b * W + cc * 128:b * W + (cc + 1) * 128],
                    in0=pst[b][:, cc * 128:(cc + 1) * 128],
                    scalar1=scale_sb[:, cc:cc + 1], scalar2=None,
                    op0=ALU.mult)
            p.inc(inst, "V")
        evac_marks[b] = p.cnt["V"]
        p.wait(g, "V")
        ssem = "S" if b == 0 else "S2"
        for cc in range(ncc):
            p.st(dst[cc * 128:(cc + 1) * 128, r * 128:(r + 1) * 128],
                 otile[:, b * W + cc * 128:b * W + (cc + 1) * 128], sem=ssem)
        store_marks[b] = p.cnt[ssem]
    p.bar()


def emit_ln_to_fm(p, x_dram, xfm_dram):
    nc = p.nc
    v, s, t, g = nc.vector, nc.scalar, nc.tensor, nc.gpsimd
    xt = p.sbuf("ln_x", [128, 2 * D], F32)
    xnb = p.sbuf("ln_xnb", [128, 2 * D], BF16)
    trash = p.sbuf("ln_trash", [128, D], BF16)
    stat = p.sbuf("ln_stat", [128, 16], F32)
    pst = [p.psum(f"ln_ps{i}", [128, D], BF16) for i in range(2)]
    xread_marks = [0, 0]
    evac_marks = [0, 0]
    store_marks = [0, 0]
    for tt in range(NT):
        b = tt % 2
        o = 8 * b
        lsem = "L" if b == 0 else "L2"
        p.wait(nc.sync, "V", xread_marks[b])
        ldv = p.ld(xt[:, b * D:(b + 1) * D],
                   x_dram[tt * 128:(tt + 1) * 128, :], sem=lsem)
        xv = xt[:, b * D:(b + 1) * D]
        p.wait(v, lsem, ldv)
        p.inc(v.reduce_sum(stat[:, o:o + 1], xv, axis=AX), "V")
        p.wait(s, lsem, ldv)
        p.wait(s, "V", xread_marks[b])
        p.inc(s.activation(out=trash[:, :], in_=xv, func=AF.Square,
                           accum_out=stat[:, o + 1:o + 2]), "A")
        sq_mark = p.cnt["A"]
        p.wait(v, "A", sq_mark)
        p.wait(v, "V")
        p.inc(v.tensor_scalar(out=stat[:, o + 3:o + 4], in0=stat[:, o:o + 1],
                              scalar1=-1.0 / D, scalar2=None, op0=ALU.mult),
              "V")
        p.wait(v, "V")
        v.tensor_tensor(out=stat[:, o + 6:o + 7], in0=stat[:, o + 3:o + 4],
                        in1=stat[:, o + 3:o + 4], op=ALU.mult)
        p.inc(v.tensor_scalar(out=stat[:, o + 4:o + 5],
                              in0=stat[:, o + 1:o + 2],
                              scalar1=1.0 / D, scalar2=None, op0=ALU.mult),
              "V")
        p.wait(v, "V")
        p.inc(v.tensor_tensor(out=stat[:, o + 5:o + 6],
                              in0=stat[:, o + 4:o + 5],
                              in1=stat[:, o + 6:o + 7], op=ALU.subtract), "V")
        var_mark = p.cnt["V"]
        p.wait(s, "V", var_mark)
        p.inc(s.activation(out=stat[:, o + 2:o + 3], in_=stat[:, o + 5:o + 6],
                           func=AF.Ln, bias=p.C(4)), "A")
        p.wait(s, "A")
        p.inc(s.activation(out=stat[:, o + 2:o + 3], in_=stat[:, o + 2:o + 3],
                           func=AF.Exp, scale=p.C(2)), "A")
        p.wait(v, "A")
        p.wait(v, "S" if b == 0 else "S2", store_marks[b])
        p.inc(v.tensor_scalar(out=xnb[:, b * D:(b + 1) * D], in0=xv,
                              scalar1=stat[:, o + 3:o + 4],
                              scalar2=stat[:, o + 2:o + 3],
                              op0=ALU.add, op1=ALU.mult), "V")
        xread_marks[b] = p.cnt["V"]
        p.wait(t, "V")
        p.wait(t, "V", evac_marks[b])
        for dc in range(ND):
            inst = t.transpose(pst[b][:, dc * 128:(dc + 1) * 128],
                               xnb[:, b * D + dc * 128:b * D + (dc + 1) * 128],
                               p.ident[:, :])
        p.inc(inst, "P")
        p.wait(v, "P")
        p.inc(v.tensor_copy(out=xnb[:, b * D:(b + 1) * D], in_=pst[b][:, :]),
              "V")
        evac_marks[b] = p.cnt["V"]
        p.wait(g, "V")
        ssem = "S" if b == 0 else "S2"
        for dc in range(ND):
            p.st(xfm_dram[dc * 128:(dc + 1) * 128, tt * 128:(tt + 1) * 128],
                 xnb[:, b * D + dc * 128:b * D + (dc + 1) * 128], sem=ssem)
        store_marks[b] = p.cnt[ssem]
    p.bar()


def emit_stage_p(p, io):
    nc = p.nc
    x0 = io["x0"]
    wq = io["wq"]
    wk = io["wk"]
    wv = io["wv"]
    wo = io["wo"]
    ipw = io["ipw"]
    nw = io["nw"]
    dtw = io["dtw"]
    xpw2 = io["xpw2"]
    opw = io["opw"]
    alog = io["alog"]

    wqT = io["wqT"]
    wkT = io["wkT"]
    wvT = io["wvT"]
    woT = io["woT"]
    iprojT = io["iprojT"]
    dtwT = io["dtwT"]
    xpT = io["xpT"]
    opT = io["opT"]
    asb_d = io["asb"]
    xn0fm = io["xn0fm"]


    v, s = nc.vector, nc.scalar
    at = p.sbuf("a_t", [128, NDI * NS], F32)
    p.ld(at[:, :].rearrange("p (c n) -> p c n", n=NS),
         alog.rearrange("(c p) n -> p c n", p=128),
         allow_slow_non_contiguous=True)
    p.wait(s, "L")
    p.inc(s.activation(out=at[:, :], in_=at[:, :], func=AF.Exp), "A")
    p.wait(v, "A")
    p.inc(v.tensor_scalar(out=at[:, :], in0=at[:, :], scalar1=-1.0,
                          scalar2=None, op0=ALU.mult), "V")
    p.wait(nc.gpsimd, "V")
    p.st(asb_d[:, :], at[:, :])
    nwt = p.sbuf("nw_t", [128, ND], F32)
    p.ld(nwt[:, :], nw.rearrange("(c p) -> p c", p=128),
         allow_slow_non_contiguous=True, sem="L2")
    p.wait(v, "L2")
    p.bar()

    scr = dict(
        stgf=p.sbuf("wt_stgf", [128, 2 * 2048], F32),
        otile=p.sbuf("wt_ot", [128, 2 * 2048], BF16),
        pst=[p.psum(f"wt_ps{i}", [128, 2048], BF16) for i in range(2)],
    )
    emit_weight_transpose(p, scr, wq, wqT, D, D)
    emit_weight_transpose(p, scr, wk, wkT, D, D)
    emit_weight_transpose(p, scr, wv, wvT, D, D)
    emit_weight_transpose(p, scr, wo, woT, D, D)
    emit_weight_transpose(p, scr, ipw, iprojT, 2 * DI, D, scale_sb=nwt)
    emit_weight_transpose(p, scr, dtw, dtwT, DI, DI)
    emit_weight_transpose(p, scr, opw, opT, D, DI)

    t, g = nc.tensor, nc.gpsimd
    xp_stgf = scr["stgf"]
    xp_stgb = scr["otile"]
    xp_ot_full = scr["otile"]
    xp_ps = scr["pst"][0]
    p.wait(v, "S")
    p.wait(v, "S2")
    p.inc(v.memset(xp_stgb[:, 0:DI], 0.0), "V")
    p.ld(xp_stgf[0:64, 0:DI], xpw2[:, :])
    p.wait(v, "L")
    p.wait(v, "V")
    p.inc(v.tensor_copy(out=xp_stgb[0:64, 0:DI], in_=xp_stgf[0:64, 0:DI]),
          "V")
    p.wait(t, "V")
    for cc in range(NDI):
        inst = t.transpose(xp_ps[:, cc * 128:(cc + 1) * 128],
                           xp_stgb[:, cc * 128:(cc + 1) * 128],
                           p.ident[:, :])
    p.inc(inst, "P")
    p.wait(v, "P")
    for cc in range(NDI):
        inst = v.tensor_copy(out=xp_ot_full[:, 2048 + cc * NS: 2048 + (cc + 1) * NS],
                             in_=xp_ps[:, cc * 128:cc * 128 + NS])
    p.inc(inst, "V")
    p.wait(g, "V")
    for cc in range(NDI):
        p.st(xpT[cc * 128:(cc + 1) * 128, :],
             xp_ot_full[:, 2048 + cc * NS: 2048 + (cc + 1) * NS])
    p.bar()

    emit_ln_to_fm(p, x0, xn0fm)

    g = nc.gpsimd
    p.wait(g, "S")
    p.wait(g, "L")
def build_stage_p():
    p = Prog("p")
    io = {}
    io["x0"] = p.dram("x0", [T, D], F32, "ExternalInput")
    io["xpw2"] = p.dram("xpw2", [NS, DI], F32, "ExternalInput")
    io["wq"] = p.dram("wq", [D, D], F32, "ExternalInput")
    io["wk"] = p.dram("wk", [D, D], F32, "ExternalInput")
    io["wv"] = p.dram("wv", [D, D], F32, "ExternalInput")
    io["wo"] = p.dram("wo", [D, D], F32, "ExternalInput")
    io["ipw"] = p.dram("ipw", [2 * DI, D], F32, "ExternalInput")
    io["nw"] = p.dram("nw", [D], F32, "ExternalInput")
    io["dtw"] = p.dram("dtw", [DI, DI], F32, "ExternalInput")
    io["opw"] = p.dram("opw", [D, DI], F32, "ExternalInput")
    io["alog"] = p.dram("alog", [DI, NS], F32, "ExternalInput")
    io["wqT"] = p.dram("wqT", [D, D], BF16, "ExternalOutput")
    io["wkT"] = p.dram("wkT", [D, D], BF16, "ExternalOutput")
    io["wvT"] = p.dram("wvT", [D, D], BF16, "ExternalOutput")
    io["woT"] = p.dram("woT", [D, D], BF16, "ExternalOutput")
    io["iprojT"] = p.dram("iprojT", [D, 2 * DI], BF16, "ExternalOutput")
    io["dtwT"] = p.dram("dtwT", [DI, DI], BF16, "ExternalOutput")
    io["xpT"] = p.dram("xpT", [DI, NS], BF16, "ExternalOutput")
    io["opT"] = p.dram("opT", [DI, D], BF16, "ExternalOutput")
    io["asb"] = p.dram("asb", [128, NDI * NS], F32, "ExternalOutput")
    io["xn0fm"] = p.dram("xn0fm", [D, T], BF16, "ExternalOutput")
    p.emit_consts()
    p.push_scope()
    emit_stage_p(p, io)
    p.pop_scope()
    return p.done()



def emit_stage_a(p, io):
    nc = p.nc
    v, s, t, g = nc.vector, nc.scalar, nc.tensor, nc.gpsimd
    xn0fm = io["xn0fm"]
    wqT = io["wqT"]
    wkT = io["wkT"]
    wvT = io["wvT"]
    woT = io["woT"]
    bq = io["bq"]
    bk = io["bk"]
    bv = io["bv"]
    bo = io["bo"]
    scale128 = io["scale128"]
    s0_d = io["s0"]


    xn0 = p.sbuf("xn0", [128, ND * T], BF16)
    wbuf = p.sbuf("wbuf", [128, ND * D], BF16)
    wq_sb = wk_sb = wv_sb = wo_sb = wbuf
    q_sb = p.sbuf("q_sb", [128, ND * T], BF16)
    k_sb = p.sbuf("k_sb", [128, ND * T], BF16)
    v_sb = p.sbuf("v_sb", [128, NT * D], BF16)
    bias_sb = p.sbuf("bias_sb", [128, 4 * ND], F32)
    scale_sb = p.sbuf("scale_sb", [128, 1], F32)
    brow = p.sbuf("brow", [1, 2 * D], BF16)
    browf = p.sbuf("browf", [1, 2 * D], F32)
    bvb = p.sbuf("bv_bcast", [128, D], F32)
    bob = p.sbuf("bo_bcast", [128, D], F32)
    ones1 = p.sbuf("ones1", [1, 128], BF16)
    maskc = p.sbuf("maskc", [128, 128], F32)
    mk_t = p.sbuf("mk_t", [128, 3 * 128], F32)
    mk_b = p.sbuf("mk_b", [128, 2], F32)
    probs = p.sbuf("probs", [128, 2 * T], BF16)
    probsT = p.sbuf("probsT", [128, T], BF16)
    ao_sb = p.sbuf("ao_sb", [128, ND * 128], BF16)
    att_st = p.sbuf("att_st", [128, 8], F32)
    s0t = p.sbuf("s0t", [128, 2 * D], F32)

    ps_sc = p.psum("ps_sc", [128, T], F32)
    ps_tr = p.psum("ps_tr", [128, T], BF16)
    ps_av = p.psum("ps_av", [128, 128], F32)
    ps_s0 = p.psum("ps_s0", [128, 512], F32)

    for dc in range(ND):
        p.ld(xn0[:, dc * T:(dc + 1) * T], xn0fm[dc * 128:(dc + 1) * 128, :])
    p.ld(wbuf[:, :].rearrange("p (c o) -> p c o", c=ND),
         wqT.rearrange("(c p) o -> p c o", p=128))
    p.ld(bias_sb[:, 0:ND], bq.rearrange("(c p) -> p c", p=128),
         allow_slow_non_contiguous=True)
    p.ld(bias_sb[:, ND:2 * ND], bk.rearrange("(c p) -> p c", p=128),
         allow_slow_non_contiguous=True)
    p.ld(scale_sb[:, 0:1], scale128[:, None])
    p.ld(browf[0:1, 0:D], bv[None, :])
    p.ld(browf[0:1, D:2 * D], bo[None, :])
    p.inc(g.iota(mk_t[:, 0:128], [[1, 2], [0, 64]], channel_multiplier=0,
                 allow_small_or_imprecise_dtypes=True), "G")
    p.inc(g.iota(mk_t[:, 128:256], [[1, 128]], channel_multiplier=-1,
                 allow_small_or_imprecise_dtypes=True), "G")
    p.inc(g.iota(mk_b[:, 0:1], [[0, 1]], channel_multiplier=1,
                 allow_small_or_imprecise_dtypes=True), "G")
    p.wait(v, "G")
    p.inc(v.tensor_scalar(out=mk_b[:, 1:2], in0=mk_b[:, 0:1], scalar1=63.5,
                          scalar2=None, op0=ALU.is_gt), "V")
    p.wait(v, "V")
    p.inc(v.tensor_scalar(out=mk_t[:, 256:384], in0=mk_t[:, 0:128],
                          scalar1=mk_b[:, 1:2], scalar2=None, op0=ALU.is_gt),
          "V")
    p.wait(v, "V")
    p.inc(v.tensor_scalar(out=mk_t[:, 0:128], in0=mk_t[:, 0:128],
                          scalar1=mk_b[:, 1:2], scalar2=None,
                          op0=ALU.is_equal), "V")
    p.wait(v, "V")
    p.inc(v.tensor_single_scalar(out=mk_t[:, 128:256], in_=mk_t[:, 128:256],
                                 scalar=0.5, op=ALU.is_le), "V")
    p.wait(v, "V")
    p.inc(v.tensor_tensor(out=mk_t[:, 0:128], in0=mk_t[:, 0:128],
                          in1=mk_t[:, 128:256], op=ALU.mult), "V")
    p.wait(v, "V")
    p.inc(v.tensor_tensor(out=mk_t[:, 0:128], in0=mk_t[:, 0:128],
                          in1=mk_t[:, 256:384], op=ALU.max), "V")
    p.wait(v, "V")
    p.inc(v.tensor_scalar(out=maskc[:, :], in0=mk_t[:, 0:128], scalar1=-1.0,
                          scalar2=1e30, op0=ALU.add, op1=ALU.mult), "V")
    p.wait(v, "L")
    p.inc(v.tensor_scalar(out=bias_sb[:, 0:ND], in0=bias_sb[:, 0:ND],
                          scalar1=scale_sb[:, 0:1], scalar2=None,
                          op0=ALU.mult), "V")
    v.memset(ones1[0:1, :], 1.0)
    p.wait(v, "V")
    p.inc(v.tensor_copy(out=brow[0:1, :], in_=browf[0:1, :]), "V")
    p.wait(t, "V")
    for j, (dst, off) in enumerate(((bvb, 0), (bvb, 512), (bob, D),
                                    (bob, D + 512))):
        inst = t.matmul(ps_sc[:, (j % 4) * 512:(j % 4) * 512 + 512],
                        ones1[0:1, :], brow[0:1, off:off + 512],
                        start=True, stop=True)
    p.inc(inst, "P")
    p.wait(v, "P")
    v.tensor_copy(out=bvb[:, :], in_=ps_sc[:, 0:1024])
    p.inc(v.tensor_copy(out=bob[:, :], in_=ps_sc[:, 1024:2048]), "V")
    p.bar()

    def proj_qk(out_sb, bias_off, use_scale):
        evac_marks = [0, 0]
        for och in range(ND):
            for tch in range(4):
                half = (och * 4 + tch) % 2
                pso = ps_s0 if half == 0 else ps_sc
                psv = pso[:, 0:512]
                p.wait(t, "V", evac_marks[half])
                for dch in range(ND):
                    inst = t.matmul(
                        psv,
                        wbuf[:, dch * D + och * 128:dch * D + (och + 1) * 128],
                        xn0[:, dch * T + tch * 512:dch * T + (tch + 1) * 512],
                        start=(dch == 0), stop=(dch == ND - 1))
                p.inc(inst, "P")
                p.wait(v, "P")
                if use_scale:
                    inst = v.tensor_scalar(
                        out=out_sb[:, och * T + tch * 512:
                                   och * T + (tch + 1) * 512],
                        in0=psv, scalar1=scale_sb[:, 0:1],
                        scalar2=bias_sb[:, bias_off + och:bias_off + och + 1],
                        op0=ALU.mult, op1=ALU.add)
                else:
                    inst = v.tensor_scalar(
                        out=out_sb[:, och * T + tch * 512:
                                   och * T + (tch + 1) * 512],
                        in0=psv,
                        scalar1=bias_sb[:, bias_off + och:bias_off + och + 1],
                        scalar2=None, op0=ALU.add)
                p.inc(inst, "V")
                evac_marks[half] = p.cnt["V"]

    proj_qk(q_sb, 0, True)
    p.bar()
    p.ld(wbuf[:, :].rearrange("p (c o) -> p c o", c=ND),
         wkT.rearrange("(c p) o -> p c o", p=128))
    p.wait(t, "L")
    proj_qk(k_sb, ND, False)
    p.bar()
    p.ld(wbuf[:, :].rearrange("p (c o) -> p c o", c=ND),
         wvT.rearrange("(c p) o -> p c o", p=128))
    p.wait(t, "L")
    evac_marks = [0, 0]
    for t128 in range(NT):
        for oh in range(2):
            half = (t128 * 2 + oh) % 2
            pso = ps_s0 if half == 0 else ps_sc
            psv = pso[:, 0:512]
            p.wait(t, "V", evac_marks[half])
            for dch in range(ND):
                inst = t.matmul(
                    psv,
                    xn0[:, dch * T + t128 * 128:dch * T + (t128 + 1) * 128],
                    wbuf[:, dch * D + oh * 512:dch * D + (oh + 1) * 512],
                    start=(dch == 0), stop=(dch == ND - 1))
            p.inc(inst, "P")
            p.wait(v, "P")
            p.inc(v.tensor_tensor(
                out=v_sb[:, t128 * D + oh * 512:t128 * D + (oh + 1) * 512],
                in0=psv, in1=bvb[:, oh * 512:(oh + 1) * 512], op=ALU.add),
                "V")
            evac_marks[half] = p.cnt["V"]
    p.bar()
    p.ld(wbuf[:, :].rearrange("p (c o) -> p c o", c=ND),
         woT.rearrange("(c p) o -> p c o", p=128))
    p.wait(t, "L")
    p.bar()

    exp_mark = 0
    trans_mark = 0
    trev_mark = 0
    aoev_mark = 0
    s0proj_mark = 0
    s0ev_mark = 0
    s0st_marks = [0, 0]
    probs_free = [0, 0]
    for tq in range(NT):
        for h in range(NH):
            hb = h % 2
            po = (h % 2) * 64
            och = h // 2
            koff = 128 * tq
            W = T - koff
            nk1 = W // 128
            qsl = q_sb[po:po + 64,
                       och * T + tq * 128:och * T + (tq + 1) * 128]
            p.wait(t, "A", exp_mark)
            nfull = (W + 511) // 512
            for c5 in range(nfull):
                wmm = min(512, W - c5 * 512)
                inst = t.matmul(
                    ps_sc[:, c5 * 512:c5 * 512 + wmm], qsl,
                    k_sb[po:po + 64,
                         och * T + koff + c5 * 512:
                         och * T + koff + c5 * 512 + wmm],
                    start=True, stop=True)
            p.inc(inst, "P")
            p.wait(v, "P")
            p.inc(v.tensor_tensor(out=ps_sc[:, 0:128], in0=ps_sc[:, 0:128],
                                  in1=maskc[:, :], op=ALU.add), "V")
            p.wait(v, "V")
            p.inc(v.reduce_max(att_st[:, 0:1], ps_sc[:, 0:W], axis=AX), "V")
            p.wait(v, "V")
            p.inc(v.tensor_scalar(out=att_st[:, 1:2], in0=att_st[:, 0:1],
                                  scalar1=-1.0, scalar2=None, op0=ALU.mult),
                  "V")
            p.wait(s, "V")
            p.wait(s, "P", probs_free[hb])
            p.inc(s.activation(out=probs[:, hb * T:hb * T + W],
                               in_=ps_sc[:, 0:W], func=AF.Exp,
                               bias=att_st[:, 1:2],
                               accum_out=att_st[:, 2:3]), "A")
            exp_mark = p.cnt["A"]
            p.wait(v, "A")
            p.inc(v.reciprocal(att_st[:, 3:4], att_st[:, 2:3]), "V")
            p.wait(v, "V")
            p.inc(v.tensor_scalar(out=probs[:, hb * T:hb * T + W],
                                  in0=probs[:, hb * T:hb * T + W],
                                  scalar1=att_st[:, 3:4], scalar2=None,
                                  op0=ALU.mult), "V")
            norm_mark = p.cnt["V"]
            p.wait(t, "V", norm_mark)
            p.wait(t, "V", trev_mark)
            for kb in range(nk1):
                inst = t.transpose(ps_tr[:, kb * 128:(kb + 1) * 128],
                                   probs[:, hb * T + kb * 128:
                                         hb * T + (kb + 1) * 128],
                                   p.ident[:, :])
            p.inc(inst, "P")
            probs_free[hb] = p.cnt["P"]
            trans_mark = p.cnt["P"]
            p.wait(v, "P", trans_mark)
            p.inc(v.tensor_copy(out=probsT[:, 0:W], in_=ps_tr[:, 0:W]), "V")
            trev_mark = p.cnt["V"]
            p.wait(t, "V", trev_mark)
            p.wait(t, "V", aoev_mark)
            for kb in range(nk1):
                inst = t.matmul(ps_av[0:64, :],
                                v_sb[:, (tq + kb) * D + h * 64:
                                     (tq + kb) * D + h * 64 + 64],
                                probsT[:, kb * 128:(kb + 1) * 128],
                                start=(kb == 0), stop=(kb == nk1 - 1))
            p.inc(inst, "P")
            p.wait(v, "P")
            p.wait(v, "P", s0proj_mark)
            p.inc(v.tensor_copy(
                out=ao_sb[po:po + 64, och * 128:(och + 1) * 128],
                in_=ps_av[0:64, :]), "V")
            aoev_mark = p.cnt["V"]
        tb = tq % 2
        p.wait(t, "V")
        for oh in range(2):
            p.wait(t, "V", s0ev_mark)
            for och in range(ND):
                inst = t.matmul(ps_s0[:, :],
                                ao_sb[:, och * 128:(och + 1) * 128],
                                wo_sb[:, och * D + oh * 512:
                                      och * D + (oh + 1) * 512],
                                start=(och == 0), stop=(och == ND - 1))
            p.inc(inst, "P")
            if oh == 1:
                s0proj_mark = p.cnt["P"]
            p.wait(v, "P")
            p.wait(v, "S" if tb == 0 else "S2", s0st_marks[tb])
            p.inc(v.tensor_tensor(
                out=s0t[:, tb * D + oh * 512:tb * D + (oh + 1) * 512],
                in0=ps_s0[:, :], in1=bob[:, oh * 512:(oh + 1) * 512],
                op=ALU.add), "V")
            s0ev_mark = p.cnt["V"]
            p.wait(g, "V")
            ssem = "S" if tb == 0 else "S2"
            p.st(s0_d[tq * 128:(tq + 1) * 128, oh * 512:(oh + 1) * 512],
                 s0t[:, tb * D + oh * 512:tb * D + (oh + 1) * 512], sem=ssem)
            s0st_marks[tb] = p.cnt[ssem]
    p.bar()
    p.wait(g, "S")
def build_stage_a():
    p = Prog("a")
    io = {}
    io["xn0fm"] = p.dram("xn0fm", [D, T], BF16, "ExternalInput")
    io["wqT"] = p.dram("wqT", [D, D], BF16, "ExternalInput")
    io["wkT"] = p.dram("wkT", [D, D], BF16, "ExternalInput")
    io["wvT"] = p.dram("wvT", [D, D], BF16, "ExternalInput")
    io["woT"] = p.dram("woT", [D, D], BF16, "ExternalInput")
    io["bq"] = p.dram("bq", [D], F32, "ExternalInput")
    io["bk"] = p.dram("bk", [D], F32, "ExternalInput")
    io["bv"] = p.dram("bv", [D], F32, "ExternalInput")
    io["bo"] = p.dram("bo", [D], F32, "ExternalInput")
    io["scale128"] = p.dram("scale128", [128], F32, "ExternalInput")
    io["s0"] = p.dram("s0", [T, D], F32, "ExternalOutput")
    p.emit_consts()
    p.push_scope()
    emit_stage_a(p, io)
    p.pop_scope()
    return p.done()



def emit_stage_f0(p, io):
    nc = p.nc
    v, s, t, g = nc.vector, nc.scalar, nc.tensor, nc.gpsimd
    xn0fm = io["xn0fm"]
    wrr = io["wrr"]
    w1 = io["w1"]
    b1r = io["b1r"]
    w2 = io["w2"]
    b2 = io["b2"]
    x0 = io["x0"]
    s0_d = io["s0"]
    x1_d = io["x1"]
    xn1_d = io["xn1"]
    xn1sfm = io["xn1sfm"]
    xhatfm = io["xhatfm"]
    xn1fm = io["xn1fm"]


    xn0 = p.sbuf("xn0", [128, ND * T], BF16)
    acc0 = p.sbuf("acc0", [128, NT * D], BF16)
    acc2 = p.sbuf("acc2", [128, NT * D], BF16)
    stg = p.sbuf("stg", [128, 4096], F32)
    w1bf = p.sbuf("w1bf", [128, 2 * 4096], BF16)
    w2bf = p.sbuf("w2bf", [128, 2 * 4096], BF16)
    h_sb = p.sbuf("h_sb", [128, 4 * T], BF16)
    wr_sb = p.sbuf("wr_sb", [128, ND * NE], BF16)
    b1_sb = p.sbuf("b1_sb", [128, NE * (FF // 128)], F32)
    b2row = p.sbuf("b2row", [1, D], BF16)
    b2rowf = p.sbuf("b2rowf", [1, D], F32)
    b2b = p.sbuf("b2b", [128, D], F32)
    ones1 = p.sbuf("ones1", [1, 128], BF16)
    logit_sb = p.sbuf("logit_sb", [128, NT * NE], F32)
    sc_sb = p.sbuf("sc_sb", [128, NT * NE], F32)
    lst = p.sbuf("lst", [128, 8], F32)
    trash8 = p.sbuf("trash8", [128, NE], F32)
    tmpc = p.sbuf("tmpc", [128, D], F32)

    ps_h = [p.psum(f"ps_h{i}", [128, 512], F32) for i in range(2)]
    ps_o = [p.psum(f"ps_o{i}", [128, 512], F32) for i in range(2)]
    ps_bc = p.psum("ps_bc", [128, D], F32)

    for dc in range(ND):
        p.ld(xn0[:, dc * T:(dc + 1) * T], xn0fm[dc * 128:(dc + 1) * 128, :])
    p.ld(stg[:, 0:ND * NE], wrr[:, :])
    p.ld(b1_sb[:, :], b1r[:, :])
    v.memset(ones1[0:1, :], 1.0)
    p.wait(v, "L")
    p.inc(v.tensor_copy(out=wr_sb[:, :], in_=stg[:, 0:ND * NE]), "V")
    p.bar()

    rev_marks = [0, 0]
    for t128 in range(NT):
        pb = t128 % 2
        p.wait(t, "V", rev_marks[pb])
        for dch in range(ND):
            inst = t.matmul(ps_h[pb][:, 0:NE],
                            xn0[:, dch * T + t128 * 128:
                                dch * T + (t128 + 1) * 128],
                            wr_sb[:, dch * NE:(dch + 1) * NE],
                            start=(dch == 0), stop=(dch == ND - 1))
        p.inc(inst, "P")
        p.wait(v, "P")
        p.inc(v.tensor_copy(out=logit_sb[:, t128 * NE:(t128 + 1) * NE],
                            in_=ps_h[pb][:, 0:NE]), "V")
        rev_marks[pb] = p.cnt["V"]
    for t128 in range(NT):
        lg = logit_sb[:, t128 * NE:(t128 + 1) * NE]
        p.wait(v, "V")
        p.inc(v.reduce_max(lst[:, 0:1], lg, axis=AX), "V")
        p.wait(v, "V")
        p.inc(v.tensor_scalar(out=lst[:, 1:2], in0=lst[:, 0:1], scalar1=-1.0,
                              scalar2=None, op0=ALU.mult), "V")
        p.wait(v, "V")
        p.inc(v.tensor_scalar(out=sc_sb[:, t128 * NE:(t128 + 1) * NE],
                              in0=lg, scalar1=lst[:, 0:1], scalar2=None,
                              op0=ALU.is_equal), "V")
        p.wait(s, "V")
        p.inc(s.activation(out=trash8[:, :], in_=lg, func=AF.Exp,
                           bias=lst[:, 1:2], accum_out=lst[:, 2:3]), "A")
        p.wait(v, "A")
        p.inc(v.reciprocal(lst[:, 3:4], lst[:, 2:3]), "V")
        p.wait(v, "V")
        p.inc(v.tensor_scalar(out=sc_sb[:, t128 * NE:(t128 + 1) * NE],
                              in0=sc_sb[:, t128 * NE:(t128 + 1) * NE],
                              scalar1=lst[:, 3:4], scalar2=None,
                              op0=ALU.mult), "V")
    p.bar()

    cast_marks = [0, 0]
    hps_marks = [0, 0]
    ops_marks = [0, 0]
    w2mm_mark = 0
    accv_mark = 0
    bcev_mark = 0
    bcps_mark = 0
    for e in range(NE):
        p.wait(nc.sync, "V", bcev_mark)
        self_ = p.inc(nc.sync.dma_start(out=b2rowf[0:1, :],
                                        in_=b2[e, :][None, :]), "L2", 16)
        p.wait(v, "L2")
        p.inc(v.tensor_copy(out=b2row[0:1, :], in_=b2rowf[0:1, :]), "V")
        p.wait(t, "V")
        p.wait(t, "V", bcev_mark)
        for j in range(2):
            inst = t.matmul(ps_bc[:, j * 512:(j + 1) * 512], ones1[0:1, :],
                            b2row[0:1, j * 512:(j + 1) * 512],
                            start=True, stop=True)
        p.inc(inst, "P")
        p.wait(v, "P")
        p.inc(v.tensor_copy(out=b2b[:, :], in_=ps_bc[:, :]), "V")
        bcev_mark = p.cnt["V"]
        for f8 in range(8):
            pb = (e * 8 + f8) % 2
            p.wait(nc.sync, "G")
            p.ld(stg[:, :].rearrange("p (c f) -> p c f", c=ND),
                 w1[e].rearrange("(c p) f -> p c f", p=128)
                 [:, :, f8 * 512:(f8 + 1) * 512])
            p.wait(g, "L")
            p.wait(g, "P", cast_marks[pb])
            p.inc(g.tensor_copy(out=w1bf[:, pb * 4096:(pb + 1) * 4096],
                                in_=stg[:, :]), "G")
            p.wait(nc.sync, "G")
            p.ld(stg[:, :].rearrange("p (c d) -> p c d", c=4),
                 w2[e].rearrange("(c p) d -> p c d", p=128)
                 [:, f8 * 4:(f8 + 1) * 4, :])
            p.wait(g, "L")
            p.inc(g.tensor_copy(out=w2bf[:, pb * 4096:(pb + 1) * 4096],
                                in_=stg[:, :]), "G")
            w1v = w1bf[:, pb * 4096:(pb + 1) * 4096]
            w2v = w2bf[:, pb * 4096:(pb + 1) * 4096]
            p.wait(t, "G")
            p.wait(t, "P", w2mm_mark)
            for ft in range(4):
                ftg = f8 * 4 + ft
                for tch in range(4):
                    hb = (ft * 4 + tch) % 2
                    p.wait(t, "A", hps_marks[hb])
                    for dch in range(ND):
                        inst = t.matmul(
                            ps_h[hb][:, :],
                            w1v[:, dch * 512 + ft * 128:
                                dch * 512 + (ft + 1) * 128],
                            xn0[:, dch * T + tch * 512:
                                dch * T + (tch + 1) * 512],
                            start=(dch == 0), stop=(dch == ND - 1))
                    p.inc(inst, "P")
                    p.wait(s, "P")
                    p.inc(s.activation(
                        out=h_sb[:, ft * T + tch * 512:ft * T + (tch + 1) * 512],
                        in_=ps_h[hb][:, :], func=AF.Gelu,
                        bias=b1_sb[:, e * 32 + ftg:e * 32 + ftg + 1]), "A")
                    hps_marks[hb] = p.cnt["A"]
            hre_mark = p.cnt["A"]
            p.wait(v, "V", accv_mark)
            accv_base = p.cnt["V"]
            for t128 in range(NT):
                for dh in range(2):
                    ob = (t128 * 2 + dh) % 2
                    p.wait(t, "A", hre_mark)
                    p.wait(t, "V", ops_marks[ob])
                    for ft in range(4):
                        inst = t.matmul(
                            ps_o[ob][:, :],
                            h_sb[:, ft * T + t128 * 128:
                                 ft * T + (t128 + 1) * 128],
                            w2v[:, ft * D + dh * 512:ft * D + (dh + 1) * 512],
                            start=(ft == 0), stop=(ft == 3))
                    p.inc(inst, "P")
                    p.wait(v, "P")
                    dst = acc2[:, t128 * D + dh * 512:t128 * D + (dh + 1) * 512]
                    if f8 == 0:
                        inst = v.tensor_copy(out=dst, in_=ps_o[ob][:, :])
                    else:
                        inst = v.tensor_tensor(out=dst, in0=ps_o[ob][:, :],
                                               in1=dst, op=ALU.add)
                    p.inc(inst, "V")
                    ops_marks[ob] = p.cnt["V"]
            accv_mark = p.cnt["V"]
            w2mm_mark = p.cnt["P"]
            cast_marks[pb] = p.cnt["P"]
        p.wait(v, "V", accv_mark)
        for t128 in range(NT):
            a2 = acc2[:, t128 * D:(t128 + 1) * D]
            a0 = acc0[:, t128 * D:(t128 + 1) * D]
            scc = sc_sb[:, t128 * NE + e:t128 * NE + e + 1]
            p.inc(v.tensor_tensor(out=tmpc[:, :], in0=a2, in1=b2b[:, :],
                                  op=ALU.add), "V")
            p.wait(v, "V")
            if e == 0:
                inst = v.tensor_scalar(out=a0, in0=tmpc[:, :], scalar1=scc,
                                       scalar2=None, op0=ALU.mult)
            else:
                inst = v.scalar_tensor_tensor(out=a0, in0=tmpc[:, :],
                                              scalar=scc, in1=a0,
                                              op0=ALU.mult, op1=ALU.add)
            p.inc(inst, "V")
            p.wait(v, "V")
        accv_mark = p.cnt["V"]
    p.bar()

    xt = p.sbuf("c_x", [128, D], F32)
    st_ = p.sbuf("c_s", [128, D], F32)
    gt_ = p.sbuf("c_g", [128, D], F32)
    x1t = p.sbuf("c_x1", [128, D], F32)
    xnt = p.sbuf("c_xn", [128, D], F32)
    xb1 = p.sbuf("c_xb", [128, 3 * D], BF16)
    stat = p.sbuf("c_stat", [128, 16], F32)
    trashd = p.sbuf("c_trash", [128, D], BF16)
    ps_t = [p.psum(f"c_pst{i}", [128, D], BF16) for i in range(2)]
    iter_mark = 0
    evac_marks = [0, 0]
    for tt in range(NT):
        ssem = "S" if tt % 2 == 0 else "S2"
        p.wait(nc.sync, "V", iter_mark)
        p.ld(xt[:, :], x0[tt * 128:(tt + 1) * 128, :])
        p.ld(st_[:, :], s0_d[tt * 128:(tt + 1) * 128, :])
        ldm = p.cnt["L"]
        p.wait(s, "A")
        p.inc(s.activation(out=gt_[:, :], in_=acc0[:, tt * D:(tt + 1) * D],
                           func=AF.Silu), "A")
        p.wait(v, "A")
        p.wait(v, "L", ldm)
        p.inc(v.tensor_tensor(out=gt_[:, :], in0=gt_[:, :],
                              in1=st_[:, :], op=ALU.mult), "V")
        p.wait(v, "V")
        p.wait(v, "S")
        p.wait(v, "S2")
        p.inc(v.tensor_tensor(out=x1t[:, :], in0=xt[:, :], in1=gt_[:, :],
                              op=ALU.add), "V")
        p.wait(v, "V")
        x1v = x1t[:, :]
        p.wait(g, "V")
        p.st(x1_d[tt * 128:(tt + 1) * 128, :], x1v, sem=ssem)
        p.inc(v.reduce_sum(stat[:, 0:1], x1v, axis=AX), "V")
        p.wait(s, "V")
        p.inc(s.activation(out=trashd[:, :], in_=x1v, func=AF.Square,
                           accum_out=stat[:, 1:2]), "A")
        p.wait(v, "A")
        p.wait(v, "V")
        p.inc(v.tensor_scalar(out=stat[:, 3:4], in0=stat[:, 0:1],
                              scalar1=-1.0 / D, scalar2=None, op0=ALU.mult),
              "V")
        p.wait(v, "V")
        v.tensor_tensor(out=stat[:, 6:7], in0=stat[:, 3:4],
                        in1=stat[:, 3:4], op=ALU.mult)
        p.inc(v.tensor_scalar(out=stat[:, 4:5], in0=stat[:, 1:2],
                              scalar1=1.0 / D, scalar2=None, op0=ALU.mult),
              "V")
        p.wait(v, "V")
        p.inc(v.tensor_tensor(out=stat[:, 5:6], in0=stat[:, 4:5],
                              in1=stat[:, 6:7], op=ALU.subtract), "V")
        p.wait(s, "V")
        p.inc(s.activation(out=stat[:, 2:3], in_=stat[:, 5:6], func=AF.Ln,
                           bias=p.C(4)), "A")
        p.wait(s, "A")
        p.inc(s.activation(out=stat[:, 2:3], in_=stat[:, 2:3], func=AF.Exp,
                           scale=p.C(2)), "A")
        p.wait(v, "A")
        p.inc(v.tensor_scalar(out=xnt[:, :], in0=x1v,
                              scalar1=stat[:, 3:4], scalar2=stat[:, 2:3],
                              op0=ALU.add, op1=ALU.mult), "V")
        xnv = xnt[:, :]
        p.wait(v, "V")
        p.wait(g, "V")
        p.st(xn1_d[tt * 128:(tt + 1) * 128, :], xnv, sem=ssem)
        p.inc(v.reduce_sum(stat[:, 8:9], xnv, axis=AX), "V")
        p.wait(s, "V")
        p.inc(s.activation(out=trashd[:, :], in_=xnv, func=AF.Square,
                           accum_out=stat[:, 9:10]), "A")
        p.wait(v, "A")
        p.wait(v, "V")
        p.inc(v.tensor_scalar(out=stat[:, 10:11], in0=stat[:, 9:10],
                              scalar1=1.0 / D, scalar2=None, op0=ALU.mult),
              "V")
        p.wait(s, "V")
        p.inc(s.activation(out=stat[:, 11:12], in_=stat[:, 10:11], func=AF.Ln,
                           bias=p.C(5)), "A")
        p.wait(s, "A")
        p.inc(s.activation(out=stat[:, 11:12], in_=stat[:, 11:12],
                           func=AF.Exp, scale=p.C(2)), "A")
        p.wait(v, "V")
        p.inc(v.tensor_scalar(out=stat[:, 12:13], in0=stat[:, 8:9],
                              scalar1=-1.0 / D, scalar2=None, op0=ALU.mult),
              "V")
        p.wait(v, "V")
        v.tensor_tensor(out=stat[:, 13:14], in0=stat[:, 12:13],
                        in1=stat[:, 12:13], op=ALU.mult)
        p.inc(v.tensor_scalar(out=stat[:, 14:15], in0=stat[:, 9:10],
                              scalar1=1.0 / D, scalar2=None, op0=ALU.mult),
              "V")
        p.wait(v, "V")
        p.inc(v.tensor_tensor(out=stat[:, 14:15], in0=stat[:, 14:15],
                              in1=stat[:, 13:14], op=ALU.subtract), "V")
        p.wait(s, "V")
        p.inc(s.activation(out=stat[:, 15:16], in_=stat[:, 14:15], func=AF.Ln,
                           bias=p.C(5)), "A")
        p.wait(s, "A")
        p.inc(s.activation(out=stat[:, 15:16], in_=stat[:, 15:16],
                           func=AF.Exp, scale=p.C(2)), "A")
        p.wait(v, "A")
        p.inc(v.tensor_scalar(out=xb1[:, 0:D], in0=xnv,
                              scalar1=stat[:, 11:12], scalar2=None,
                              op0=ALU.mult), "V")
        p.inc(v.tensor_scalar(out=xb1[:, D:2 * D], in0=xnv,
                              scalar1=stat[:, 12:13], scalar2=stat[:, 15:16],
                              op0=ALU.add, op1=ALU.mult), "V")
        p.inc(v.tensor_copy(out=xb1[:, 2 * D:3 * D], in_=xnv), "V")
        xb_mark = p.cnt["V"]
        for j, dstd in enumerate((xn1sfm, xhatfm, xn1fm)):
            jb = j % 2
            p.wait(t, "V", xb_mark)
            p.wait(t, "V", evac_marks[jb])
            for dc in range(ND):
                inst = t.transpose(ps_t[jb][:, dc * 128:(dc + 1) * 128],
                                   xb1[:, j * D + dc * 128:
                                       j * D + (dc + 1) * 128],
                                   p.ident[:, :])
            p.inc(inst, "P")
            p.wait(v, "P")
            p.inc(v.tensor_copy(out=xb1[:, j * D:(j + 1) * D],
                                in_=ps_t[jb][:, :]), "V")
            evac_marks[jb] = p.cnt["V"]
            p.wait(g, "V")
            for dc in range(ND):
                p.st(dstd[dc * 128:(dc + 1) * 128, tt * 128:(tt + 1) * 128],
                     xb1[:, j * D + dc * 128:j * D + (dc + 1) * 128],
                     sem=ssem)
        iter_mark = p.cnt["V"]
    p.bar()
    p.wait(g, "S")
    p.wait(g, "S2")
    p.wait(g, "L")
def build_stage_f0():
    p = Prog("f0")
    io = {}
    io["xn0fm"] = p.dram("xn0fm", [D, T], BF16, "ExternalInput")
    io["wrr"] = p.dram("wrr", [128, ND * NE], F32, "ExternalInput")
    io["w1"] = p.dram("w1", [NE, D, FF], F32, "ExternalInput")
    io["b1r"] = p.dram("b1r", [128, NE * (FF // 128)], F32, "ExternalInput")
    io["w2"] = p.dram("w2", [NE, FF, D], F32, "ExternalInput")
    io["b2"] = p.dram("b2", [NE, D], F32, "ExternalInput")
    io["x0"] = p.dram("x0", [T, D], F32, "ExternalInput")
    io["s0"] = p.dram("s0", [T, D], F32, "ExternalInput")
    io["x1"] = p.dram("x1", [T, D], F32, "ExternalOutput")
    io["xn1"] = p.dram("xn1", [T, D], F32, "ExternalOutput")
    io["xn1sfm"] = p.dram("xn1sfm", [D, T], BF16, "ExternalOutput")
    io["xhatfm"] = p.dram("xhatfm", [D, T], BF16, "ExternalOutput")
    io["xn1fm"] = p.dram("xn1fm", [D, T], BF16, "ExternalOutput")
    p.emit_consts()
    p.push_scope()
    emit_stage_f0(p, io)
    p.pop_scope()
    return p.done()



S_UPTO = 4
S_DT_MODE = 2


def emit_stage_s(p, io):
    nc = p.nc
    v, s, t, g = nc.vector, nc.scalar, nc.tensor, nc.gpsimd
    xn1sfm = io["xn1sfm"]
    iprojT = io["iprojT"]
    cwr = io["cwr"]
    cbr = io["cbr"]
    xpT = io["xpT"]
    dtwT = io["dtwT"]
    dtbr = io["dtbr"]
    xc_d = io["xc"]
    dtm_d = io["dtm"]
    bm_d = io["bm"]
    sres_d = io["sres"]


    XW = T + 3
    xn1 = p.sbuf("xn1", [128, ND * T], BF16)
    xi_sb = p.sbuf("xi_sb", [128, NDI * XW], BF16)
    xc_sb = p.sbuf("xc_sb", [128, NDI * T], BF16)
    wcol = p.sbuf("wcol", [128, 2 * ND * 128], BF16)
    dtcol = p.sbuf("dtcol", [128, 2 * NDI * 128], BF16)
    small = p.sbuf("small", [128, NDI * KC + 3 * NDI], F32)
    xp_sb = p.sbuf("xp_sb", [128, NDI * NS], BF16)
    srt = p.sbuf("srt", [128, 2 * 512], BF16)
    cacc = p.sbuf("cacc", [128, 2 * T], F32)
    ut = p.sbuf("ut", [128, 2 * 512], F32)
    at = p.sbuf("at", [128, 2 * 512], F32)
    dtt = p.sbuf("dtt", [128, 2 * 512], F32)
    bt_o = p.sbuf("bt_o", [64, 2 * 512], BF16)
    ps_m = [p.psum(f"s_ps{i}", [128, 512], F32) for i in range(2)]

    cw_v = small[:, 0:NDI * KC]
    cb_v = small[:, NDI * KC:NDI * KC + NDI]
    dtb_v = small[:, NDI * KC + NDI:NDI * KC + 2 * NDI]

    for dc in range(ND):
        p.ld(xn1[:, dc * T:(dc + 1) * T], xn1sfm[dc * 128:(dc + 1) * 128, :])
    p.ld(cw_v, cwr[:, :])
    p.ld(cb_v, cbr[:, :])
    p.ld(dtb_v, dtbr[:, :])
    p.ld(xp_sb[:, :].rearrange("p (c n) -> p c n", c=NDI),
         xpT.rearrange("(c p) n -> p c n", p=128))
    for cc in range(NDI):
        inst = g.memset(xi_sb[:, cc * XW:cc * XW + 3], 0.0)
    p.inc(inst, "G")
    p.bar()

    ev_marks = [0, 0]
    ev_sems = [None, None]
    wrd_marks = [0, 0]
    for och in range(2 * DI // 128):
        pb = och % 2
        lsem = "L" if pb == 0 else "L2"
        p.wait(nc.sync, "P", wrd_marks[pb])
        p.ld(wcol[:, pb * 1024:(pb + 1) * 1024]
             .rearrange("p (c o) -> p c o", c=ND),
             iprojT.rearrange("(c p) o -> p c o", p=128)
             [:, :, och * 128:(och + 1) * 128], sem=lsem)
        p.wait(t, lsem)
        for tch in range(4):
            hb = tch % 2
            if ev_sems[hb] is not None:
                p.wait(t, ev_sems[hb], ev_marks[hb])
            for dch in range(ND):
                inst = t.matmul(
                    ps_m[hb][:, :],
                    wcol[:, pb * 1024 + dch * 128:pb * 1024 + (dch + 1) * 128],
                    xn1[:, dch * T + tch * 512:dch * T + (tch + 1) * 512],
                    start=(dch == 0), stop=(dch == ND - 1))
            p.inc(inst, "P")
            if tch == 3:
                wrd_marks[pb] = p.cnt["P"]
            if och < NDI:
                p.wait(v, "P")
                p.inc(v.tensor_copy(
                    out=xi_sb[:, och * XW + 3 + tch * 512:
                              och * XW + 3 + (tch + 1) * 512],
                    in_=ps_m[hb][:, :]), "V")
                ev_marks[hb] = p.cnt["V"]
                ev_sems[hb] = "V"
            else:
                oi = och - NDI
                sb = (och * 4 + tch) % 2
                p.wait(s, "P")
                p.wait(s, "S" if sb == 0 else "S2")
                p.inc(s.activation(out=srt[:, sb * 512:(sb + 1) * 512],
                                   in_=ps_m[hb][:, :], func=AF.Silu), "A")
                ev_marks[hb] = p.cnt["A"]
                ev_sems[hb] = "A"
                p.wait(g, "A")
                p.st(sres_d[oi * 128:(oi + 1) * 128,
                            tch * 512:(tch + 1) * 512],
                     srt[:, sb * 512:(sb + 1) * 512],
                     sem="S" if sb == 0 else "S2")
    p.bar()
    if S_UPTO < 2:
        p.wait(g, "S")
        p.wait(g, "S2")
    

    for och in range(NDI):
        cb_ = och % 2
        cv = cacc[:, cb_ * T:(cb_ + 1) * T]
        p.wait(v, "A")
        p.inc(v.tensor_scalar(out=cv, in0=xi_sb[:, och * XW:och * XW + T],
                              scalar1=cw_v[:, och * KC:och * KC + 1],
                              scalar2=cb_v[:, och:och + 1],
                              op0=ALU.mult, op1=ALU.add), "V")
        for j in range(1, KC):
            p.wait(v, "V")
            p.inc(v.scalar_tensor_tensor(
                out=cv, in0=xi_sb[:, och * XW + j:och * XW + j + T],
                scalar=cw_v[:, och * KC + j:och * KC + j + 1], in1=cv,
                op0=ALU.mult, op1=ALU.add), "V")
        p.wait(s, "V")
        p.inc(s.activation(out=xc_sb[:, och * T:(och + 1) * T], in_=cv,
                           func=AF.Silu), "A")
        p.wait(g, "A")
        p.st(xc_d[och * 128:(och + 1) * 128, :],
             xc_sb[:, och * T:(och + 1) * T],
             sem="S" if cb_ == 0 else "S2")
    p.bar()
    if S_UPTO < 3:
        p.wait(g, "S")
        p.wait(g, "S2")
    

    bev_marks = [0, 0]
    for tch in range(4):
        hb = tch % 2
        p.wait(t, "V", bev_marks[hb])
        for cch in range(NDI):
            inst = t.matmul(ps_m[hb][0:64, :],
                            xp_sb[:, cch * NS:(cch + 1) * NS],
                            xc_sb[:, cch * T + tch * 512:
                                  cch * T + (tch + 1) * 512],
                            start=(cch == 0), stop=(cch == NDI - 1))
        p.inc(inst, "P")
        p.wait(v, "P")
        p.wait(v, "S" if hb == 0 else "S2")
        p.inc(v.tensor_copy(out=bt_o[0:64, hb * 512:(hb + 1) * 512],
                            in_=ps_m[hb][0:64, :]), "V")
        bev_marks[hb] = p.cnt["V"]
        p.wait(g, "V")
        p.st(bm_d[:, tch * 512:(tch + 1) * 512],
             bt_o[0:64, hb * 512:(hb + 1) * 512],
             sem="S" if hb == 0 else "S2")
    p.bar()
    if S_UPTO < 4:
        p.wait(g, "S")
        p.wait(g, "S2")
    

    wrd_marks = [0, 0]
    spv_marks = [0, 0]
    spa_marks = [0, 0]
    for och in range(NDI):
        pb = och % 2
        lsem = "L" if pb == 0 else "L2"
        p.wait(nc.sync, "P", wrd_marks[pb])
        p.ld(dtcol[:, pb * T:(pb + 1) * T]
             .rearrange("p (c o) -> p c o", c=NDI),
             dtwT.rearrange("(c p) o -> p c o", p=128)
             [:, :, och * 128:(och + 1) * 128], sem=lsem)
        p.wait(t, lsem)
        for tch in range(4):
            hb = tch % 2
            p.wait(t, "V", spv_marks[hb])
            p.wait(t, "A", spa_marks[hb])
            for cch in range(NDI):
                inst = t.matmul(
                    ps_m[hb][:, :],
                    dtcol[:, pb * T + cch * 128:pb * T + (cch + 1) * 128],
                    xc_sb[:, cch * T + tch * 512:cch * T + (tch + 1) * 512],
                    start=(cch == 0), stop=(cch == NDI - 1))
            p.inc(inst, "P")
            if tch == 3:
                wrd_marks[pb] = p.cnt["P"]
            if S_DT_MODE == 0:
                p.wait(v, "P")
                p.wait(v, "V", spv_marks[hb])
                p.wait(v, "S" if hb == 0 else "S2")
                p.inc(v.tensor_copy(out=dtt[:, hb * 512:(hb + 1) * 512],
                                    in_=ps_m[hb][:, :]), "V")
                spv_marks[hb] = p.cnt["V"]
                spa_marks[hb] = 0
                p.wait(g, "V")
                p.st(dtm_d[och * 128:(och + 1) * 128,
                           tch * 512:(tch + 1) * 512],
                     dtt[:, hb * 512:(hb + 1) * 512],
                     sem="S" if hb == 0 else "S2")
                continue
            p.wait(v, "P")
            p.inc(v.tensor_scalar(out=ut[:, hb * 512:(hb + 1) * 512],
                                  in0=ps_m[hb][:, :],
                                  scalar1=dtb_v[:, och:och + 1], scalar2=0.0,
                                  op0=ALU.add, op1=ALU.max), "V")
            p.wait(s, "P")
            p.wait(s, "V")
            p.inc(s.activation(out=at[:, hb * 512:(hb + 1) * 512],
                               in_=ps_m[hb][:, :], func=AF.Abs,
                               bias=dtb_v[:, och:och + 1]), "A")
            spa_marks[hb] = p.cnt["A"]
            p.wait(s, "A")
            p.inc(s.activation(out=at[:, hb * 512:(hb + 1) * 512],
                               in_=at[:, hb * 512:(hb + 1) * 512],
                               func=AF.Exp, scale=p.C(0)), "A")
            p.wait(s, "A")
            p.inc(s.activation(out=at[:, hb * 512:(hb + 1) * 512],
                               in_=at[:, hb * 512:(hb + 1) * 512],
                               func=AF.Ln, bias=p.C(1)), "A")
            p.wait(v, "A")
            p.wait(v, "V")
            p.wait(v, "S" if hb == 0 else "S2")
            p.inc(v.tensor_tensor(out=dtt[:, hb * 512:(hb + 1) * 512],
                                  in0=ut[:, hb * 512:(hb + 1) * 512],
                                  in1=at[:, hb * 512:(hb + 1) * 512],
                                  op=ALU.add), "V")
            spv_marks[hb] = p.cnt["V"]
            p.wait(g, "V")
            p.st(dtm_d[och * 128:(och + 1) * 128, tch * 512:(tch + 1) * 512],
                 dtt[:, hb * 512:(hb + 1) * 512],
                 sem="S" if hb == 0 else "S2")
    p.bar()
    p.wait(g, "S")
    p.wait(g, "S2")
def build_stage_s():
    p = Prog("s")
    io = {}
    io["xn1sfm"] = p.dram("xn1sfm", [D, T], BF16, "ExternalInput")
    io["iprojT"] = p.dram("iprojT", [D, 2 * DI], BF16, "ExternalInput")
    io["cwr"] = p.dram("cwr", [128, NDI * KC], F32, "ExternalInput")
    io["cbr"] = p.dram("cbr", [128, NDI], F32, "ExternalInput")
    io["xpT"] = p.dram("xpT", [DI, NS], BF16, "ExternalInput")
    io["dtwT"] = p.dram("dtwT", [DI, DI], BF16, "ExternalInput")
    io["dtbr"] = p.dram("dtbr", [128, NDI], F32, "ExternalInput")
    io["xc"] = p.dram("xc", [DI, T], BF16, "ExternalOutput")
    io["dtm"] = p.dram("dtm", [DI, T], F32, "ExternalOutput")
    io["bm"] = p.dram("bm", [NS, T], BF16, "ExternalOutput")
    io["sres"] = p.dram("sres", [DI, T], BF16, "ExternalOutput")
    p.emit_consts()
    p.push_scope()
    emit_stage_s(p, io)
    p.pop_scope()
    return p.done()



def gen_c_scan(p, io):
    nc = p.nc
    p.prefix = "c1_"
    v, s, t, g = nc.vector, nc.scalar, nc.tensor, nc.gpsimd
    xc_d, dtm_d, bm_d = io["xc"], io["dtm"], io["bm"]
    asb_d, dpr, sres_d = io["asb"], io["dpr"], io["sres"]
    y2_d = io["y2"]

    TQ = 512
    NB = 8
    NBK = NS // NB
    asb = p.sbuf("asb_s", [128, NDI * NS], F32)
    dp_sb = p.sbuf("dp_sb", [128, NDI], F32)
    ones1 = p.sbuf("ones1", [1, 128], BF16)
    bt_sb = p.sbuf("bt_sb", [64, TQ], BF16)
    brow = p.sbuf("brow", [1, 2 * TQ], BF16)
    bbc = p.sbuf("bbc", [128, NB * TQ], BF16)
    dt_t = p.sbuf("dt_t", [128, 2 * TQ], F32)
    xc_t = p.sbuf("xc_t", [128, 2 * TQ], BF16)
    sr_t = p.sbuf("sr_t", [128, 2 * TQ], BF16)
    ad_t = p.sbuf("ad_t", [128, 2 * TQ], F32)
    bx_t = p.sbuf("bx_t", [128, 2 * TQ], BF16)
    h_t = p.sbuf("h_t", [128, 2 * TQ], BF16)
    carry = p.sbuf("carry", [128, NDI * NS], BF16)
    TQP = 512
    y_acc = p.sbuf("y_acc", [128, NDI * TQ], BF16)
    y2_t = p.sbuf("y2_t", [128, TQ], BF16)
    ps_y = p.psum("ps_y", [128, TQ], F32)
    ps_c = p.psum("ps_c", [128, NS], F32)
    ps_b = p.psum("ps_b", [128, TQ], F32)

    p.ld(asb[:, :], asb_d[:, :], sem="L5")
    p.ld(dp_sb[:, :], dpr[:, :], sem="L5")
    p.wait(v, "L5")
    p.inc(v.memset(ones1[0:1, :], 1.0), "V")
    prol_mark = p.cnt["V"]
    admk = [0, 0]
    hmk = [0, 0]
    bxmk = [0, 0]
    ldg_v = [0, 0]
    ldg_g = [0, 0]
    ldg_a = [0, 0]
    yield
    for tq in range(T // TQ):
        p.wait(nc.sync, "V", prol_mark)
        p.wait(nc.sync, "L5")
        p.ld(bt_sb[0:64, :], bm_d[:, tq * TQ:(tq + 1) * TQ], sem="L5")
        for nb_ in range(NBK):
            bev = 0
            first_ev = True
            for ng in range(NB // 2):
                p.wait(nc.sync, "L5")
                p.wait(nc.sync, "P")
                p.inc(nc.sync.dma_start(
                    out=brow[0:1, :],
                    in_=bt_sb[nb_ * NB + ng * 2:nb_ * NB + (ng + 1) * 2, :]),
                    "L5", 16)
                p.wait(t, "L5")
                for nl in range(2):
                    n_ = ng * 2 + nl
                    p.wait(t, "V", bev)
                    inst = t.matmul(ps_b[:, :], ones1[0:1, :],
                                    brow[0:1, nl * TQ:(nl + 1) * TQ],
                                    start=True, stop=True)
                    p.inc(inst, "P")
                    p.wait(v, "P")
                    if first_ev:
                        p.wait(v, "G")
                        first_ev = False
                    p.inc(v.tensor_copy(out=bbc[:, n_ * TQ:(n_ + 1) * TQ],
                                        in_=ps_b[:, :]), "V")
                    bev = p.cnt["V"]
            bev_all = p.cnt["V"]
            yield
            for dch in range(NDI):
                db_ = dch % 2
                lsem = "L3" if db_ == 0 else "L4"
                p.wait(nc.sync, "V", ldg_v[db_])
                p.wait(nc.sync, "G", ldg_g[db_])
                p.wait(nc.sync, "A", ldg_a[db_])
                p.ld(dt_t[:, db_ * TQ:(db_ + 1) * TQ],
                     dtm_d[dch * 128:(dch + 1) * 128,
                           tq * TQ:(tq + 1) * TQ], sem=lsem)
                p.ld(xc_t[:, db_ * TQ:(db_ + 1) * TQ],
                     xc_d[dch * 128:(dch + 1) * 128,
                          tq * TQ:(tq + 1) * TQ], sem=lsem)
                if nb_ == NBK - 1:
                    p.ld(sr_t[:, db_ * TQ:(db_ + 1) * TQ],
                         sres_d[dch * 128:(dch + 1) * 128,
                                tq * TQ:(tq + 1) * TQ], sem=lsem)
                p.wait(s, lsem)
                p.wait(g, lsem)
                p.wait(v, lsem)
                dtv = dt_t[:, db_ * TQ:(db_ + 1) * TQ]
                xcv = xc_t[:, db_ * TQ:(db_ + 1) * TQ]
                srv = sr_t[:, db_ * TQ:(db_ + 1) * TQ]
                for n_ in range(NB):
                    n = nb_ * NB + n_
                    nbuf = n_ % 2
                    adv = ad_t[:, nbuf * TQ:(nbuf + 1) * TQ]
                    bxv = bx_t[:, nbuf * TQ:(nbuf + 1) * TQ]
                    hv = h_t[:, nbuf * TQ:(nbuf + 1) * TQ]
                    col = dch * NS + n
                    p.wait(s, "V", admk[nbuf])
                    p.inc(s.activation(out=adv, in_=dtv, func=AF.Exp,
                                       scale=asb[:, col:col + 1]), "A")
                    use_pool = True
                    eng = g
                    esem = "G"
                    p.wait(eng, "V", bev_all)
                    p.wait(eng, "V", bxmk[nbuf])
                    p.inc(eng.tensor_tensor(
                        out=bxv, in0=xcv,
                        in1=bbc[:, n_ * TQ:(n_ + 1) * TQ], op=ALU.mult), esem)
                    bx_mark = p.cnt[esem]
                    p.wait(v, "A")
                    p.wait(v, esem, bx_mark)
                    p.wait(v, "P", hmk[nbuf])
                    init = 0.0 if tq == 0 else carry[:, col:col + 1]
                    p.inc(v.tensor_tensor_scan(out=hv, data0=adv, data1=bxv,
                                               initial=init, op0=ALU.mult,
                                               op1=ALU.add), "V")
                    admk[nbuf] = p.cnt["V"]
                    bxmk[nbuf] = p.cnt["V"]
                    p.wait(t, "V")
                    inst = t.matmul(ps_y[:, :], p.ident[:, :], hv,
                                    start=(n_ == 0), stop=(n_ == NB - 1))
                    if tq < (T // TQ) - 1:
                        inst = t.matmul(ps_c[:, n:n + 1], p.ident[:, :],
                                        hv[:, TQ - 1:TQ], start=(n_ == 0),
                                        stop=(n_ == NB - 1))
                    p.inc(inst, "P")
                    hmk[nbuf] = p.cnt["P"]
                p.wait(v, "P")
                yv = y_acc[:, dch * TQP:(dch + 1) * TQP]
                if nb_ == 0:
                    p.inc(v.tensor_copy(out=yv, in_=ps_y[:, :]), "V")
                else:
                    p.inc(v.tensor_tensor(out=yv, in0=ps_y[:, :],
                                          in1=yv, op=ALU.add), "V")
                if tq < (T // TQ) - 1:
                    p.wait(v, "V")
                    p.inc(v.tensor_copy(
                        out=carry[:, dch * NS + nb_ * NB:
                                  dch * NS + (nb_ + 1) * NB],
                        in_=ps_c[:, nb_ * NB:(nb_ + 1) * NB]), "V")
                if nb_ == NBK - 1:
                    p.wait(v, "V")
                    p.inc(v.scalar_tensor_tensor(
                        out=yv, in0=xcv,
                        scalar=dp_sb[:, dch:dch + 1], in1=yv,
                        op0=ALU.mult, op1=ALU.add), "V")
                    p.wait(v, "V")
                    ssem = "S" if db_ == 0 else "S2"
                    p.wait(v, "S")
                    p.wait(v, "S2")
                    p.inc(v.tensor_tensor(out=y2_t[:, :], in0=yv,
                                          in1=srv, op=ALU.mult), "V")
                    p.wait(g, "V")
                    p.st(y2_d[dch * 128:(dch + 1) * 128,
                              tq * TQ:(tq + 1) * TQ], y2_t[:, :], sem=ssem)
                ldg_v[db_] = p.cnt["V"]
                ldg_g[db_] = p.cnt["G"]
                ldg_a[db_] = p.cnt["A"]
                yield
    p.bar()


def emit_c_outproj(p, io):
    nc = p.nc
    p.prefix = "c2_"
    v, s, t, g = nc.vector, nc.scalar, nc.tensor, nc.gpsimd
    opT, xn1_d, s1_d, y2_d = io["opT"], io["xn1"], io["s1"], io["y2"]
    opt_sb = p.sbuf("opt_sb", [128, NDI * D], BF16)
    y2l = p.sbuf("y2l", [128, 2 * NDI * 128], BF16)
    xnt = p.sbuf("xnt", [128, 2 * D], F32)
    s1t = p.sbuf("s1t", [128, 2 * D], F32)
    ps_b2 = [p.psum(f"ps_op{i}", [128, 512], F32) for i in range(2)]
    p.ld(opt_sb[:, :].rearrange("p (c o) -> p c o", c=NDI),
         opT.rearrange("(c p) o -> p c o", p=128))
    p.bar()
    opev = [0, 0]
    ldg = [0, 0]
    for t128 in range(NT):
        pb = t128 % 2
        lsem = "L" if pb == 0 else "L2"
        ssem = "S" if pb == 0 else "S2"
        p.wait(nc.sync, "V", ldg[pb])
        for cch in range(NDI):
            p.ld(y2l[:, pb * DI + cch * 128:pb * DI + (cch + 1) * 128],
                 y2_d[cch * 128:(cch + 1) * 128,
                      t128 * 128:(t128 + 1) * 128], sem=lsem)
        p.ld(xnt[:, pb * D:(pb + 1) * D],
             xn1_d[t128 * 128:(t128 + 1) * 128, :], sem=lsem)
        p.wait(t, lsem)
        for oh in range(2):
            hb = (t128 * 2 + oh) % 2
            p.wait(t, "V", opev[hb])
            for cch in range(NDI):
                inst = t.matmul(
                    ps_b2[hb][:, :],
                    y2l[:, pb * DI + cch * 128:pb * DI + (cch + 1) * 128],
                    opt_sb[:, cch * D + oh * 512:cch * D + (oh + 1) * 512],
                    start=(cch == 0), stop=(cch == NDI - 1))
            p.inc(inst, "P")
            p.wait(v, "P")
            p.wait(v, ssem)
            p.inc(v.tensor_tensor(
                out=s1t[:, pb * D + oh * 512:pb * D + (oh + 1) * 512],
                in0=ps_b2[hb][:, :],
                in1=xnt[:, pb * D + oh * 512:pb * D + (oh + 1) * 512],
                op=ALU.add), "V")
            opev[hb] = p.cnt["V"]
            p.wait(g, "V")
            p.st(s1_d[t128 * 128:(t128 + 1) * 128,
                      oh * 512:(oh + 1) * 512],
                 s1t[:, pb * D + oh * 512:pb * D + (oh + 1) * 512],
                 sem=ssem)
        ldg[pb] = p.cnt["V"]
    p.bar()


def emit_f1_prologue(p, io, bufs):
    nc = p.nc
    p.prefix = "f1_"
    v, s, t, g = nc.vector, nc.scalar, nc.tensor, nc.gpsimd
    xn1fm, wgr = io["xn1fm"], io["wgr"]
    gr, br, b1r = io["gr"], io["br"], io["b1r"]
    normed = bufs["normed"] = p.sbuf("normed", [128, ND * T], BF16)
    acc1 = bufs["acc1"]
    bufs["acc2"] = p.sbuf("acc2", [128, NT * D], BF16)
    bufs["stg"] = p.sbuf("stg", [128, 1024], F32)
    bufs["w1bf"] = p.sbuf("w1bf", [128, 2 * 4096], BF16)
    bufs["w2bf"] = p.sbuf("w2bf", [128, 4096], BF16)
    bufs["h_sb"] = p.sbuf("h_sb", [128, 4 * T], BF16)
    wg_sb = bufs["wg_sb"] = p.sbuf("wg_sb", [128, ND * NE], BF16)
    gb_sb = bufs["gb_sb"] = p.sbuf("gb_sb", [128, 2 * NE * ND], F32)
    bufs["b1_sb"] = p.sbuf("b1_sb", [128, NE * (FF // 128)], F32)
    bufs["b2row"] = p.sbuf("b2row", [1, D], BF16)
    bufs["b2rowf"] = p.sbuf("b2rowf", [1, D], F32)
    bufs["b2b"] = p.sbuf("b2b", [128, D], BF16)
    bufs["ones1"] = p.sbuf("ones1f", [1, 128], BF16)
    logit_sb = bufs["logit_sb"] = p.sbuf("logit_sb", [128, NT * NE], F32)
    cw_sb = bufs["cw_sb"]
    sw_sb = bufs["sw_sb"]
    lst = bufs["lst"] = p.sbuf("lst", [128, 8], F32)
    sp8 = bufs["sp8"] = p.sbuf("sp8", [128, NE], F32)
    ps_h = bufs["ps_h"] = [p.psum(f"ps_h{i}", [128, 512], F32)
                           for i in range(2)]
    bufs["ps_o"] = [p.psum(f"ps_o{i}", [128, 512], F32) for i in range(2)]
    bufs["ps_bc"] = p.psum("ps_bc", [128, 512], F32)

    for dc in range(ND):
        p.ld(normed[:, dc * T:(dc + 1) * T],
             xn1fm[dc * 128:(dc + 1) * 128, :])
    p.ld(bufs["stg"][:, 0:ND * NE], wgr[:, :])
    p.ld(bufs["b1_sb"][:, :], b1r[:, :])
    p.ld(gb_sb[:, 0:NE * ND], gr[:, :])
    p.ld(gb_sb[:, NE * ND:2 * NE * ND], br[:, :])
    v.memset(bufs["ones1"][0:1, :], 1.0)
    p.wait(v, "L")
    p.inc(v.tensor_copy(out=wg_sb[:, :], in_=bufs["stg"][:, 0:ND * NE]),
          "V")
    p.bar()
    rev_marks = [0, 0]
    for t128 in range(NT):
        pb = t128 % 2
        p.wait(t, "V", rev_marks[pb])
        for dch in range(ND):
            inst = t.matmul(ps_h[pb][:, 0:NE],
                            normed[:, dch * T + t128 * 128:
                                   dch * T + (t128 + 1) * 128],
                            wg_sb[:, dch * NE:(dch + 1) * NE],
                            start=(dch == 0), stop=(dch == ND - 1))
        p.inc(inst, "P")
        p.wait(v, "P")
        p.inc(v.tensor_copy(out=logit_sb[:, t128 * NE:(t128 + 1) * NE],
                            in_=ps_h[pb][:, 0:NE]), "V")
        rev_marks[pb] = p.cnt["V"]
    for t128 in range(NT):
        lg = logit_sb[:, t128 * NE:(t128 + 1) * NE]
        cwv = cw_sb[:, t128 * NE:(t128 + 1) * NE]
        p.wait(v, "V")
        p.inc(v.reduce_max(lst[:, 0:1], lg, axis=AX), "V")
        p.wait(v, "V")
        p.inc(v.tensor_scalar(out=cwv, in0=lg, scalar1=lst[:, 0:1],
                              scalar2=None, op0=ALU.is_equal), "V")
        p.wait(v, "V")
        p.inc(v.scalar_tensor_tensor(out=sp8[:, :], in0=cwv, scalar=-1e30,
                                     in1=lg, op0=ALU.mult, op1=ALU.add), "V")
        p.wait(v, "V")
        p.inc(v.reduce_max(lst[:, 1:2], sp8[:, :], axis=AX), "V")
        p.wait(v, "V")
        p.inc(v.tensor_tensor(out=lst[:, 2:3], in0=lst[:, 0:1],
                              in1=lst[:, 1:2], op=ALU.subtract), "V")
        p.wait(s, "V")
        s.activation(out=lst[:, 3:4], in_=lst[:, 2:3], func=AF.Sigmoid)
        p.inc(s.activation(out=lst[:, 4:5], in_=lst[:, 2:3], func=AF.Sigmoid,
                           scale=p.C(0)), "A")
        p.wait(v, "A")
        p.inc(v.tensor_scalar(out=sp8[:, :], in0=sp8[:, :],
                              scalar1=lst[:, 1:2], scalar2=None,
                              op0=ALU.is_equal), "V")
        p.wait(v, "V")
        p.inc(v.tensor_scalar(out=cwv, in0=cwv, scalar1=lst[:, 3:4],
                              scalar2=None, op0=ALU.mult), "V")
        p.wait(v, "V")
        p.inc(v.scalar_tensor_tensor(out=cwv, in0=sp8[:, :],
                                     scalar=lst[:, 4:5], in1=cwv,
                                     op0=ALU.mult, op1=ALU.add), "V")
        p.inc(v.tensor_tensor(out=sw_sb[:, t128:t128 + 1], in0=lst[:, 3:4],
                              in1=lst[:, 4:5], op=ALU.add), "V")
        p.wait(v, "V")
    p.bar()
    return bufs


def gen_f1_experts(p, io, bufs):
    nc = p.nc
    v, s, t, g = nc.vector, nc.scalar, nc.tensor, nc.gpsimd
    xhatfm, w1, w2, b2 = io["xhatfm"], io["w1"], io["w2"], io["b2"]
    normed, acc1, acc2 = bufs["normed"], bufs["acc1"], bufs["acc2"]
    stg, w1bf, w2bf = bufs["stg"], bufs["w1bf"], bufs["w2bf"]
    h_sb = bufs["h_sb"]
    gb_sb, b1_sb = bufs["gb_sb"], bufs["b1_sb"]
    b2row, b2rowf, b2b = bufs["b2row"], bufs["b2rowf"], bufs["b2b"]
    ones1, cw_sb, tmpc = bufs["ones1"], bufs["cw_sb"], bufs["tmpc"]
    ps_h, ps_o, ps_bc = bufs["ps_h"], bufs["ps_o"], bufs["ps_bc"]
    cast_marks = [0, 0]
    hps_marks = [0, 0]
    ops_marks = [0, 0]
    w2mm_mark = 0
    accv_mark = 0
    bcev_mark = 0
    nrm_mark = 0
    yield
    for e in range(NE):
        p.wait(t, "V")
        p.wait(nc.sync, "G2")
        for dc in range(ND):
            p.wait(nc.sync, "V", nrm_mark)
            p.ld(normed[:, dc * T:(dc + 1) * T],
                 xhatfm[dc * 128:(dc + 1) * 128, :], sem="L")
            p.wait(v, "L")
            p.wait(v, "P", w2mm_mark)
            p.inc(v.tensor_scalar(
                out=normed[:, dc * T:(dc + 1) * T],
                in0=normed[:, dc * T:(dc + 1) * T],
                scalar1=gb_sb[:, e * ND + dc:e * ND + dc + 1],
                scalar2=gb_sb[:, NE * ND + e * ND + dc:
                              NE * ND + e * ND + dc + 1],
                op0=ALU.mult, op1=ALU.add), "V")
            nrm_mark = p.cnt["V"]
        p.wait(nc.sync, "V", bcev_mark)
        p.wait(nc.sync, "V", nrm_mark)
        p.inc(nc.sync.dma_start(out=b2rowf[0:1, :], in_=b2[e, :][None, :]),
              "L2", 16)
        p.wait(v, "L2")
        p.inc(v.tensor_copy(out=b2row[0:1, :], in_=b2rowf[0:1, :]), "V")
        for j in range(2):
            p.wait(t, "V")
            p.wait(t, "V", bcev_mark)
            inst = t.matmul(ps_bc[:, :], ones1[0:1, :],
                            b2row[0:1, j * 512:(j + 1) * 512],
                            start=True, stop=True)
            p.inc(inst, "P")
            p.wait(v, "P")
            p.inc(v.tensor_copy(out=b2b[:, j * 512:(j + 1) * 512],
                                in_=ps_bc[:, :]), "V")
            bcev_mark = p.cnt["V"]
        yield
        for f8 in range(8):
            pb = (e * 8 + f8) % 2
            wv1 = w1bf[:, pb * 4096:(pb + 1) * 4096]
            wv2 = w2bf[:, :]
            for half in range(4):
                p.wait(nc.sync, "G2")
                if f8 == 0 and half == 0:
                    p.wait(nc.sync, "V", nrm_mark)
                p.ld(stg[:, :].rearrange("p (c f) -> p c f", c=2),
                     w1[e].rearrange("(c p) f -> p c f", p=128)
                     [:, half * 2:(half + 1) * 2,
                      f8 * 512:(f8 + 1) * 512])
                p.wait(v, "L")
                if half == 0:
                    p.wait(v, "P", cast_marks[pb])
                p.inc(v.tensor_copy(
                    out=wv1[:, half * 1024:(half + 1) * 1024],
                    in_=stg[:, :]), "G2")
            for half in range(4):
                p.wait(nc.sync, "G2")
                p.ld(stg[:, :].rearrange("p (c d) -> p c d", c=1),
                     w2[e].rearrange("(c p) d -> p c d", p=128)
                     [:, f8 * 4 + half:f8 * 4 + half + 1, :])
                p.wait(v, "L")
                if half == 0:
                    p.wait(v, "P", w2mm_mark)
                p.inc(v.tensor_copy(
                    out=wv2[:, half * 1024:(half + 1) * 1024],
                    in_=stg[:, :]), "G2")
            p.wait(t, "G2")
            p.wait(t, "P", w2mm_mark)
            p.wait(t, "V", nrm_mark)
            for ft in range(4):
                ftg = f8 * 4 + ft
                for tch in range(4):
                    hb = (ft * 4 + tch) % 2
                    p.wait(t, "A", hps_marks[hb])
                    for dch in range(ND):
                        inst = t.matmul(
                            ps_h[hb][:, :],
                            wv1[:, dch * 512 + ft * 128:
                                dch * 512 + (ft + 1) * 128],
                            normed[:, dch * T + tch * 512:
                                   dch * T + (tch + 1) * 512],
                            start=(dch == 0), stop=(dch == ND - 1))
                    p.inc(inst, "P")
                    p.wait(s, "P")
                    p.inc(s.activation(
                        out=h_sb[:, ft * T + tch * 512:
                                 ft * T + (tch + 1) * 512],
                        in_=ps_h[hb][:, :], func=AF.Relu,
                        bias=b1_sb[:, e * 32 + ftg:e * 32 + ftg + 1]), "A")
                    hps_marks[hb] = p.cnt["A"]
                    if tch == 3:
                        yield
            hre_mark = p.cnt["A"]
            for t128 in range(NT):
                if t128 % 4 == 0:
                    yield
                for dh in range(2):
                    ob = (t128 * 2 + dh) % 2
                    p.wait(t, "A", hre_mark)
                    p.wait(t, "A", ops_marks[ob])
                    dst = acc2[:, t128 * D + dh * 512:
                               t128 * D + (dh + 1) * 512]
                    first = True
                    if f8 > 0:
                        inst = t.matmul(ps_o[ob][:, :], p.ident[:, :], dst,
                                        start=True, stop=False)
                        first = False
                    for ft in range(4):
                        inst = t.matmul(
                            ps_o[ob][:, :],
                            h_sb[:, ft * T + t128 * 128:
                                 ft * T + (t128 + 1) * 128],
                            wv2[:, ft * D + dh * 512:ft * D + (dh + 1) * 512],
                            start=(first and ft == 0), stop=(ft == 3))
                    p.inc(inst, "P")
                    p.wait(s, "P")
                    p.inc(s.activation(out=dst, in_=ps_o[ob][:, :],
                                       func=AF.Copy), "A")
                    ops_marks[ob] = p.cnt["A"]
            accv_mark = p.cnt["V"]
            w2mm_mark = p.cnt["P"]
            cast_marks[pb] = p.cnt["P"]
            yield
        p.wait(v, "V", accv_mark)
        p.wait(v, "A")
        for t128 in range(NT):
            a2 = acc2[:, t128 * D:(t128 + 1) * D]
            a0 = acc1[:, t128 * D:(t128 + 1) * D]
            scc = cw_sb[:, t128 * NE + e:t128 * NE + e + 1]
            p.inc(v.tensor_tensor(out=tmpc[:, :], in0=a2, in1=b2b[:, :],
                                  op=ALU.add), "V")
            p.wait(v, "V")
            if e == 0:
                inst = v.tensor_scalar(out=a0, in0=tmpc[:, :], scalar1=scc,
                                       scalar2=None, op0=ALU.mult)
            else:
                inst = v.scalar_tensor_tensor(out=a0, in0=tmpc[:, :],
                                              scalar=scc, in1=a0,
                                              op0=ALU.mult, op1=ALU.add)
            p.inc(inst, "V")
            p.wait(v, "V")
        yield


def emit_f1_final(p, io, bufs):
    nc = p.nc
    p.prefix = "f2_"
    v, s, t, g = nc.vector, nc.scalar, nc.tensor, nc.gpsimd
    xn1_d, x1_d, s1_d, out_d = io["xn1"], io["x1"], io["s1"], io["out"]
    acc1, sw_sb, tmpc = bufs["acc1"], bufs["sw_sb"], bufs["tmpc"]
    xnt = p.sbuf("f_xn", [128, D], F32)
    x1t = p.sbuf("f_x1", [128, D], F32)
    s1t = p.sbuf("f_s1", [128, D], F32)
    gt_ = p.sbuf("f_g", [128, D], F32)
    ot = p.sbuf("f_o", [128, D], F32)
    yt = tmpc
    iter_mark = 0
    for tt in range(NT):
        ssem = "S" if tt % 2 == 0 else "S2"
        p.wait(nc.sync, "V", iter_mark)
        p.ld(xnt[:, :], xn1_d[tt * 128:(tt + 1) * 128, :])
        p.ld(x1t[:, :], x1_d[tt * 128:(tt + 1) * 128, :])
        p.ld(s1t[:, :], s1_d[tt * 128:(tt + 1) * 128, :])
        ldm = p.cnt["L"]
        p.wait(v, "L", ldm)
        p.wait(v, "A")
        p.inc(v.scalar_tensor_tensor(out=yt[:, :], in0=xnt[:, :],
                                     scalar=sw_sb[:, tt:tt + 1],
                                     in1=acc1[:, tt * D:(tt + 1) * D],
                                     op0=ALU.mult, op1=ALU.add), "V")
        p.wait(s, "V")
        p.inc(s.activation(out=gt_[:, :], in_=yt[:, :], func=AF.Silu), "A")
        p.wait(v, "A")
        p.inc(v.tensor_tensor(out=gt_[:, :], in0=gt_[:, :], in1=s1t[:, :],
                              op=ALU.mult), "V")
        p.wait(v, "V")
        p.wait(v, "S")
        p.wait(v, "S2")
        p.inc(v.tensor_tensor(out=ot[:, :], in0=x1t[:, :], in1=gt_[:, :],
                              op=ALU.add), "V")
        iter_mark = p.cnt["V"]
        p.wait(g, "V")
        p.st(out_d[tt * 128:(tt + 1) * 128, :], ot[:, :], sem=ssem)
    p.bar()
    p.wait(g, "S")
    p.wait(g, "S2")


def emit_stage_cf1(p, ioc, iof1):
    nc = p.nc
    g = nc.gpsimd
    p.prefix = "f1p_"
    bufs = dict(
        acc1=p.sbuf("acc1", [128, NT * D], BF16),
        cw_sb=p.sbuf("cw_sb", [128, NT * NE], F32),
        sw_sb=p.sbuf("sw_sb", [128, NT], F32),
        tmpc=p.sbuf("tmpc", [128, D], F32),
    )
    p.push_scope()
    emit_f1_prologue(p, iof1, bufs)
    gc = gen_c_scan(p, ioc)
    gf = gen_f1_experts(p, iof1, bufs)
    next(gc)
    next(gf)
    done_c = done_f = False
    while not (done_c and done_f):
        if not done_c:
            try:
                next(gc)
            except StopIteration:
                done_c = True
        if not done_f:
            try:
                next(gf)
            except StopIteration:
                done_f = True
    for sm in ("S", "S2", "L", "L2", "L3", "L4", "L5"):
        p.wait(g, sm)
    p.bar()
    p.pop_scope()
    emit_c_outproj(p, ioc)
    emit_f1_final(p, iof1, bufs)


def build_fused():
    p = Prog("fused")
    EI, EO = "ExternalInput", "ExternalOutput"
    d = p.dram
    iop = dict(
        x0=d("x0", [T, D], F32, EI), wq=d("wq", [D, D], F32, EI),
        wk=d("wk", [D, D], F32, EI), wv=d("wv", [D, D], F32, EI),
        wo=d("wo", [D, D], F32, EI), ipw=d("ipw", [2 * DI, D], F32, EI),
        nw=d("nw", [D], F32, EI), dtw=d("dtw", [DI, DI], F32, EI),
        xpw2=d("xpw2", [NS, DI], F32, EI), opw=d("opw", [D, DI], F32, EI),
        alog=d("alog", [DI, NS], F32, EI),
        wqT=d("wqT", [D, D], BF16), wkT=d("wkT", [D, D], BF16),
        wvT=d("wvT", [D, D], BF16), woT=d("woT", [D, D], BF16),
        iprojT=d("iprojT", [D, 2 * DI], BF16), dtwT=d("dtwT", [DI, DI], BF16),
        xpT=d("xpT", [DI, NS], BF16), opT=d("opT", [DI, D], BF16),
        asb=d("asb", [128, NDI * NS], F32), xn0fm=d("xn0fm", [D, T], BF16),
    )
    ioa = dict(
        xn0fm=iop["xn0fm"], wqT=iop["wqT"], wkT=iop["wkT"], wvT=iop["wvT"],
        woT=iop["woT"], bq=d("bq", [D], F32, EI), bk=d("bk", [D], F32, EI),
        bv=d("bv", [D], F32, EI), bo=d("bo", [D], F32, EI),
        scale128=d("scale128", [128], F32, EI), s0=d("s0", [T, D], F32),
    )
    iof0 = dict(
        xn0fm=iop["xn0fm"], wrr=d("wrr", [128, ND * NE], F32, EI),
        w1=d("w1_0", [NE, D, FF], F32, EI),
        b1r=d("b1r_0", [128, NE * (FF // 128)], F32, EI),
        w2=d("w2_0", [NE, FF, D], F32, EI), b2=d("b2_0", [NE, D], F32, EI),
        x0=iop["x0"], s0=ioa["s0"], x1=d("x1", [T, D], F32),
        xn1=d("xn1", [T, D], F32), xn1sfm=d("xn1sfm", [D, T], BF16),
        xhatfm=d("xhatfm", [D, T], BF16), xn1fm=d("xn1fm", [D, T], BF16),
    )
    ios = dict(
        xn1sfm=iof0["xn1sfm"], iprojT=iop["iprojT"],
        cwr=d("cwr", [128, NDI * KC], F32, EI),
        cbr=d("cbr", [128, NDI], F32, EI), xpT=iop["xpT"],
        dtwT=iop["dtwT"], dtbr=d("dtbr", [128, NDI], F32, EI),
        xc=d("xc", [DI, T], BF16), dtm=d("dtm", [DI, T], F32),
        bm=d("bm", [NS, T], BF16), sres=d("sres", [DI, T], BF16),
    )
    ioc = dict(
        xc=ios["xc"], dtm=ios["dtm"], bm=ios["bm"], asb=iop["asb"],
        dpr=d("dpr", [128, NDI], F32, EI), sres=ios["sres"], opT=iop["opT"],
        xn1=iof0["xn1"], s1=d("s1", [T, D], F32),
        y2=d("y2", [DI, T], BF16),
    )
    iof1 = dict(
        xn1fm=iof0["xn1fm"], xhatfm=iof0["xhatfm"],
        wgr=d("wgr", [128, ND * NE], F32, EI),
        gr=d("gr", [128, NE * ND], F32, EI),
        br=d("br", [128, NE * ND], F32, EI),
        w1=d("w1_1", [NE, D, FF], F32, EI),
        b1r=d("b1r_1", [128, NE * (FF // 128)], F32, EI),
        w2=d("w2_1", [NE, FF, D], F32, EI), b2=d("b2_1", [NE, D], F32, EI),
        xn1=iof0["xn1"], x1=iof0["x1"], s1=ioc["s1"],
        out=d("out", [T, D], F32, EO),
    )
    p.emit_consts()
    g = p.nc.gpsimd
    for emit, io in ((emit_stage_p, iop), (emit_stage_a, ioa),
                     (emit_stage_f0, iof0), (emit_stage_s, ios)):
        p.push_scope()
        p.prefix = emit.__name__.replace("emit_stage_", "") + "_"
        emit(p, io)
        p.pop_scope()
        for sm in ("S", "S2", "L", "L2"):
            p.wait(g, sm)
        p.bar()
    p.push_scope()
    emit_stage_cf1(p, ioc, iof1)
    p.pop_scope()
    return p.done()


_progs = {}


def _get_prog(name, builder):
    if name not in _progs:
        _progs[name] = builder()
    return _progs[name]


def run_stage(name, builder, in_maps):
    nc = _get_prog(name, builder)
    res = run_bass_kernel_spmd(nc, in_maps, core_ids=list(range(8)))
    return res.results


def _host_layouts(inp):
    asc = np.ascontiguousarray
    cw = inp["L1_attn_conv_w"].reshape(DI, KC)

    def pcol(a, c):
        return asc(np.asarray(a).reshape(c, 128).T)

    return dict(
        xpw2=asc(inp["L1_attn_x_proj_w"][NS:, :]),
        wrr=asc(inp["L0_ffn_Wr"].reshape(ND, 128, NE).transpose(1, 0, 2)
                .reshape(128, ND * NE)),
        b1r_0=asc(inp["L0_ffn_b1"].reshape(NE, 32, 128).transpose(2, 0, 1)
                  .reshape(128, NE * 32)),
        cwr=asc(cw.reshape(NDI, 128, KC).transpose(1, 0, 2)
                .reshape(128, NDI * KC)),
        cbr=pcol(inp["L1_attn_conv_b"], NDI),
        dtbr=pcol(inp["L1_attn_dt_b"], NDI),
        dpr=pcol(inp["L1_attn_D_param"], NDI),
        wgr=asc(inp["L1_ffn_Wg"].reshape(ND, 128, NE).transpose(1, 0, 2)
                .reshape(128, ND * NE)),
        gr=asc(inp["L1_ffn_gamma"].reshape(NE, ND, 128).transpose(2, 0, 1)
               .reshape(128, NE * ND)),
        br=asc(inp["L1_ffn_beta"].reshape(NE, ND, 128).transpose(2, 0, 1)
               .reshape(128, NE * ND)),
        b1r_1=asc(inp["L1_ffn_b1"].reshape(NE, 32, 128).transpose(2, 0, 1)
                  .reshape(128, NE * 32)),
        scale128=np.repeat(np.asarray(inp["L0_attn_attn_scale"]), 128),
    )


def kernel(**inp):
    B = 8
    asc = np.ascontiguousarray
    hl = _host_layouts(inp)
    x_all = asc(inp["L0_attn_x"])
    maps = [dict(
        x0=asc(x_all[b]),
        wq=inp["L0_attn_Wq"], wk=inp["L0_attn_Wk"],
        wv=inp["L0_attn_Wv"], wo=inp["L0_attn_Wo"],
        ipw=inp["L1_attn_in_proj_w"], nw=inp["L1_attn_norm_w"],
        dtw=inp["L1_attn_dt_w"], opw=inp["L1_attn_out_proj_w"],
        alog=inp["L1_attn_A_log"],
        bq=inp["L0_attn_bq"], bk=inp["L0_attn_bk"],
        bv=inp["L0_attn_bv"], bo=inp["L0_attn_bo"],
        w1_0=inp["L0_ffn_W1"], w2_0=inp["L0_ffn_W2"], b2_0=inp["L0_ffn_b2"],
        w1_1=inp["L1_ffn_W1"], w2_1=inp["L1_ffn_W2"], b2_1=inp["L1_ffn_b2"],
        **hl,
    ) for b in range(B)]
    res = run_stage("fused", build_fused, maps)
    return np.stack([np.asarray(res[b]["out"], np.float32) for b in range(B)])


# version 08
# speedup vs baseline: 1.0042x; 1.0042x over previous
import contextlib
import numpy as np
import ml_dtypes
import concourse.bass as bass
import concourse.mybir as mybir
from concourse.bass_utils import run_bass_kernel_spmd

F32 = mybir.dt.float32
BF16 = mybir.dt.bfloat16
AF = mybir.ActivationFunctionType
ALU = mybir.AluOpType
AX = mybir.AxisListType.X

T = 2048
D = 1024
NT = T // 128
ND = D // 128
NH = 16
DH = 64
NE = 8
FF = 4096
DI = 2048
NDI = DI // 128
NS = 64
KC = 4

nbf = ml_dtypes.bfloat16


class Prog:
    def __init__(self, name="prog"):
        self.nc = bass.Bass("TRN2", target_bir_lowering=False)
        self.stack = contextlib.ExitStack()
        self.scopes = [self.stack]
        self.cnt = {}
        self.sems = {}
        for s in ("L", "S", "P", "V", "A", "G", "S2", "L2", "L3", "L4",
                  "L5", "G2"):
            self.sems[s] = self.stack.enter_context(self.nc.semaphore(s))
            self.cnt[s] = 0

    prefix = ""

    def push_scope(self):
        self.scopes.append(contextlib.ExitStack())

    def pop_scope(self):
        self.scopes.pop().close()

    def sbuf(self, name, shape, dt):
        return self.scopes[-1].enter_context(
            self.nc.sbuf_tensor(self.prefix + name, shape, dt))

    def psum(self, name, shape, dt):
        return self.scopes[-1].enter_context(
            self.nc.psum_tensor(self.prefix + name, shape, dt))

    def dram(self, name, shape, dt, kind=None):
        if kind is None:
            return self.nc.dram_tensor(name, shape, dt).ap()
        return self.nc.dram_tensor(name, shape, dt, kind=kind).ap()

    def inc(self, inst, sem, n=1):
        inst.then_inc(self.sems[sem], n)
        self.cnt[sem] += n
        return inst

    def ld(self, out, in_, sem="L", **kw):
        self.inc(self.nc.sync.dma_start(out=out, in_=in_, **kw), sem, 16)
        return self.cnt[sem]

    def st(self, out, in_, sem="S", **kw):
        self.inc(self.nc.gpsimd.dma_start(out=out, in_=in_, **kw), sem, 16)
        return self.cnt[sem]

    def wait(self, eng, sem, val=None):
        if val is None:
            val = self.cnt[sem]
        if val > 0:
            eng.wait_ge(self.sems[sem], val)

    def bar(self):
        self.nc.all_engine_barrier()

    def emit_consts(self):
        c = self.sbuf("consts", [128, 8], F32)
        v = self.nc.vector
        for i, val in enumerate([-1.0, 1.0, -0.5, 1.0 / D, 1e-5, 1e-6,
                                 1.0 / DI, 0.0]):
            v.memset(c[:, i:i + 1], val)
        ident = self.sbuf("identity", [128, 128], BF16)
        g = self.nc.gpsimd
        self.inc(g.iota(ident[:, :], [[1, 128]], channel_multiplier=-1,
                        allow_small_or_imprecise_dtypes=True), "G")
        self.wait(g, "G")
        g.tensor_single_scalar(out=ident[:], in_=ident[:],
                               scalar=0.0, op=ALU.is_equal)
        self.consts = c
        self.ident = ident
        self.bar()

    def C(self, i):
        return self.consts[:, i:i + 1]

    def done(self):
        self.stack.close()
        return self.nc


def emit_w_to_bf16(p, scr, src, dst, nelem):
    nc = p.nc
    v, g = nc.vector, nc.gpsimd
    CH = 128 * 4096
    nch = nelem // CH
    assert nelem % CH == 0
    stgf = scr["wc_stgf"]
    stgb = scr["wc_stgb"]
    sflat = src
    cast_marks = [p.cnt["V"]] * 2
    store_marks = [p.cnt["S"], p.cnt["S2"]]
    for c in range(nch):
        b = c % 2
        lsem = "L" if b == 0 else "L2"
        ssem = "S" if b == 0 else "S2"
        p.wait(nc.sync, "V", cast_marks[b])
        p.ld(stgf[:, b * 4096:(b + 1) * 4096],
             sflat[c * CH:(c + 1) * CH].rearrange("(p f) -> p f", p=128),
             sem=lsem)
        p.wait(v, lsem)
        p.wait(v, ssem, store_marks[b])
        p.inc(v.tensor_copy(out=stgb[:, b * 4096:(b + 1) * 4096],
                            in_=stgf[:, b * 4096:(b + 1) * 4096]), "V")
        cast_marks[b] = p.cnt["V"]
        p.wait(g, "V")
        p.st(dst[c * CH:(c + 1) * CH].rearrange("(p f) -> p f", p=128),
             stgb[:, b * 4096:(b + 1) * 4096], sem=ssem)
        store_marks[b] = p.cnt[ssem]
    p.bar()


def emit_weight_transpose(p, scr, src, dst, R, C, scale_sb=None):
    nc = p.nc
    assert C <= 2048 and R % 128 == 0 and C % 128 == 0
    nr, ncc = R // 128, C // 128
    stgf, otile, pst = scr["stgf"], scr["otile"], scr["pst"]
    v, t, g = nc.vector, nc.tensor, nc.gpsimd
    W = 2048
    cast_marks = [p.cnt["V"]] * 2
    evac_marks = [p.cnt["V"]] * 2
    store_marks = [p.cnt["S"], p.cnt["S2"]]
    for r in range(nr):
        b = r % 2
        lsem = "L" if b == 0 else "L2"
        p.wait(nc.sync, "V", cast_marks[b])
        p.ld(stgf[:, b * W:b * W + C], src[r * 128:(r + 1) * 128, :],
             sem=lsem)
        p.wait(v, lsem)
        p.wait(v, "S" if b == 0 else "S2", store_marks[b])
        p.inc(v.tensor_copy(out=otile[:, b * W:b * W + C],
                            in_=stgf[:, b * W:b * W + C]), "V")
        cast_marks[b] = p.cnt["V"]
        p.wait(t, "V")
        p.wait(t, "V", evac_marks[b])
        for cc in range(ncc):
            inst = t.transpose(pst[b][:, cc * 128:(cc + 1) * 128],
                               otile[:, b * W + cc * 128:b * W + (cc + 1) * 128],
                               p.ident[:, :])
        p.inc(inst, "P")
        p.wait(v, "P")
        if scale_sb is None:
            p.inc(v.tensor_copy(out=otile[:, b * W:b * W + C],
                                in_=pst[b][:, 0:C]), "V")
        else:
            for cc in range(ncc):
                inst = v.tensor_scalar(
                    out=otile[:, b * W + cc * 128:b * W + (cc + 1) * 128],
                    in0=pst[b][:, cc * 128:(cc + 1) * 128],
                    scalar1=scale_sb[:, cc:cc + 1], scalar2=None,
                    op0=ALU.mult)
            p.inc(inst, "V")
        evac_marks[b] = p.cnt["V"]
        p.wait(g, "V")
        ssem = "S" if b == 0 else "S2"
        for cc in range(ncc):
            p.st(dst[cc * 128:(cc + 1) * 128, r * 128:(r + 1) * 128],
                 otile[:, b * W + cc * 128:b * W + (cc + 1) * 128], sem=ssem)
        store_marks[b] = p.cnt[ssem]
    p.bar()


def emit_ln_to_fm(p, x_dram, xfm_dram):
    nc = p.nc
    v, s, t, g = nc.vector, nc.scalar, nc.tensor, nc.gpsimd
    xt = p.sbuf("ln_x", [128, 2 * D], F32)
    xnb = p.sbuf("ln_xnb", [128, 2 * D], BF16)
    trash = p.sbuf("ln_trash", [128, D], BF16)
    stat = p.sbuf("ln_stat", [128, 16], F32)
    pst = [p.psum(f"ln_ps{i}", [128, D], BF16) for i in range(2)]
    xread_marks = [0, 0]
    evac_marks = [0, 0]
    store_marks = [0, 0]
    for tt in range(NT):
        b = tt % 2
        o = 8 * b
        lsem = "L" if b == 0 else "L2"
        p.wait(nc.sync, "V", xread_marks[b])
        ldv = p.ld(xt[:, b * D:(b + 1) * D],
                   x_dram[tt * 128:(tt + 1) * 128, :], sem=lsem)
        xv = xt[:, b * D:(b + 1) * D]
        p.wait(v, lsem, ldv)
        p.inc(v.reduce_sum(stat[:, o:o + 1], xv, axis=AX), "V")
        p.wait(s, lsem, ldv)
        p.wait(s, "V", xread_marks[b])
        p.inc(s.activation(out=trash[:, :], in_=xv, func=AF.Square,
                           accum_out=stat[:, o + 1:o + 2]), "A")
        sq_mark = p.cnt["A"]
        p.wait(v, "A", sq_mark)
        p.wait(v, "V")
        p.inc(v.tensor_scalar(out=stat[:, o + 3:o + 4], in0=stat[:, o:o + 1],
                              scalar1=-1.0 / D, scalar2=None, op0=ALU.mult),
              "V")
        p.wait(v, "V")
        v.tensor_tensor(out=stat[:, o + 6:o + 7], in0=stat[:, o + 3:o + 4],
                        in1=stat[:, o + 3:o + 4], op=ALU.mult)
        p.inc(v.tensor_scalar(out=stat[:, o + 4:o + 5],
                              in0=stat[:, o + 1:o + 2],
                              scalar1=1.0 / D, scalar2=None, op0=ALU.mult),
              "V")
        p.wait(v, "V")
        p.inc(v.tensor_tensor(out=stat[:, o + 5:o + 6],
                              in0=stat[:, o + 4:o + 5],
                              in1=stat[:, o + 6:o + 7], op=ALU.subtract), "V")
        var_mark = p.cnt["V"]
        p.wait(s, "V", var_mark)
        p.inc(s.activation(out=stat[:, o + 2:o + 3], in_=stat[:, o + 5:o + 6],
                           func=AF.Ln, bias=p.C(4)), "A")
        p.wait(s, "A")
        p.inc(s.activation(out=stat[:, o + 2:o + 3], in_=stat[:, o + 2:o + 3],
                           func=AF.Exp, scale=p.C(2)), "A")
        p.wait(v, "A")
        p.wait(v, "S" if b == 0 else "S2", store_marks[b])
        p.inc(v.tensor_scalar(out=xnb[:, b * D:(b + 1) * D], in0=xv,
                              scalar1=stat[:, o + 3:o + 4],
                              scalar2=stat[:, o + 2:o + 3],
                              op0=ALU.add, op1=ALU.mult), "V")
        xread_marks[b] = p.cnt["V"]
        p.wait(t, "V")
        p.wait(t, "V", evac_marks[b])
        for dc in range(ND):
            inst = t.transpose(pst[b][:, dc * 128:(dc + 1) * 128],
                               xnb[:, b * D + dc * 128:b * D + (dc + 1) * 128],
                               p.ident[:, :])
        p.inc(inst, "P")
        p.wait(v, "P")
        p.inc(v.tensor_copy(out=xnb[:, b * D:(b + 1) * D], in_=pst[b][:, :]),
              "V")
        evac_marks[b] = p.cnt["V"]
        p.wait(g, "V")
        ssem = "S" if b == 0 else "S2"
        for dc in range(ND):
            p.st(xfm_dram[dc * 128:(dc + 1) * 128, tt * 128:(tt + 1) * 128],
                 xnb[:, b * D + dc * 128:b * D + (dc + 1) * 128], sem=ssem)
        store_marks[b] = p.cnt[ssem]
    p.bar()


def emit_stage_p(p, io):
    nc = p.nc
    x0 = io["x0"]
    wq = io["wq"]
    wk = io["wk"]
    wv = io["wv"]
    wo = io["wo"]
    ipw = io["ipw"]
    nw = io["nw"]
    dtw = io["dtw"]
    xpw2 = io["xpw2"]
    opw = io["opw"]
    alog = io["alog"]

    wqT = io["wqT"]
    wkT = io["wkT"]
    wvT = io["wvT"]
    woT = io["woT"]
    iprojT = io["iprojT"]
    dtwT = io["dtwT"]
    xpT = io["xpT"]
    opT = io["opT"]
    asb_d = io["asb"]
    xn0fm = io["xn0fm"]


    v, s = nc.vector, nc.scalar
    at = p.sbuf("a_t", [128, NDI * NS], F32)
    p.ld(at[:, :].rearrange("p (c n) -> p c n", n=NS),
         alog.rearrange("(c p) n -> p c n", p=128),
         allow_slow_non_contiguous=True)
    p.wait(s, "L")
    p.inc(s.activation(out=at[:, :], in_=at[:, :], func=AF.Exp), "A")
    p.wait(v, "A")
    p.inc(v.tensor_scalar(out=at[:, :], in0=at[:, :], scalar1=-1.0,
                          scalar2=None, op0=ALU.mult), "V")
    p.wait(nc.gpsimd, "V")
    p.st(asb_d[:, :], at[:, :])
    nwt = p.sbuf("nw_t", [128, ND], F32)
    p.ld(nwt[:, :], nw.rearrange("(c p) -> p c", p=128),
         allow_slow_non_contiguous=True, sem="L2")
    p.wait(v, "L2")
    p.bar()

    scr = dict(
        stgf=p.sbuf("wt_stgf", [128, 2 * 2048], F32),
        otile=p.sbuf("wt_ot", [128, 2 * 2048], BF16),
        pst=[p.psum(f"wt_ps{i}", [128, 2048], BF16) for i in range(2)],
    )
    emit_weight_transpose(p, scr, wq, wqT, D, D)
    emit_weight_transpose(p, scr, wk, wkT, D, D)
    emit_weight_transpose(p, scr, wv, wvT, D, D)
    emit_weight_transpose(p, scr, wo, woT, D, D)
    emit_weight_transpose(p, scr, ipw, iprojT, 2 * DI, D, scale_sb=nwt)
    emit_weight_transpose(p, scr, dtw, dtwT, DI, DI)
    emit_weight_transpose(p, scr, opw, opT, D, DI)

    t, g = nc.tensor, nc.gpsimd
    xp_stgf = scr["stgf"]
    xp_stgb = scr["otile"]
    xp_ot_full = scr["otile"]
    xp_ps = scr["pst"][0]
    p.wait(v, "S")
    p.wait(v, "S2")
    p.inc(v.memset(xp_stgb[:, 0:DI], 0.0), "V")
    p.ld(xp_stgf[0:64, 0:DI], xpw2[:, :])
    p.wait(v, "L")
    p.wait(v, "V")
    p.inc(v.tensor_copy(out=xp_stgb[0:64, 0:DI], in_=xp_stgf[0:64, 0:DI]),
          "V")
    p.wait(t, "V")
    for cc in range(NDI):
        inst = t.transpose(xp_ps[:, cc * 128:(cc + 1) * 128],
                           xp_stgb[:, cc * 128:(cc + 1) * 128],
                           p.ident[:, :])
    p.inc(inst, "P")
    p.wait(v, "P")
    for cc in range(NDI):
        inst = v.tensor_copy(out=xp_ot_full[:, 2048 + cc * NS: 2048 + (cc + 1) * NS],
                             in_=xp_ps[:, cc * 128:cc * 128 + NS])
    p.inc(inst, "V")
    p.wait(g, "V")
    for cc in range(NDI):
        p.st(xpT[cc * 128:(cc + 1) * 128, :],
             xp_ot_full[:, 2048 + cc * NS: 2048 + (cc + 1) * NS])
    p.bar()

    p.push_scope()
    wscr = dict(wc_stgf=p.sbuf("wc_stgf", [128, 2 * 4096], F32),
                wc_stgb=p.sbuf("wc_stgb", [128, 2 * 4096], BF16))
    for wsrc, wdst in (("w1_0", "w1b_0"), ("w2_0", "w2b_0"),
                       ("w1_1", "w1b_1"), ("w2_1", "w2b_1")):
        emit_w_to_bf16(p, wscr,
                       io[wsrc].rearrange("e a b -> (e a b)"),
                       io[wdst], NE * D * FF)
    p.pop_scope()

    emit_ln_to_fm(p, x0, xn0fm)

    g = nc.gpsimd
    p.wait(g, "S")
    p.wait(g, "L")
def build_stage_p():
    p = Prog("p")
    io = {}
    io["x0"] = p.dram("x0", [T, D], F32, "ExternalInput")
    io["xpw2"] = p.dram("xpw2", [NS, DI], F32, "ExternalInput")
    io["wq"] = p.dram("wq", [D, D], F32, "ExternalInput")
    io["wk"] = p.dram("wk", [D, D], F32, "ExternalInput")
    io["wv"] = p.dram("wv", [D, D], F32, "ExternalInput")
    io["wo"] = p.dram("wo", [D, D], F32, "ExternalInput")
    io["ipw"] = p.dram("ipw", [2 * DI, D], F32, "ExternalInput")
    io["nw"] = p.dram("nw", [D], F32, "ExternalInput")
    io["dtw"] = p.dram("dtw", [DI, DI], F32, "ExternalInput")
    io["opw"] = p.dram("opw", [D, DI], F32, "ExternalInput")
    io["alog"] = p.dram("alog", [DI, NS], F32, "ExternalInput")
    io["wqT"] = p.dram("wqT", [D, D], BF16, "ExternalOutput")
    io["wkT"] = p.dram("wkT", [D, D], BF16, "ExternalOutput")
    io["wvT"] = p.dram("wvT", [D, D], BF16, "ExternalOutput")
    io["woT"] = p.dram("woT", [D, D], BF16, "ExternalOutput")
    io["iprojT"] = p.dram("iprojT", [D, 2 * DI], BF16, "ExternalOutput")
    io["dtwT"] = p.dram("dtwT", [DI, DI], BF16, "ExternalOutput")
    io["xpT"] = p.dram("xpT", [DI, NS], BF16, "ExternalOutput")
    io["opT"] = p.dram("opT", [DI, D], BF16, "ExternalOutput")
    io["asb"] = p.dram("asb", [128, NDI * NS], F32, "ExternalOutput")
    io["xn0fm"] = p.dram("xn0fm", [D, T], BF16, "ExternalOutput")
    p.emit_consts()
    p.push_scope()
    emit_stage_p(p, io)
    p.pop_scope()
    return p.done()



def emit_stage_a(p, io):
    nc = p.nc
    v, s, t, g = nc.vector, nc.scalar, nc.tensor, nc.gpsimd
    xn0fm = io["xn0fm"]
    wqT = io["wqT"]
    wkT = io["wkT"]
    wvT = io["wvT"]
    woT = io["woT"]
    bq = io["bq"]
    bk = io["bk"]
    bv = io["bv"]
    bo = io["bo"]
    scale128 = io["scale128"]
    s0_d = io["s0"]


    xn0 = p.sbuf("xn0", [128, ND * T], BF16)
    wbuf = p.sbuf("wbuf", [128, ND * D], BF16)
    wq_sb = wk_sb = wv_sb = wo_sb = wbuf
    q_sb = p.sbuf("q_sb", [128, ND * T], BF16)
    k_sb = p.sbuf("k_sb", [128, ND * T], BF16)
    v_sb = p.sbuf("v_sb", [128, NT * D], BF16)
    bias_sb = p.sbuf("bias_sb", [128, 4 * ND], F32)
    scale_sb = p.sbuf("scale_sb", [128, 1], F32)
    brow = p.sbuf("brow", [1, 2 * D], BF16)
    browf = p.sbuf("browf", [1, 2 * D], F32)
    bvb = p.sbuf("bv_bcast", [128, D], F32)
    bob = p.sbuf("bo_bcast", [128, D], F32)
    ones1 = p.sbuf("ones1", [1, 128], BF16)
    maskc = p.sbuf("maskc", [128, 128], F32)
    mk_t = p.sbuf("mk_t", [128, 3 * 128], F32)
    mk_b = p.sbuf("mk_b", [128, 2], F32)
    probs = p.sbuf("probs", [128, 2 * T], BF16)
    probsT = p.sbuf("probsT", [128, T], BF16)
    ao_sb = p.sbuf("ao_sb", [128, ND * 128], BF16)
    att_st = p.sbuf("att_st", [128, 8], F32)
    s0t = p.sbuf("s0t", [128, 2 * D], F32)

    ps_sc = p.psum("ps_sc", [128, T], F32)
    ps_tr = p.psum("ps_tr", [128, T], BF16)
    ps_av = p.psum("ps_av", [128, 128], F32)
    ps_s0 = p.psum("ps_s0", [128, 512], F32)

    for dc in range(ND):
        p.ld(xn0[:, dc * T:(dc + 1) * T], xn0fm[dc * 128:(dc + 1) * 128, :])
    p.ld(wbuf[:, :].rearrange("p (c o) -> p c o", c=ND),
         wqT.rearrange("(c p) o -> p c o", p=128))
    p.ld(bias_sb[:, 0:ND], bq.rearrange("(c p) -> p c", p=128),
         allow_slow_non_contiguous=True)
    p.ld(bias_sb[:, ND:2 * ND], bk.rearrange("(c p) -> p c", p=128),
         allow_slow_non_contiguous=True)
    p.ld(scale_sb[:, 0:1], scale128[:, None])
    p.ld(browf[0:1, 0:D], bv[None, :])
    p.ld(browf[0:1, D:2 * D], bo[None, :])
    p.inc(g.iota(mk_t[:, 0:128], [[1, 2], [0, 64]], channel_multiplier=0,
                 allow_small_or_imprecise_dtypes=True), "G")
    p.inc(g.iota(mk_t[:, 128:256], [[1, 128]], channel_multiplier=-1,
                 allow_small_or_imprecise_dtypes=True), "G")
    p.inc(g.iota(mk_b[:, 0:1], [[0, 1]], channel_multiplier=1,
                 allow_small_or_imprecise_dtypes=True), "G")
    p.wait(v, "G")
    p.inc(v.tensor_scalar(out=mk_b[:, 1:2], in0=mk_b[:, 0:1], scalar1=63.5,
                          scalar2=None, op0=ALU.is_gt), "V")
    p.wait(v, "V")
    p.inc(v.tensor_scalar(out=mk_t[:, 256:384], in0=mk_t[:, 0:128],
                          scalar1=mk_b[:, 1:2], scalar2=None, op0=ALU.is_gt),
          "V")
    p.wait(v, "V")
    p.inc(v.tensor_scalar(out=mk_t[:, 0:128], in0=mk_t[:, 0:128],
                          scalar1=mk_b[:, 1:2], scalar2=None,
                          op0=ALU.is_equal), "V")
    p.wait(v, "V")
    p.inc(v.tensor_single_scalar(out=mk_t[:, 128:256], in_=mk_t[:, 128:256],
                                 scalar=0.5, op=ALU.is_le), "V")
    p.wait(v, "V")
    p.inc(v.tensor_tensor(out=mk_t[:, 0:128], in0=mk_t[:, 0:128],
                          in1=mk_t[:, 128:256], op=ALU.mult), "V")
    p.wait(v, "V")
    p.inc(v.tensor_tensor(out=mk_t[:, 0:128], in0=mk_t[:, 0:128],
                          in1=mk_t[:, 256:384], op=ALU.max), "V")
    p.wait(v, "V")
    p.inc(v.tensor_scalar(out=maskc[:, :], in0=mk_t[:, 0:128], scalar1=-1.0,
                          scalar2=1e30, op0=ALU.add, op1=ALU.mult), "V")
    p.wait(v, "L")
    p.inc(v.tensor_scalar(out=bias_sb[:, 0:ND], in0=bias_sb[:, 0:ND],
                          scalar1=scale_sb[:, 0:1], scalar2=None,
                          op0=ALU.mult), "V")
    v.memset(ones1[0:1, :], 1.0)
    p.wait(v, "V")
    p.inc(v.tensor_copy(out=brow[0:1, :], in_=browf[0:1, :]), "V")
    p.wait(t, "V")
    for j, (dst, off) in enumerate(((bvb, 0), (bvb, 512), (bob, D),
                                    (bob, D + 512))):
        inst = t.matmul(ps_sc[:, (j % 4) * 512:(j % 4) * 512 + 512],
                        ones1[0:1, :], brow[0:1, off:off + 512],
                        start=True, stop=True)
    p.inc(inst, "P")
    p.wait(v, "P")
    v.tensor_copy(out=bvb[:, :], in_=ps_sc[:, 0:1024])
    p.inc(v.tensor_copy(out=bob[:, :], in_=ps_sc[:, 1024:2048]), "V")
    p.bar()

    def proj_qk(out_sb, bias_off, use_scale):
        evac_marks = [0, 0]
        for och in range(ND):
            for tch in range(4):
                half = (och * 4 + tch) % 2
                pso = ps_s0 if half == 0 else ps_sc
                psv = pso[:, 0:512]
                p.wait(t, "V", evac_marks[half])
                for dch in range(ND):
                    inst = t.matmul(
                        psv,
                        wbuf[:, dch * D + och * 128:dch * D + (och + 1) * 128],
                        xn0[:, dch * T + tch * 512:dch * T + (tch + 1) * 512],
                        start=(dch == 0), stop=(dch == ND - 1))
                p.inc(inst, "P")
                p.wait(v, "P")
                if use_scale:
                    inst = v.tensor_scalar(
                        out=out_sb[:, och * T + tch * 512:
                                   och * T + (tch + 1) * 512],
                        in0=psv, scalar1=scale_sb[:, 0:1],
                        scalar2=bias_sb[:, bias_off + och:bias_off + och + 1],
                        op0=ALU.mult, op1=ALU.add)
                else:
                    inst = v.tensor_scalar(
                        out=out_sb[:, och * T + tch * 512:
                                   och * T + (tch + 1) * 512],
                        in0=psv,
                        scalar1=bias_sb[:, bias_off + och:bias_off + och + 1],
                        scalar2=None, op0=ALU.add)
                p.inc(inst, "V")
                evac_marks[half] = p.cnt["V"]

    proj_qk(q_sb, 0, True)
    p.bar()
    p.ld(wbuf[:, :].rearrange("p (c o) -> p c o", c=ND),
         wkT.rearrange("(c p) o -> p c o", p=128))
    p.wait(t, "L")
    proj_qk(k_sb, ND, False)
    p.bar()
    p.ld(wbuf[:, :].rearrange("p (c o) -> p c o", c=ND),
         wvT.rearrange("(c p) o -> p c o", p=128))
    p.wait(t, "L")
    evac_marks = [0, 0]
    for t128 in range(NT):
        for oh in range(2):
            half = (t128 * 2 + oh) % 2
            pso = ps_s0 if half == 0 else ps_sc
            psv = pso[:, 0:512]
            p.wait(t, "V", evac_marks[half])
            for dch in range(ND):
                inst = t.matmul(
                    psv,
                    xn0[:, dch * T + t128 * 128:dch * T + (t128 + 1) * 128],
                    wbuf[:, dch * D + oh * 512:dch * D + (oh + 1) * 512],
                    start=(dch == 0), stop=(dch == ND - 1))
            p.inc(inst, "P")
            p.wait(v, "P")
            p.inc(v.tensor_tensor(
                out=v_sb[:, t128 * D + oh * 512:t128 * D + (oh + 1) * 512],
                in0=psv, in1=bvb[:, oh * 512:(oh + 1) * 512], op=ALU.add),
                "V")
            evac_marks[half] = p.cnt["V"]
    p.bar()
    p.ld(wbuf[:, :].rearrange("p (c o) -> p c o", c=ND),
         woT.rearrange("(c p) o -> p c o", p=128))
    p.wait(t, "L")
    p.bar()

    exp_mark = 0
    trans_mark = 0
    trev_mark = 0
    aoev_mark = 0
    s0proj_mark = 0
    s0ev_mark = 0
    s0st_marks = [0, 0]
    probs_free = [0, 0]
    for tq in range(NT):
        for h in range(NH):
            hb = h % 2
            po = (h % 2) * 64
            och = h // 2
            koff = 128 * tq
            W = T - koff
            nk1 = W // 128
            qsl = q_sb[po:po + 64,
                       och * T + tq * 128:och * T + (tq + 1) * 128]
            p.wait(t, "A", exp_mark)
            nfull = (W + 511) // 512
            for c5 in range(nfull):
                wmm = min(512, W - c5 * 512)
                inst = t.matmul(
                    ps_sc[:, c5 * 512:c5 * 512 + wmm], qsl,
                    k_sb[po:po + 64,
                         och * T + koff + c5 * 512:
                         och * T + koff + c5 * 512 + wmm],
                    start=True, stop=True)
            p.inc(inst, "P")
            p.wait(v, "P")
            p.inc(v.tensor_tensor(out=ps_sc[:, 0:128], in0=ps_sc[:, 0:128],
                                  in1=maskc[:, :], op=ALU.add), "V")
            p.wait(v, "V")
            p.inc(v.reduce_max(att_st[:, 0:1], ps_sc[:, 0:W], axis=AX), "V")
            p.wait(v, "V")
            p.inc(v.tensor_scalar(out=att_st[:, 1:2], in0=att_st[:, 0:1],
                                  scalar1=-1.0, scalar2=None, op0=ALU.mult),
                  "V")
            p.wait(s, "V")
            p.wait(s, "P", probs_free[hb])
            p.inc(s.activation(out=probs[:, hb * T:hb * T + W],
                               in_=ps_sc[:, 0:W], func=AF.Exp,
                               bias=att_st[:, 1:2],
                               accum_out=att_st[:, 2:3]), "A")
            exp_mark = p.cnt["A"]
            p.wait(v, "A")
            p.inc(v.reciprocal(att_st[:, 3:4], att_st[:, 2:3]), "V")
            p.wait(v, "V")
            p.inc(v.tensor_scalar(out=probs[:, hb * T:hb * T + W],
                                  in0=probs[:, hb * T:hb * T + W],
                                  scalar1=att_st[:, 3:4], scalar2=None,
                                  op0=ALU.mult), "V")
            norm_mark = p.cnt["V"]
            p.wait(t, "V", norm_mark)
            p.wait(t, "V", trev_mark)
            for kb in range(nk1):
                inst = t.transpose(ps_tr[:, kb * 128:(kb + 1) * 128],
                                   probs[:, hb * T + kb * 128:
                                         hb * T + (kb + 1) * 128],
                                   p.ident[:, :])
            p.inc(inst, "P")
            probs_free[hb] = p.cnt["P"]
            trans_mark = p.cnt["P"]
            p.wait(v, "P", trans_mark)
            p.inc(v.tensor_copy(out=probsT[:, 0:W], in_=ps_tr[:, 0:W]), "V")
            trev_mark = p.cnt["V"]
            p.wait(t, "V", trev_mark)
            p.wait(t, "V", aoev_mark)
            for kb in range(nk1):
                inst = t.matmul(ps_av[0:64, :],
                                v_sb[:, (tq + kb) * D + h * 64:
                                     (tq + kb) * D + h * 64 + 64],
                                probsT[:, kb * 128:(kb + 1) * 128],
                                start=(kb == 0), stop=(kb == nk1 - 1))
            p.inc(inst, "P")
            p.wait(v, "P")
            p.wait(v, "P", s0proj_mark)
            p.inc(v.tensor_copy(
                out=ao_sb[po:po + 64, och * 128:(och + 1) * 128],
                in_=ps_av[0:64, :]), "V")
            aoev_mark = p.cnt["V"]
        tb = tq % 2
        p.wait(t, "V")
        for oh in range(2):
            p.wait(t, "V", s0ev_mark)
            for och in range(ND):
                inst = t.matmul(ps_s0[:, :],
                                ao_sb[:, och * 128:(och + 1) * 128],
                                wo_sb[:, och * D + oh * 512:
                                      och * D + (oh + 1) * 512],
                                start=(och == 0), stop=(och == ND - 1))
            p.inc(inst, "P")
            if oh == 1:
                s0proj_mark = p.cnt["P"]
            p.wait(v, "P")
            p.wait(v, "S" if tb == 0 else "S2", s0st_marks[tb])
            p.inc(v.tensor_tensor(
                out=s0t[:, tb * D + oh * 512:tb * D + (oh + 1) * 512],
                in0=ps_s0[:, :], in1=bob[:, oh * 512:(oh + 1) * 512],
                op=ALU.add), "V")
            s0ev_mark = p.cnt["V"]
            p.wait(g, "V")
            ssem = "S" if tb == 0 else "S2"
            p.st(s0_d[tq * 128:(tq + 1) * 128, oh * 512:(oh + 1) * 512],
                 s0t[:, tb * D + oh * 512:tb * D + (oh + 1) * 512], sem=ssem)
            s0st_marks[tb] = p.cnt[ssem]
    p.bar()
    p.wait(g, "S")
def build_stage_a():
    p = Prog("a")
    io = {}
    io["xn0fm"] = p.dram("xn0fm", [D, T], BF16, "ExternalInput")
    io["wqT"] = p.dram("wqT", [D, D], BF16, "ExternalInput")
    io["wkT"] = p.dram("wkT", [D, D], BF16, "ExternalInput")
    io["wvT"] = p.dram("wvT", [D, D], BF16, "ExternalInput")
    io["woT"] = p.dram("woT", [D, D], BF16, "ExternalInput")
    io["bq"] = p.dram("bq", [D], F32, "ExternalInput")
    io["bk"] = p.dram("bk", [D], F32, "ExternalInput")
    io["bv"] = p.dram("bv", [D], F32, "ExternalInput")
    io["bo"] = p.dram("bo", [D], F32, "ExternalInput")
    io["scale128"] = p.dram("scale128", [128], F32, "ExternalInput")
    io["s0"] = p.dram("s0", [T, D], F32, "ExternalOutput")
    p.emit_consts()
    p.push_scope()
    emit_stage_a(p, io)
    p.pop_scope()
    return p.done()



def emit_stage_f0(p, io):
    nc = p.nc
    v, s, t, g = nc.vector, nc.scalar, nc.tensor, nc.gpsimd
    xn0fm = io["xn0fm"]
    wrr = io["wrr"]
    w1 = io["w1"]
    b1r = io["b1r"]
    w2 = io["w2"]
    b2 = io["b2"]
    x0 = io["x0"]
    s0_d = io["s0"]
    x1_d = io["x1"]
    xn1_d = io["xn1"]
    xn1sfm = io["xn1sfm"]
    xhatfm = io["xhatfm"]
    xn1fm = io["xn1fm"]


    xn0 = p.sbuf("xn0", [128, ND * T], BF16)
    acc0 = p.sbuf("acc0", [128, NT * D], BF16)
    acc2 = p.sbuf("acc2", [128, NT * D], BF16)
    stg = p.sbuf("stg", [128, 4096], F32)
    w1bf = p.sbuf("w1bf", [128, 2 * 4096], BF16)
    w2bf = p.sbuf("w2bf", [128, 2 * 4096], BF16)
    h_sb = p.sbuf("h_sb", [128, 4 * T], BF16)
    wr_sb = p.sbuf("wr_sb", [128, ND * NE], BF16)
    b1_sb = p.sbuf("b1_sb", [128, NE * (FF // 128)], F32)
    b2row = p.sbuf("b2row", [1, D], BF16)
    b2rowf = p.sbuf("b2rowf", [1, D], F32)
    b2b = p.sbuf("b2b", [128, D], F32)
    ones1 = p.sbuf("ones1", [1, 128], BF16)
    logit_sb = p.sbuf("logit_sb", [128, NT * NE], F32)
    sc_sb = p.sbuf("sc_sb", [128, NT * NE], F32)
    lst = p.sbuf("lst", [128, 8], F32)
    trash8 = p.sbuf("trash8", [128, NE], F32)
    tmpc = p.sbuf("tmpc", [128, D], F32)

    ps_h = [p.psum(f"ps_h{i}", [128, 512], F32) for i in range(2)]
    ps_o = [p.psum(f"ps_o{i}", [128, 512], F32) for i in range(2)]
    ps_bc = p.psum("ps_bc", [128, D], F32)

    for dc in range(ND):
        p.ld(xn0[:, dc * T:(dc + 1) * T], xn0fm[dc * 128:(dc + 1) * 128, :])
    p.ld(stg[:, 0:ND * NE], wrr[:, :])
    p.ld(b1_sb[:, :], b1r[:, :])
    v.memset(ones1[0:1, :], 1.0)
    p.wait(v, "L")
    p.inc(v.tensor_copy(out=wr_sb[:, :], in_=stg[:, 0:ND * NE]), "V")
    p.bar()

    rev_marks = [0, 0]
    for t128 in range(NT):
        pb = t128 % 2
        p.wait(t, "V", rev_marks[pb])
        for dch in range(ND):
            inst = t.matmul(ps_h[pb][:, 0:NE],
                            xn0[:, dch * T + t128 * 128:
                                dch * T + (t128 + 1) * 128],
                            wr_sb[:, dch * NE:(dch + 1) * NE],
                            start=(dch == 0), stop=(dch == ND - 1))
        p.inc(inst, "P")
        p.wait(v, "P")
        p.inc(v.tensor_copy(out=logit_sb[:, t128 * NE:(t128 + 1) * NE],
                            in_=ps_h[pb][:, 0:NE]), "V")
        rev_marks[pb] = p.cnt["V"]
    for t128 in range(NT):
        lg = logit_sb[:, t128 * NE:(t128 + 1) * NE]
        p.wait(v, "V")
        p.inc(v.reduce_max(lst[:, 0:1], lg, axis=AX), "V")
        p.wait(v, "V")
        p.inc(v.tensor_scalar(out=lst[:, 1:2], in0=lst[:, 0:1], scalar1=-1.0,
                              scalar2=None, op0=ALU.mult), "V")
        p.wait(v, "V")
        p.inc(v.tensor_scalar(out=sc_sb[:, t128 * NE:(t128 + 1) * NE],
                              in0=lg, scalar1=lst[:, 0:1], scalar2=None,
                              op0=ALU.is_equal), "V")
        p.wait(s, "V")
        p.inc(s.activation(out=trash8[:, :], in_=lg, func=AF.Exp,
                           bias=lst[:, 1:2], accum_out=lst[:, 2:3]), "A")
        p.wait(v, "A")
        p.inc(v.reciprocal(lst[:, 3:4], lst[:, 2:3]), "V")
        p.wait(v, "V")
        p.inc(v.tensor_scalar(out=sc_sb[:, t128 * NE:(t128 + 1) * NE],
                              in0=sc_sb[:, t128 * NE:(t128 + 1) * NE],
                              scalar1=lst[:, 3:4], scalar2=None,
                              op0=ALU.mult), "V")
    p.bar()

    cast_marks = [0, 0]
    hps_marks = [0, 0]
    ops_marks = [0, 0]
    w2mm_mark = 0
    accv_mark = 0
    bcev_mark = 0
    bcps_mark = 0
    for e in range(NE):
        p.wait(nc.sync, "V", bcev_mark)
        self_ = p.inc(nc.sync.dma_start(out=b2rowf[0:1, :],
                                        in_=b2[e, :][None, :]), "L3", 16)
        p.wait(v, "L3")
        p.inc(v.tensor_copy(out=b2row[0:1, :], in_=b2rowf[0:1, :]), "V")
        p.wait(t, "V")
        p.wait(t, "V", bcev_mark)
        for j in range(2):
            inst = t.matmul(ps_bc[:, j * 512:(j + 1) * 512], ones1[0:1, :],
                            b2row[0:1, j * 512:(j + 1) * 512],
                            start=True, stop=True)
        p.inc(inst, "P")
        p.wait(v, "P")
        p.inc(v.tensor_copy(out=b2b[:, :], in_=ps_bc[:, :]), "V")
        bcev_mark = p.cnt["V"]
        for f8 in range(8):
            pb = (e * 8 + f8) % 2
            lsem = "L" if pb == 0 else "L2"
            p.wait(nc.sync, "P", cast_marks[pb])
            p.ld(w1bf[:, pb * 4096:(pb + 1) * 4096]
                 .rearrange("p (c f) -> p c f", c=ND),
                 w1[e].rearrange("(c p) f -> p c f", p=128)
                 [:, :, f8 * 512:(f8 + 1) * 512], sem=lsem)
            p.ld(w2bf[:, pb * 4096:(pb + 1) * 4096]
                 .rearrange("p (c d) -> p c d", c=4),
                 w2[e].rearrange("(c p) d -> p c d", p=128)
                 [:, f8 * 4:(f8 + 1) * 4, :], sem=lsem)
            w1v = w1bf[:, pb * 4096:(pb + 1) * 4096]
            w2v = w2bf[:, pb * 4096:(pb + 1) * 4096]
            p.wait(t, lsem)
            p.wait(t, "P", w2mm_mark)
            for ft in range(4):
                ftg = f8 * 4 + ft
                for tch in range(4):
                    hb = (ft * 4 + tch) % 2
                    p.wait(t, "A", hps_marks[hb])
                    for dch in range(ND):
                        inst = t.matmul(
                            ps_h[hb][:, :],
                            w1v[:, dch * 512 + ft * 128:
                                dch * 512 + (ft + 1) * 128],
                            xn0[:, dch * T + tch * 512:
                                dch * T + (tch + 1) * 512],
                            start=(dch == 0), stop=(dch == ND - 1))
                    p.inc(inst, "P")
                    p.wait(s, "P")
                    p.inc(s.activation(
                        out=h_sb[:, ft * T + tch * 512:ft * T + (tch + 1) * 512],
                        in_=ps_h[hb][:, :], func=AF.Gelu,
                        bias=b1_sb[:, e * 32 + ftg:e * 32 + ftg + 1]), "A")
                    hps_marks[hb] = p.cnt["A"]
            hre_mark = p.cnt["A"]
            p.wait(v, "V", accv_mark)
            accv_base = p.cnt["V"]
            for t128 in range(NT):
                for dh in range(2):
                    ob = (t128 * 2 + dh) % 2
                    p.wait(t, "A", hre_mark)
                    p.wait(t, "V", ops_marks[ob])
                    for ft in range(4):
                        inst = t.matmul(
                            ps_o[ob][:, :],
                            h_sb[:, ft * T + t128 * 128:
                                 ft * T + (t128 + 1) * 128],
                            w2v[:, ft * D + dh * 512:ft * D + (dh + 1) * 512],
                            start=(ft == 0), stop=(ft == 3))
                    p.inc(inst, "P")
                    p.wait(v, "P")
                    dst = acc2[:, t128 * D + dh * 512:t128 * D + (dh + 1) * 512]
                    if f8 == 0:
                        inst = v.tensor_copy(out=dst, in_=ps_o[ob][:, :])
                    else:
                        inst = v.tensor_tensor(out=dst, in0=ps_o[ob][:, :],
                                               in1=dst, op=ALU.add)
                    p.inc(inst, "V")
                    ops_marks[ob] = p.cnt["V"]
            accv_mark = p.cnt["V"]
            w2mm_mark = p.cnt["P"]
            cast_marks[pb] = p.cnt["P"]
        p.wait(v, "V", accv_mark)
        for t128 in range(NT):
            a2 = acc2[:, t128 * D:(t128 + 1) * D]
            a0 = acc0[:, t128 * D:(t128 + 1) * D]
            scc = sc_sb[:, t128 * NE + e:t128 * NE + e + 1]
            p.inc(v.tensor_tensor(out=tmpc[:, :], in0=a2, in1=b2b[:, :],
                                  op=ALU.add), "V")
            p.wait(v, "V")
            if e == 0:
                inst = v.tensor_scalar(out=a0, in0=tmpc[:, :], scalar1=scc,
                                       scalar2=None, op0=ALU.mult)
            else:
                inst = v.scalar_tensor_tensor(out=a0, in0=tmpc[:, :],
                                              scalar=scc, in1=a0,
                                              op0=ALU.mult, op1=ALU.add)
            p.inc(inst, "V")
            p.wait(v, "V")
        accv_mark = p.cnt["V"]
    p.bar()

    xt = p.sbuf("c_x", [128, D], F32)
    st_ = p.sbuf("c_s", [128, D], F32)
    gt_ = p.sbuf("c_g", [128, D], F32)
    x1t = p.sbuf("c_x1", [128, D], F32)
    xnt = p.sbuf("c_xn", [128, D], F32)
    xb1 = p.sbuf("c_xb", [128, 3 * D], BF16)
    stat = p.sbuf("c_stat", [128, 16], F32)
    trashd = p.sbuf("c_trash", [128, D], BF16)
    ps_t = [p.psum(f"c_pst{i}", [128, D], BF16) for i in range(2)]
    iter_mark = 0
    evac_marks = [0, 0]
    for tt in range(NT):
        ssem = "S" if tt % 2 == 0 else "S2"
        p.wait(nc.sync, "V", iter_mark)
        p.ld(xt[:, :], x0[tt * 128:(tt + 1) * 128, :])
        p.ld(st_[:, :], s0_d[tt * 128:(tt + 1) * 128, :])
        ldm = p.cnt["L"]
        p.wait(s, "A")
        p.inc(s.activation(out=gt_[:, :], in_=acc0[:, tt * D:(tt + 1) * D],
                           func=AF.Silu), "A")
        p.wait(v, "A")
        p.wait(v, "L", ldm)
        p.inc(v.tensor_tensor(out=gt_[:, :], in0=gt_[:, :],
                              in1=st_[:, :], op=ALU.mult), "V")
        p.wait(v, "V")
        p.wait(v, "S")
        p.wait(v, "S2")
        p.inc(v.tensor_tensor(out=x1t[:, :], in0=xt[:, :], in1=gt_[:, :],
                              op=ALU.add), "V")
        p.wait(v, "V")
        x1v = x1t[:, :]
        p.wait(g, "V")
        p.st(x1_d[tt * 128:(tt + 1) * 128, :], x1v, sem=ssem)
        p.inc(v.reduce_sum(stat[:, 0:1], x1v, axis=AX), "V")
        p.wait(s, "V")
        p.inc(s.activation(out=trashd[:, :], in_=x1v, func=AF.Square,
                           accum_out=stat[:, 1:2]), "A")
        p.wait(v, "A")
        p.wait(v, "V")
        p.inc(v.tensor_scalar(out=stat[:, 3:4], in0=stat[:, 0:1],
                              scalar1=-1.0 / D, scalar2=None, op0=ALU.mult),
              "V")
        p.wait(v, "V")
        v.tensor_tensor(out=stat[:, 6:7], in0=stat[:, 3:4],
                        in1=stat[:, 3:4], op=ALU.mult)
        p.inc(v.tensor_scalar(out=stat[:, 4:5], in0=stat[:, 1:2],
                              scalar1=1.0 / D, scalar2=None, op0=ALU.mult),
              "V")
        p.wait(v, "V")
        p.inc(v.tensor_tensor(out=stat[:, 5:6], in0=stat[:, 4:5],
                              in1=stat[:, 6:7], op=ALU.subtract), "V")
        p.wait(s, "V")
        p.inc(s.activation(out=stat[:, 2:3], in_=stat[:, 5:6], func=AF.Ln,
                           bias=p.C(4)), "A")
        p.wait(s, "A")
        p.inc(s.activation(out=stat[:, 2:3], in_=stat[:, 2:3], func=AF.Exp,
                           scale=p.C(2)), "A")
        p.wait(v, "A")
        p.inc(v.tensor_scalar(out=xnt[:, :], in0=x1v,
                              scalar1=stat[:, 3:4], scalar2=stat[:, 2:3],
                              op0=ALU.add, op1=ALU.mult), "V")
        xnv = xnt[:, :]
        p.wait(v, "V")
        p.wait(g, "V")
        p.st(xn1_d[tt * 128:(tt + 1) * 128, :], xnv, sem=ssem)
        p.inc(v.reduce_sum(stat[:, 8:9], xnv, axis=AX), "V")
        p.wait(s, "V")
        p.inc(s.activation(out=trashd[:, :], in_=xnv, func=AF.Square,
                           accum_out=stat[:, 9:10]), "A")
        p.wait(v, "A")
        p.wait(v, "V")
        p.inc(v.tensor_scalar(out=stat[:, 10:11], in0=stat[:, 9:10],
                              scalar1=1.0 / D, scalar2=None, op0=ALU.mult),
              "V")
        p.wait(s, "V")
        p.inc(s.activation(out=stat[:, 11:12], in_=stat[:, 10:11], func=AF.Ln,
                           bias=p.C(5)), "A")
        p.wait(s, "A")
        p.inc(s.activation(out=stat[:, 11:12], in_=stat[:, 11:12],
                           func=AF.Exp, scale=p.C(2)), "A")
        p.wait(v, "V")
        p.inc(v.tensor_scalar(out=stat[:, 12:13], in0=stat[:, 8:9],
                              scalar1=-1.0 / D, scalar2=None, op0=ALU.mult),
              "V")
        p.wait(v, "V")
        v.tensor_tensor(out=stat[:, 13:14], in0=stat[:, 12:13],
                        in1=stat[:, 12:13], op=ALU.mult)
        p.inc(v.tensor_scalar(out=stat[:, 14:15], in0=stat[:, 9:10],
                              scalar1=1.0 / D, scalar2=None, op0=ALU.mult),
              "V")
        p.wait(v, "V")
        p.inc(v.tensor_tensor(out=stat[:, 14:15], in0=stat[:, 14:15],
                              in1=stat[:, 13:14], op=ALU.subtract), "V")
        p.wait(s, "V")
        p.inc(s.activation(out=stat[:, 15:16], in_=stat[:, 14:15], func=AF.Ln,
                           bias=p.C(5)), "A")
        p.wait(s, "A")
        p.inc(s.activation(out=stat[:, 15:16], in_=stat[:, 15:16],
                           func=AF.Exp, scale=p.C(2)), "A")
        p.wait(v, "A")
        p.inc(v.tensor_scalar(out=xb1[:, 0:D], in0=xnv,
                              scalar1=stat[:, 11:12], scalar2=None,
                              op0=ALU.mult), "V")
        p.inc(v.tensor_scalar(out=xb1[:, D:2 * D], in0=xnv,
                              scalar1=stat[:, 12:13], scalar2=stat[:, 15:16],
                              op0=ALU.add, op1=ALU.mult), "V")
        p.inc(v.tensor_copy(out=xb1[:, 2 * D:3 * D], in_=xnv), "V")
        xb_mark = p.cnt["V"]
        for j, dstd in enumerate((xn1sfm, xhatfm, xn1fm)):
            jb = j % 2
            p.wait(t, "V", xb_mark)
            p.wait(t, "V", evac_marks[jb])
            for dc in range(ND):
                inst = t.transpose(ps_t[jb][:, dc * 128:(dc + 1) * 128],
                                   xb1[:, j * D + dc * 128:
                                       j * D + (dc + 1) * 128],
                                   p.ident[:, :])
            p.inc(inst, "P")
            p.wait(v, "P")
            p.inc(v.tensor_copy(out=xb1[:, j * D:(j + 1) * D],
                                in_=ps_t[jb][:, :]), "V")
            evac_marks[jb] = p.cnt["V"]
            p.wait(g, "V")
            for dc in range(ND):
                p.st(dstd[dc * 128:(dc + 1) * 128, tt * 128:(tt + 1) * 128],
                     xb1[:, j * D + dc * 128:j * D + (dc + 1) * 128],
                     sem=ssem)
        iter_mark = p.cnt["V"]
    p.bar()
    p.wait(g, "S")
    p.wait(g, "S2")
    p.wait(g, "L")
def build_stage_f0():
    p = Prog("f0")
    io = {}
    io["xn0fm"] = p.dram("xn0fm", [D, T], BF16, "ExternalInput")
    io["wrr"] = p.dram("wrr", [128, ND * NE], F32, "ExternalInput")
    io["w1"] = p.dram("w1", [NE, D, FF], F32, "ExternalInput")
    io["b1r"] = p.dram("b1r", [128, NE * (FF // 128)], F32, "ExternalInput")
    io["w2"] = p.dram("w2", [NE, FF, D], F32, "ExternalInput")
    io["b2"] = p.dram("b2", [NE, D], F32, "ExternalInput")
    io["x0"] = p.dram("x0", [T, D], F32, "ExternalInput")
    io["s0"] = p.dram("s0", [T, D], F32, "ExternalInput")
    io["x1"] = p.dram("x1", [T, D], F32, "ExternalOutput")
    io["xn1"] = p.dram("xn1", [T, D], F32, "ExternalOutput")
    io["xn1sfm"] = p.dram("xn1sfm", [D, T], BF16, "ExternalOutput")
    io["xhatfm"] = p.dram("xhatfm", [D, T], BF16, "ExternalOutput")
    io["xn1fm"] = p.dram("xn1fm", [D, T], BF16, "ExternalOutput")
    p.emit_consts()
    p.push_scope()
    emit_stage_f0(p, io)
    p.pop_scope()
    return p.done()



S_UPTO = 4
S_DT_MODE = 2


def emit_stage_s(p, io):
    nc = p.nc
    v, s, t, g = nc.vector, nc.scalar, nc.tensor, nc.gpsimd
    xn1sfm = io["xn1sfm"]
    iprojT = io["iprojT"]
    cwr = io["cwr"]
    cbr = io["cbr"]
    xpT = io["xpT"]
    dtwT = io["dtwT"]
    dtbr = io["dtbr"]
    xc_d = io["xc"]
    dtm_d = io["dtm"]
    bm_d = io["bm"]
    sres_d = io["sres"]


    XW = T + 3
    xn1 = p.sbuf("xn1", [128, ND * T], BF16)
    xi_sb = p.sbuf("xi_sb", [128, NDI * XW], BF16)
    xc_sb = p.sbuf("xc_sb", [128, NDI * T], BF16)
    wcol = p.sbuf("wcol", [128, 2 * ND * 128], BF16)
    dtcol = p.sbuf("dtcol", [128, 2 * NDI * 128], BF16)
    small = p.sbuf("small", [128, NDI * KC + 3 * NDI], F32)
    xp_sb = p.sbuf("xp_sb", [128, NDI * NS], BF16)
    srt = p.sbuf("srt", [128, 2 * 512], BF16)
    cacc = p.sbuf("cacc", [128, 2 * T], F32)
    ut = p.sbuf("ut", [128, 2 * 512], F32)
    at = p.sbuf("at", [128, 2 * 512], F32)
    dtt = p.sbuf("dtt", [128, 2 * 512], F32)
    bt_o = p.sbuf("bt_o", [64, 2 * 512], BF16)
    ps_m = [p.psum(f"s_ps{i}", [128, 512], F32) for i in range(2)]

    cw_v = small[:, 0:NDI * KC]
    cb_v = small[:, NDI * KC:NDI * KC + NDI]
    dtb_v = small[:, NDI * KC + NDI:NDI * KC + 2 * NDI]

    for dc in range(ND):
        p.ld(xn1[:, dc * T:(dc + 1) * T], xn1sfm[dc * 128:(dc + 1) * 128, :])
    p.ld(cw_v, cwr[:, :])
    p.ld(cb_v, cbr[:, :])
    p.ld(dtb_v, dtbr[:, :])
    p.ld(xp_sb[:, :].rearrange("p (c n) -> p c n", c=NDI),
         xpT.rearrange("(c p) n -> p c n", p=128))
    for cc in range(NDI):
        inst = g.memset(xi_sb[:, cc * XW:cc * XW + 3], 0.0)
    p.inc(inst, "G")
    p.bar()

    ev_marks = [0, 0]
    ev_sems = [None, None]
    wrd_marks = [0, 0]
    for och in range(2 * DI // 128):
        pb = och % 2
        lsem = "L" if pb == 0 else "L2"
        p.wait(nc.sync, "P", wrd_marks[pb])
        p.ld(wcol[:, pb * 1024:(pb + 1) * 1024]
             .rearrange("p (c o) -> p c o", c=ND),
             iprojT.rearrange("(c p) o -> p c o", p=128)
             [:, :, och * 128:(och + 1) * 128], sem=lsem)
        p.wait(t, lsem)
        for tch in range(4):
            hb = tch % 2
            if ev_sems[hb] is not None:
                p.wait(t, ev_sems[hb], ev_marks[hb])
            for dch in range(ND):
                inst = t.matmul(
                    ps_m[hb][:, :],
                    wcol[:, pb * 1024 + dch * 128:pb * 1024 + (dch + 1) * 128],
                    xn1[:, dch * T + tch * 512:dch * T + (tch + 1) * 512],
                    start=(dch == 0), stop=(dch == ND - 1))
            p.inc(inst, "P")
            if tch == 3:
                wrd_marks[pb] = p.cnt["P"]
            if och < NDI:
                p.wait(v, "P")
                p.inc(v.tensor_copy(
                    out=xi_sb[:, och * XW + 3 + tch * 512:
                              och * XW + 3 + (tch + 1) * 512],
                    in_=ps_m[hb][:, :]), "V")
                ev_marks[hb] = p.cnt["V"]
                ev_sems[hb] = "V"
            else:
                oi = och - NDI
                sb = (och * 4 + tch) % 2
                p.wait(s, "P")
                p.wait(s, "S" if sb == 0 else "S2")
                p.inc(s.activation(out=srt[:, sb * 512:(sb + 1) * 512],
                                   in_=ps_m[hb][:, :], func=AF.Silu), "A")
                ev_marks[hb] = p.cnt["A"]
                ev_sems[hb] = "A"
                p.wait(g, "A")
                p.st(sres_d[oi * 128:(oi + 1) * 128,
                            tch * 512:(tch + 1) * 512],
                     srt[:, sb * 512:(sb + 1) * 512],
                     sem="S" if sb == 0 else "S2")
    p.bar()
    if S_UPTO < 2:
        p.wait(g, "S")
        p.wait(g, "S2")
    

    for och in range(NDI):
        cb_ = och % 2
        cv = cacc[:, cb_ * T:(cb_ + 1) * T]
        p.wait(v, "A")
        p.inc(v.tensor_scalar(out=cv, in0=xi_sb[:, och * XW:och * XW + T],
                              scalar1=cw_v[:, och * KC:och * KC + 1],
                              scalar2=cb_v[:, och:och + 1],
                              op0=ALU.mult, op1=ALU.add), "V")
        for j in range(1, KC):
            p.wait(v, "V")
            p.inc(v.scalar_tensor_tensor(
                out=cv, in0=xi_sb[:, och * XW + j:och * XW + j + T],
                scalar=cw_v[:, och * KC + j:och * KC + j + 1], in1=cv,
                op0=ALU.mult, op1=ALU.add), "V")
        p.wait(s, "V")
        p.inc(s.activation(out=xc_sb[:, och * T:(och + 1) * T], in_=cv,
                           func=AF.Silu), "A")
        p.wait(g, "A")
        p.st(xc_d[och * 128:(och + 1) * 128, :],
             xc_sb[:, och * T:(och + 1) * T],
             sem="S" if cb_ == 0 else "S2")
    p.bar()
    if S_UPTO < 3:
        p.wait(g, "S")
        p.wait(g, "S2")
    

    bev_marks = [0, 0]
    for tch in range(4):
        hb = tch % 2
        p.wait(t, "V", bev_marks[hb])
        for cch in range(NDI):
            inst = t.matmul(ps_m[hb][0:64, :],
                            xp_sb[:, cch * NS:(cch + 1) * NS],
                            xc_sb[:, cch * T + tch * 512:
                                  cch * T + (tch + 1) * 512],
                            start=(cch == 0), stop=(cch == NDI - 1))
        p.inc(inst, "P")
        p.wait(v, "P")
        p.wait(v, "S" if hb == 0 else "S2")
        p.inc(v.tensor_copy(out=bt_o[0:64, hb * 512:(hb + 1) * 512],
                            in_=ps_m[hb][0:64, :]), "V")
        bev_marks[hb] = p.cnt["V"]
        p.wait(g, "V")
        p.st(bm_d[:, tch * 512:(tch + 1) * 512],
             bt_o[0:64, hb * 512:(hb + 1) * 512],
             sem="S" if hb == 0 else "S2")
    p.bar()
    if S_UPTO < 4:
        p.wait(g, "S")
        p.wait(g, "S2")
    

    wrd_marks = [0, 0]
    spv_marks = [0, 0]
    spa_marks = [0, 0]
    for och in range(NDI):
        pb = och % 2
        lsem = "L" if pb == 0 else "L2"
        p.wait(nc.sync, "P", wrd_marks[pb])
        p.ld(dtcol[:, pb * T:(pb + 1) * T]
             .rearrange("p (c o) -> p c o", c=NDI),
             dtwT.rearrange("(c p) o -> p c o", p=128)
             [:, :, och * 128:(och + 1) * 128], sem=lsem)
        p.wait(t, lsem)
        for tch in range(4):
            hb = tch % 2
            p.wait(t, "V", spv_marks[hb])
            p.wait(t, "A", spa_marks[hb])
            for cch in range(NDI):
                inst = t.matmul(
                    ps_m[hb][:, :],
                    dtcol[:, pb * T + cch * 128:pb * T + (cch + 1) * 128],
                    xc_sb[:, cch * T + tch * 512:cch * T + (tch + 1) * 512],
                    start=(cch == 0), stop=(cch == NDI - 1))
            p.inc(inst, "P")
            if tch == 3:
                wrd_marks[pb] = p.cnt["P"]
            if S_DT_MODE == 0:
                p.wait(v, "P")
                p.wait(v, "V", spv_marks[hb])
                p.wait(v, "S" if hb == 0 else "S2")
                p.inc(v.tensor_copy(out=dtt[:, hb * 512:(hb + 1) * 512],
                                    in_=ps_m[hb][:, :]), "V")
                spv_marks[hb] = p.cnt["V"]
                spa_marks[hb] = 0
                p.wait(g, "V")
                p.st(dtm_d[och * 128:(och + 1) * 128,
                           tch * 512:(tch + 1) * 512],
                     dtt[:, hb * 512:(hb + 1) * 512],
                     sem="S" if hb == 0 else "S2")
                continue
            p.wait(v, "P")
            p.inc(v.tensor_scalar(out=ut[:, hb * 512:(hb + 1) * 512],
                                  in0=ps_m[hb][:, :],
                                  scalar1=dtb_v[:, och:och + 1], scalar2=0.0,
                                  op0=ALU.add, op1=ALU.max), "V")
            p.wait(s, "P")
            p.wait(s, "V")
            p.inc(s.activation(out=at[:, hb * 512:(hb + 1) * 512],
                               in_=ps_m[hb][:, :], func=AF.Abs,
                               bias=dtb_v[:, och:och + 1]), "A")
            spa_marks[hb] = p.cnt["A"]
            p.wait(s, "A")
            p.inc(s.activation(out=at[:, hb * 512:(hb + 1) * 512],
                               in_=at[:, hb * 512:(hb + 1) * 512],
                               func=AF.Exp, scale=p.C(0)), "A")
            p.wait(s, "A")
            p.inc(s.activation(out=at[:, hb * 512:(hb + 1) * 512],
                               in_=at[:, hb * 512:(hb + 1) * 512],
                               func=AF.Ln, bias=p.C(1)), "A")
            p.wait(v, "A")
            p.wait(v, "V")
            p.wait(v, "S" if hb == 0 else "S2")
            p.inc(v.tensor_tensor(out=dtt[:, hb * 512:(hb + 1) * 512],
                                  in0=ut[:, hb * 512:(hb + 1) * 512],
                                  in1=at[:, hb * 512:(hb + 1) * 512],
                                  op=ALU.add), "V")
            spv_marks[hb] = p.cnt["V"]
            p.wait(g, "V")
            p.st(dtm_d[och * 128:(och + 1) * 128, tch * 512:(tch + 1) * 512],
                 dtt[:, hb * 512:(hb + 1) * 512],
                 sem="S" if hb == 0 else "S2")
    p.bar()
    p.wait(g, "S")
    p.wait(g, "S2")
def build_stage_s():
    p = Prog("s")
    io = {}
    io["xn1sfm"] = p.dram("xn1sfm", [D, T], BF16, "ExternalInput")
    io["iprojT"] = p.dram("iprojT", [D, 2 * DI], BF16, "ExternalInput")
    io["cwr"] = p.dram("cwr", [128, NDI * KC], F32, "ExternalInput")
    io["cbr"] = p.dram("cbr", [128, NDI], F32, "ExternalInput")
    io["xpT"] = p.dram("xpT", [DI, NS], BF16, "ExternalInput")
    io["dtwT"] = p.dram("dtwT", [DI, DI], BF16, "ExternalInput")
    io["dtbr"] = p.dram("dtbr", [128, NDI], F32, "ExternalInput")
    io["xc"] = p.dram("xc", [DI, T], BF16, "ExternalOutput")
    io["dtm"] = p.dram("dtm", [DI, T], F32, "ExternalOutput")
    io["bm"] = p.dram("bm", [NS, T], BF16, "ExternalOutput")
    io["sres"] = p.dram("sres", [DI, T], BF16, "ExternalOutput")
    p.emit_consts()
    p.push_scope()
    emit_stage_s(p, io)
    p.pop_scope()
    return p.done()



def gen_c_scan(p, io):
    nc = p.nc
    p.prefix = "c1_"
    v, s, t, g = nc.vector, nc.scalar, nc.tensor, nc.gpsimd
    xc_d, dtm_d, bm_d = io["xc"], io["dtm"], io["bm"]
    asb_d, dpr, sres_d = io["asb"], io["dpr"], io["sres"]
    y2_d = io["y2"]

    TQ = 512
    NB = 8
    NBK = NS // NB
    asb = p.sbuf("asb_s", [128, NDI * NS], F32)
    dp_sb = p.sbuf("dp_sb", [128, NDI], F32)
    ones1 = p.sbuf("ones1", [1, 128], BF16)
    bt_sb = p.sbuf("bt_sb", [64, TQ], BF16)
    brow = p.sbuf("brow", [1, 2 * TQ], BF16)
    bbc = p.sbuf("bbc", [128, NB * TQ], BF16)
    dt_t = p.sbuf("dt_t", [128, 2 * TQ], F32)
    xc_t = p.sbuf("xc_t", [128, 2 * TQ], BF16)
    sr_t = p.sbuf("sr_t", [128, 2 * TQ], BF16)
    ad_t = p.sbuf("ad_t", [128, 2 * TQ], F32)
    bx_t = p.sbuf("bx_t", [128, 2 * TQ], BF16)
    h_t = p.sbuf("h_t", [128, 2 * TQ], BF16)
    carry = p.sbuf("carry", [128, NDI * NS], BF16)
    TQP = 512
    y_acc = p.sbuf("y_acc", [128, NDI * TQ], BF16)
    y2_t = p.sbuf("y2_t", [128, TQ], BF16)
    ps_y = p.psum("ps_y", [128, TQ], F32)
    ps_c = p.psum("ps_c", [128, NS], F32)
    ps_b = p.psum("ps_b", [128, TQ], F32)

    p.ld(asb[:, :], asb_d[:, :], sem="L5")
    p.ld(dp_sb[:, :], dpr[:, :], sem="L5")
    p.wait(v, "L5")
    p.inc(v.memset(ones1[0:1, :], 1.0), "V")
    prol_mark = p.cnt["V"]
    admk = [0, 0]
    hmk = [0, 0]
    bxmk = [0, 0]
    ldg_v = [0, 0]
    ldg_g = [0, 0]
    ldg_a = [0, 0]
    yield
    for tq in range(T // TQ):
        p.wait(nc.sync, "V", prol_mark)
        p.wait(nc.sync, "L5")
        p.ld(bt_sb[0:64, :], bm_d[:, tq * TQ:(tq + 1) * TQ], sem="L5")
        for nb_ in range(NBK):
            bev = 0
            first_ev = True
            for ng in range(NB // 2):
                p.wait(nc.sync, "L5")
                p.wait(nc.sync, "P")
                p.inc(nc.sync.dma_start(
                    out=brow[0:1, :],
                    in_=bt_sb[nb_ * NB + ng * 2:nb_ * NB + (ng + 1) * 2, :]),
                    "L5", 16)
                p.wait(t, "L5")
                for nl in range(2):
                    n_ = ng * 2 + nl
                    p.wait(t, "V", bev)
                    inst = t.matmul(ps_b[:, :], ones1[0:1, :],
                                    brow[0:1, nl * TQ:(nl + 1) * TQ],
                                    start=True, stop=True)
                    p.inc(inst, "P")
                    p.wait(v, "P")
                    if first_ev:
                        p.wait(v, "G")
                        first_ev = False
                    p.inc(v.tensor_copy(out=bbc[:, n_ * TQ:(n_ + 1) * TQ],
                                        in_=ps_b[:, :]), "V")
                    bev = p.cnt["V"]
            bev_all = p.cnt["V"]
            yield
            for dch in range(NDI):
                db_ = dch % 2
                lsem = "L3" if db_ == 0 else "L4"
                p.wait(nc.sync, "V", ldg_v[db_])
                p.wait(nc.sync, "G", ldg_g[db_])
                p.wait(nc.sync, "A", ldg_a[db_])
                p.ld(dt_t[:, db_ * TQ:(db_ + 1) * TQ],
                     dtm_d[dch * 128:(dch + 1) * 128,
                           tq * TQ:(tq + 1) * TQ], sem=lsem)
                p.ld(xc_t[:, db_ * TQ:(db_ + 1) * TQ],
                     xc_d[dch * 128:(dch + 1) * 128,
                          tq * TQ:(tq + 1) * TQ], sem=lsem)
                if nb_ == NBK - 1:
                    p.ld(sr_t[:, db_ * TQ:(db_ + 1) * TQ],
                         sres_d[dch * 128:(dch + 1) * 128,
                                tq * TQ:(tq + 1) * TQ], sem=lsem)
                p.wait(s, lsem)
                p.wait(g, lsem)
                p.wait(v, lsem)
                dtv = dt_t[:, db_ * TQ:(db_ + 1) * TQ]
                xcv = xc_t[:, db_ * TQ:(db_ + 1) * TQ]
                srv = sr_t[:, db_ * TQ:(db_ + 1) * TQ]
                for n_ in range(NB):
                    n = nb_ * NB + n_
                    nbuf = n_ % 2
                    adv = ad_t[:, nbuf * TQ:(nbuf + 1) * TQ]
                    bxv = bx_t[:, nbuf * TQ:(nbuf + 1) * TQ]
                    hv = h_t[:, nbuf * TQ:(nbuf + 1) * TQ]
                    col = dch * NS + n
                    p.wait(s, "V", admk[nbuf])
                    p.inc(s.activation(out=adv, in_=dtv, func=AF.Exp,
                                       scale=asb[:, col:col + 1]), "A")
                    use_pool = True
                    eng = g
                    esem = "G"
                    p.wait(eng, "V", bev_all)
                    p.wait(eng, "V", bxmk[nbuf])
                    p.inc(eng.tensor_tensor(
                        out=bxv, in0=xcv,
                        in1=bbc[:, n_ * TQ:(n_ + 1) * TQ], op=ALU.mult), esem)
                    bx_mark = p.cnt[esem]
                    p.wait(v, "A")
                    p.wait(v, esem, bx_mark)
                    p.wait(v, "P", hmk[nbuf])
                    init = 0.0 if tq == 0 else carry[:, col:col + 1]
                    p.inc(v.tensor_tensor_scan(out=hv, data0=adv, data1=bxv,
                                               initial=init, op0=ALU.mult,
                                               op1=ALU.add), "V")
                    admk[nbuf] = p.cnt["V"]
                    bxmk[nbuf] = p.cnt["V"]
                    p.wait(t, "V")
                    inst = t.matmul(ps_y[:, :], p.ident[:, :], hv,
                                    start=(n_ == 0), stop=(n_ == NB - 1))
                    if tq < (T // TQ) - 1:
                        inst = t.matmul(ps_c[:, n:n + 1], p.ident[:, :],
                                        hv[:, TQ - 1:TQ], start=(n_ == 0),
                                        stop=(n_ == NB - 1))
                    p.inc(inst, "P")
                    hmk[nbuf] = p.cnt["P"]
                p.wait(v, "P")
                yv = y_acc[:, dch * TQP:(dch + 1) * TQP]
                if nb_ == 0:
                    p.inc(v.tensor_copy(out=yv, in_=ps_y[:, :]), "V")
                else:
                    p.inc(v.tensor_tensor(out=yv, in0=ps_y[:, :],
                                          in1=yv, op=ALU.add), "V")
                if tq < (T // TQ) - 1:
                    p.wait(v, "V")
                    p.inc(v.tensor_copy(
                        out=carry[:, dch * NS + nb_ * NB:
                                  dch * NS + (nb_ + 1) * NB],
                        in_=ps_c[:, nb_ * NB:(nb_ + 1) * NB]), "V")
                if nb_ == NBK - 1:
                    p.wait(v, "V")
                    p.inc(v.scalar_tensor_tensor(
                        out=yv, in0=xcv,
                        scalar=dp_sb[:, dch:dch + 1], in1=yv,
                        op0=ALU.mult, op1=ALU.add), "V")
                    p.wait(v, "V")
                    ssem = "S" if db_ == 0 else "S2"
                    p.wait(v, "S")
                    p.wait(v, "S2")
                    p.inc(v.tensor_tensor(out=y2_t[:, :], in0=yv,
                                          in1=srv, op=ALU.mult), "V")
                    p.wait(g, "V")
                    p.st(y2_d[dch * 128:(dch + 1) * 128,
                              tq * TQ:(tq + 1) * TQ], y2_t[:, :], sem=ssem)
                ldg_v[db_] = p.cnt["V"]
                ldg_g[db_] = p.cnt["G"]
                ldg_a[db_] = p.cnt["A"]
                yield
    p.bar()


def emit_c_outproj(p, io):
    nc = p.nc
    p.prefix = "c2_"
    v, s, t, g = nc.vector, nc.scalar, nc.tensor, nc.gpsimd
    opT, xn1_d, s1_d, y2_d = io["opT"], io["xn1"], io["s1"], io["y2"]
    opt_sb = p.sbuf("opt_sb", [128, NDI * D], BF16)
    y2l = p.sbuf("y2l", [128, 2 * NDI * 128], BF16)
    xnt = p.sbuf("xnt", [128, 2 * D], F32)
    s1t = p.sbuf("s1t", [128, 2 * D], F32)
    ps_b2 = [p.psum(f"ps_op{i}", [128, 512], F32) for i in range(2)]
    p.ld(opt_sb[:, :].rearrange("p (c o) -> p c o", c=NDI),
         opT.rearrange("(c p) o -> p c o", p=128))
    p.bar()
    opev = [0, 0]
    ldg = [0, 0]
    for t128 in range(NT):
        pb = t128 % 2
        lsem = "L" if pb == 0 else "L2"
        ssem = "S" if pb == 0 else "S2"
        p.wait(nc.sync, "V", ldg[pb])
        for cch in range(NDI):
            p.ld(y2l[:, pb * DI + cch * 128:pb * DI + (cch + 1) * 128],
                 y2_d[cch * 128:(cch + 1) * 128,
                      t128 * 128:(t128 + 1) * 128], sem=lsem)
        p.ld(xnt[:, pb * D:(pb + 1) * D],
             xn1_d[t128 * 128:(t128 + 1) * 128, :], sem=lsem)
        p.wait(t, lsem)
        for oh in range(2):
            hb = (t128 * 2 + oh) % 2
            p.wait(t, "V", opev[hb])
            for cch in range(NDI):
                inst = t.matmul(
                    ps_b2[hb][:, :],
                    y2l[:, pb * DI + cch * 128:pb * DI + (cch + 1) * 128],
                    opt_sb[:, cch * D + oh * 512:cch * D + (oh + 1) * 512],
                    start=(cch == 0), stop=(cch == NDI - 1))
            p.inc(inst, "P")
            p.wait(v, "P")
            p.wait(v, ssem)
            p.inc(v.tensor_tensor(
                out=s1t[:, pb * D + oh * 512:pb * D + (oh + 1) * 512],
                in0=ps_b2[hb][:, :],
                in1=xnt[:, pb * D + oh * 512:pb * D + (oh + 1) * 512],
                op=ALU.add), "V")
            opev[hb] = p.cnt["V"]
            p.wait(g, "V")
            p.st(s1_d[t128 * 128:(t128 + 1) * 128,
                      oh * 512:(oh + 1) * 512],
                 s1t[:, pb * D + oh * 512:pb * D + (oh + 1) * 512],
                 sem=ssem)
        ldg[pb] = p.cnt["V"]
    p.bar()


def emit_f1_prologue(p, io, bufs):
    nc = p.nc
    p.prefix = "f1_"
    v, s, t, g = nc.vector, nc.scalar, nc.tensor, nc.gpsimd
    xn1fm, wgr = io["xn1fm"], io["wgr"]
    gr, br, b1r = io["gr"], io["br"], io["b1r"]
    normed = bufs["normed"] = p.sbuf("normed", [128, ND * T], BF16)
    acc1 = bufs["acc1"]
    bufs["acc2"] = p.sbuf("acc2", [128, NT * D], BF16)
    bufs["stg"] = p.sbuf("stg", [128, 1024], F32)
    bufs["w1bf"] = p.sbuf("w1bf", [128, 2 * 4096], BF16)
    bufs["w2bf"] = p.sbuf("w2bf", [128, 4096], BF16)
    bufs["h_sb"] = p.sbuf("h_sb", [128, 4 * T], BF16)
    wg_sb = bufs["wg_sb"] = p.sbuf("wg_sb", [128, ND * NE], BF16)
    gb_sb = bufs["gb_sb"] = p.sbuf("gb_sb", [128, 2 * NE * ND], F32)
    bufs["b1_sb"] = p.sbuf("b1_sb", [128, NE * (FF // 128)], F32)
    bufs["b2row"] = p.sbuf("b2row", [1, D], BF16)
    bufs["b2rowf"] = p.sbuf("b2rowf", [1, D], F32)
    bufs["b2b"] = p.sbuf("b2b", [128, D], BF16)
    bufs["ones1"] = p.sbuf("ones1f", [1, 128], BF16)
    logit_sb = bufs["logit_sb"] = p.sbuf("logit_sb", [128, NT * NE], F32)
    cw_sb = bufs["cw_sb"]
    sw_sb = bufs["sw_sb"]
    lst = bufs["lst"] = p.sbuf("lst", [128, 8], F32)
    sp8 = bufs["sp8"] = p.sbuf("sp8", [128, NE], F32)
    ps_h = bufs["ps_h"] = [p.psum(f"ps_h{i}", [128, 512], F32)
                           for i in range(2)]
    bufs["ps_o"] = [p.psum(f"ps_o{i}", [128, 512], F32) for i in range(2)]
    bufs["ps_bc"] = p.psum("ps_bc", [128, 512], F32)

    for dc in range(ND):
        p.ld(normed[:, dc * T:(dc + 1) * T],
             xn1fm[dc * 128:(dc + 1) * 128, :])
    p.ld(bufs["stg"][:, 0:ND * NE], wgr[:, :])
    p.ld(bufs["b1_sb"][:, :], b1r[:, :])
    p.ld(gb_sb[:, 0:NE * ND], gr[:, :])
    p.ld(gb_sb[:, NE * ND:2 * NE * ND], br[:, :])
    v.memset(bufs["ones1"][0:1, :], 1.0)
    p.wait(v, "L")
    p.inc(v.tensor_copy(out=wg_sb[:, :], in_=bufs["stg"][:, 0:ND * NE]),
          "V")
    p.bar()
    rev_marks = [0, 0]
    for t128 in range(NT):
        pb = t128 % 2
        p.wait(t, "V", rev_marks[pb])
        for dch in range(ND):
            inst = t.matmul(ps_h[pb][:, 0:NE],
                            normed[:, dch * T + t128 * 128:
                                   dch * T + (t128 + 1) * 128],
                            wg_sb[:, dch * NE:(dch + 1) * NE],
                            start=(dch == 0), stop=(dch == ND - 1))
        p.inc(inst, "P")
        p.wait(v, "P")
        p.inc(v.tensor_copy(out=logit_sb[:, t128 * NE:(t128 + 1) * NE],
                            in_=ps_h[pb][:, 0:NE]), "V")
        rev_marks[pb] = p.cnt["V"]
    for t128 in range(NT):
        lg = logit_sb[:, t128 * NE:(t128 + 1) * NE]
        cwv = cw_sb[:, t128 * NE:(t128 + 1) * NE]
        p.wait(v, "V")
        p.inc(v.reduce_max(lst[:, 0:1], lg, axis=AX), "V")
        p.wait(v, "V")
        p.inc(v.tensor_scalar(out=cwv, in0=lg, scalar1=lst[:, 0:1],
                              scalar2=None, op0=ALU.is_equal), "V")
        p.wait(v, "V")
        p.inc(v.scalar_tensor_tensor(out=sp8[:, :], in0=cwv, scalar=-1e30,
                                     in1=lg, op0=ALU.mult, op1=ALU.add), "V")
        p.wait(v, "V")
        p.inc(v.reduce_max(lst[:, 1:2], sp8[:, :], axis=AX), "V")
        p.wait(v, "V")
        p.inc(v.tensor_tensor(out=lst[:, 2:3], in0=lst[:, 0:1],
                              in1=lst[:, 1:2], op=ALU.subtract), "V")
        p.wait(s, "V")
        s.activation(out=lst[:, 3:4], in_=lst[:, 2:3], func=AF.Sigmoid)
        p.inc(s.activation(out=lst[:, 4:5], in_=lst[:, 2:3], func=AF.Sigmoid,
                           scale=p.C(0)), "A")
        p.wait(v, "A")
        p.inc(v.tensor_scalar(out=sp8[:, :], in0=sp8[:, :],
                              scalar1=lst[:, 1:2], scalar2=None,
                              op0=ALU.is_equal), "V")
        p.wait(v, "V")
        p.inc(v.tensor_scalar(out=cwv, in0=cwv, scalar1=lst[:, 3:4],
                              scalar2=None, op0=ALU.mult), "V")
        p.wait(v, "V")
        p.inc(v.scalar_tensor_tensor(out=cwv, in0=sp8[:, :],
                                     scalar=lst[:, 4:5], in1=cwv,
                                     op0=ALU.mult, op1=ALU.add), "V")
        p.inc(v.tensor_tensor(out=sw_sb[:, t128:t128 + 1], in0=lst[:, 3:4],
                              in1=lst[:, 4:5], op=ALU.add), "V")
        p.wait(v, "V")
    p.bar()
    return bufs


def gen_f1_experts(p, io, bufs):
    nc = p.nc
    v, s, t, g = nc.vector, nc.scalar, nc.tensor, nc.gpsimd
    xhatfm, w1, w2, b2 = io["xhatfm"], io["w1"], io["w2"], io["b2"]
    normed, acc1, acc2 = bufs["normed"], bufs["acc1"], bufs["acc2"]
    stg, w1bf, w2bf = bufs["stg"], bufs["w1bf"], bufs["w2bf"]
    h_sb = bufs["h_sb"]
    gb_sb, b1_sb = bufs["gb_sb"], bufs["b1_sb"]
    b2row, b2rowf, b2b = bufs["b2row"], bufs["b2rowf"], bufs["b2b"]
    ones1, cw_sb, tmpc = bufs["ones1"], bufs["cw_sb"], bufs["tmpc"]
    ps_h, ps_o, ps_bc = bufs["ps_h"], bufs["ps_o"], bufs["ps_bc"]
    cast_marks = [0, 0]
    hps_marks = [0, 0]
    ops_marks = [0, 0]
    w2mm_mark = 0
    accv_mark = 0
    bcev_mark = 0
    nrm_mark = 0
    yield
    for e in range(NE):
        p.wait(t, "V")
        p.wait(nc.sync, "P", w2mm_mark)
        for dc in range(ND):
            p.wait(nc.sync, "V", nrm_mark)
            p.ld(normed[:, dc * T:(dc + 1) * T],
                 xhatfm[dc * 128:(dc + 1) * 128, :], sem="L")
            p.wait(v, "L")
            p.wait(v, "P", w2mm_mark)
            p.inc(v.tensor_scalar(
                out=normed[:, dc * T:(dc + 1) * T],
                in0=normed[:, dc * T:(dc + 1) * T],
                scalar1=gb_sb[:, e * ND + dc:e * ND + dc + 1],
                scalar2=gb_sb[:, NE * ND + e * ND + dc:
                              NE * ND + e * ND + dc + 1],
                op0=ALU.mult, op1=ALU.add), "V")
            nrm_mark = p.cnt["V"]
        p.wait(nc.sync, "V", bcev_mark)
        p.wait(nc.sync, "V", nrm_mark)
        p.inc(nc.sync.dma_start(out=b2rowf[0:1, :], in_=b2[e, :][None, :]),
              "G2", 16)
        p.wait(v, "G2")
        p.inc(v.tensor_copy(out=b2row[0:1, :], in_=b2rowf[0:1, :]), "V")
        for j in range(2):
            p.wait(t, "V")
            p.wait(t, "V", bcev_mark)
            inst = t.matmul(ps_bc[:, :], ones1[0:1, :],
                            b2row[0:1, j * 512:(j + 1) * 512],
                            start=True, stop=True)
            p.inc(inst, "P")
            p.wait(v, "P")
            p.inc(v.tensor_copy(out=b2b[:, j * 512:(j + 1) * 512],
                                in_=ps_bc[:, :]), "V")
            bcev_mark = p.cnt["V"]
        yield
        for f8 in range(8):
            pb = (e * 8 + f8) % 2
            lsem = "L" if pb == 0 else "L2"
            wv1 = w1bf[:, pb * 4096:(pb + 1) * 4096]
            wv2 = w2bf[:, :]
            p.wait(nc.sync, "P", cast_marks[pb])
            if f8 == 0:
                p.wait(nc.sync, "V", nrm_mark)
            else:
                p.wait(nc.sync, "P", w2mm_mark)
            p.ld(wv1[:, :].rearrange("p (c f) -> p c f", c=ND),
                 w1[e].rearrange("(c p) f -> p c f", p=128)
                 [:, :, f8 * 512:(f8 + 1) * 512], sem=lsem)
            p.ld(wv2[:, :].rearrange("p (c d) -> p c d", c=4),
                 w2[e].rearrange("(c p) d -> p c d", p=128)
                 [:, f8 * 4:(f8 + 1) * 4, :], sem=lsem)
            p.wait(t, lsem)
            p.wait(t, "P", w2mm_mark)
            p.wait(t, "V", nrm_mark)
            for ft in range(4):
                ftg = f8 * 4 + ft
                for tch in range(4):
                    hb = (ft * 4 + tch) % 2
                    p.wait(t, "A", hps_marks[hb])
                    for dch in range(ND):
                        inst = t.matmul(
                            ps_h[hb][:, :],
                            wv1[:, dch * 512 + ft * 128:
                                dch * 512 + (ft + 1) * 128],
                            normed[:, dch * T + tch * 512:
                                   dch * T + (tch + 1) * 512],
                            start=(dch == 0), stop=(dch == ND - 1))
                    p.inc(inst, "P")
                    p.wait(s, "P")
                    p.inc(s.activation(
                        out=h_sb[:, ft * T + tch * 512:
                                 ft * T + (tch + 1) * 512],
                        in_=ps_h[hb][:, :], func=AF.Relu,
                        bias=b1_sb[:, e * 32 + ftg:e * 32 + ftg + 1]), "A")
                    hps_marks[hb] = p.cnt["A"]
                    if tch == 3:
                        yield
            hre_mark = p.cnt["A"]
            for t128 in range(NT):
                if t128 % 4 == 0:
                    yield
                for dh in range(2):
                    ob = (t128 * 2 + dh) % 2
                    p.wait(t, "A", hre_mark)
                    p.wait(t, "A", ops_marks[ob])
                    dst = acc2[:, t128 * D + dh * 512:
                               t128 * D + (dh + 1) * 512]
                    first = True
                    if f8 > 0:
                        inst = t.matmul(ps_o[ob][:, :], p.ident[:, :], dst,
                                        start=True, stop=False)
                        first = False
                    for ft in range(4):
                        inst = t.matmul(
                            ps_o[ob][:, :],
                            h_sb[:, ft * T + t128 * 128:
                                 ft * T + (t128 + 1) * 128],
                            wv2[:, ft * D + dh * 512:ft * D + (dh + 1) * 512],
                            start=(first and ft == 0), stop=(ft == 3))
                    p.inc(inst, "P")
                    p.wait(s, "P")
                    p.inc(s.activation(out=dst, in_=ps_o[ob][:, :],
                                       func=AF.Copy), "A")
                    ops_marks[ob] = p.cnt["A"]
            accv_mark = p.cnt["V"]
            w2mm_mark = p.cnt["P"]
            cast_marks[pb] = p.cnt["P"]
            yield
        p.wait(v, "V", accv_mark)
        p.wait(v, "A")
        for t128 in range(NT):
            a2 = acc2[:, t128 * D:(t128 + 1) * D]
            a0 = acc1[:, t128 * D:(t128 + 1) * D]
            scc = cw_sb[:, t128 * NE + e:t128 * NE + e + 1]
            p.inc(v.tensor_tensor(out=tmpc[:, :], in0=a2, in1=b2b[:, :],
                                  op=ALU.add), "V")
            p.wait(v, "V")
            if e == 0:
                inst = v.tensor_scalar(out=a0, in0=tmpc[:, :], scalar1=scc,
                                       scalar2=None, op0=ALU.mult)
            else:
                inst = v.scalar_tensor_tensor(out=a0, in0=tmpc[:, :],
                                              scalar=scc, in1=a0,
                                              op0=ALU.mult, op1=ALU.add)
            p.inc(inst, "V")
            p.wait(v, "V")
        yield


def emit_f1_final(p, io, bufs):
    nc = p.nc
    p.prefix = "f2_"
    v, s, t, g = nc.vector, nc.scalar, nc.tensor, nc.gpsimd
    xn1_d, x1_d, s1_d, out_d = io["xn1"], io["x1"], io["s1"], io["out"]
    acc1, sw_sb, tmpc = bufs["acc1"], bufs["sw_sb"], bufs["tmpc"]
    xnt = p.sbuf("f_xn", [128, D], F32)
    x1t = p.sbuf("f_x1", [128, D], F32)
    s1t = p.sbuf("f_s1", [128, D], F32)
    gt_ = p.sbuf("f_g", [128, D], F32)
    ot = p.sbuf("f_o", [128, D], F32)
    yt = tmpc
    iter_mark = 0
    for tt in range(NT):
        ssem = "S" if tt % 2 == 0 else "S2"
        p.wait(nc.sync, "V", iter_mark)
        p.ld(xnt[:, :], xn1_d[tt * 128:(tt + 1) * 128, :])
        p.ld(x1t[:, :], x1_d[tt * 128:(tt + 1) * 128, :])
        p.ld(s1t[:, :], s1_d[tt * 128:(tt + 1) * 128, :])
        ldm = p.cnt["L"]
        p.wait(v, "L", ldm)
        p.wait(v, "A")
        p.inc(v.scalar_tensor_tensor(out=yt[:, :], in0=xnt[:, :],
                                     scalar=sw_sb[:, tt:tt + 1],
                                     in1=acc1[:, tt * D:(tt + 1) * D],
                                     op0=ALU.mult, op1=ALU.add), "V")
        p.wait(s, "V")
        p.inc(s.activation(out=gt_[:, :], in_=yt[:, :], func=AF.Silu), "A")
        p.wait(v, "A")
        p.inc(v.tensor_tensor(out=gt_[:, :], in0=gt_[:, :], in1=s1t[:, :],
                              op=ALU.mult), "V")
        p.wait(v, "V")
        p.wait(v, "S")
        p.wait(v, "S2")
        p.inc(v.tensor_tensor(out=ot[:, :], in0=x1t[:, :], in1=gt_[:, :],
                              op=ALU.add), "V")
        iter_mark = p.cnt["V"]
        p.wait(g, "V")
        p.st(out_d[tt * 128:(tt + 1) * 128, :], ot[:, :], sem=ssem)
    p.bar()
    p.wait(g, "S")
    p.wait(g, "S2")


def emit_stage_cf1(p, ioc, iof1):
    nc = p.nc
    g = nc.gpsimd
    p.prefix = "f1p_"
    bufs = dict(
        acc1=p.sbuf("acc1", [128, NT * D], BF16),
        cw_sb=p.sbuf("cw_sb", [128, NT * NE], F32),
        sw_sb=p.sbuf("sw_sb", [128, NT], F32),
        tmpc=p.sbuf("tmpc", [128, D], F32),
    )
    p.push_scope()
    emit_f1_prologue(p, iof1, bufs)
    gc = gen_c_scan(p, ioc)
    gf = gen_f1_experts(p, iof1, bufs)
    next(gc)
    next(gf)
    done_c = done_f = False
    while not (done_c and done_f):
        if not done_c:
            try:
                next(gc)
            except StopIteration:
                done_c = True
        if not done_f:
            try:
                next(gf)
            except StopIteration:
                done_f = True
    for sm in ("S", "S2", "L", "L2", "L3", "L4", "L5"):
        p.wait(g, sm)
    p.bar()
    p.pop_scope()
    emit_c_outproj(p, ioc)
    emit_f1_final(p, iof1, bufs)


def build_fused():
    p = Prog("fused")
    EI, EO = "ExternalInput", "ExternalOutput"
    d = p.dram
    iop = dict(
        x0=d("x0", [T, D], F32, EI), wq=d("wq", [D, D], F32, EI),
        wk=d("wk", [D, D], F32, EI), wv=d("wv", [D, D], F32, EI),
        wo=d("wo", [D, D], F32, EI), ipw=d("ipw", [2 * DI, D], F32, EI),
        nw=d("nw", [D], F32, EI), dtw=d("dtw", [DI, DI], F32, EI),
        xpw2=d("xpw2", [NS, DI], F32, EI), opw=d("opw", [D, DI], F32, EI),
        alog=d("alog", [DI, NS], F32, EI),
        wqT=d("wqT", [D, D], BF16), wkT=d("wkT", [D, D], BF16),
        wvT=d("wvT", [D, D], BF16), woT=d("woT", [D, D], BF16),
        iprojT=d("iprojT", [D, 2 * DI], BF16), dtwT=d("dtwT", [DI, DI], BF16),
        xpT=d("xpT", [DI, NS], BF16), opT=d("opT", [DI, D], BF16),
        asb=d("asb", [128, NDI * NS], F32), xn0fm=d("xn0fm", [D, T], BF16),
        w1_0=d("w1_0", [NE, D, FF], F32, EI),
        w2_0=d("w2_0", [NE, FF, D], F32, EI),
        w1_1=d("w1_1", [NE, D, FF], F32, EI),
        w2_1=d("w2_1", [NE, FF, D], F32, EI),
        w1b_0=d("w1b_0", [NE * D * FF], BF16),
        w2b_0=d("w2b_0", [NE * FF * D], BF16),
        w1b_1=d("w1b_1", [NE * D * FF], BF16),
        w2b_1=d("w2b_1", [NE * FF * D], BF16),
    )
    ioa = dict(
        xn0fm=iop["xn0fm"], wqT=iop["wqT"], wkT=iop["wkT"], wvT=iop["wvT"],
        woT=iop["woT"], bq=d("bq", [D], F32, EI), bk=d("bk", [D], F32, EI),
        bv=d("bv", [D], F32, EI), bo=d("bo", [D], F32, EI),
        scale128=d("scale128", [128], F32, EI), s0=d("s0", [T, D], F32),
    )
    iof0 = dict(
        xn0fm=iop["xn0fm"], wrr=d("wrr", [128, ND * NE], F32, EI),
        w1=iop["w1b_0"].rearrange("(e d f) -> e d f", e=NE, d=D),
        b1r=d("b1r_0", [128, NE * (FF // 128)], F32, EI),
        w2=iop["w2b_0"].rearrange("(e f dd) -> e f dd", e=NE, f=FF),
        b2=d("b2_0", [NE, D], F32, EI),
        x0=iop["x0"], s0=ioa["s0"], x1=d("x1", [T, D], F32),
        xn1=d("xn1", [T, D], F32), xn1sfm=d("xn1sfm", [D, T], BF16),
        xhatfm=d("xhatfm", [D, T], BF16), xn1fm=d("xn1fm", [D, T], BF16),
    )
    ios = dict(
        xn1sfm=iof0["xn1sfm"], iprojT=iop["iprojT"],
        cwr=d("cwr", [128, NDI * KC], F32, EI),
        cbr=d("cbr", [128, NDI], F32, EI), xpT=iop["xpT"],
        dtwT=iop["dtwT"], dtbr=d("dtbr", [128, NDI], F32, EI),
        xc=d("xc", [DI, T], BF16), dtm=d("dtm", [DI, T], F32),
        bm=d("bm", [NS, T], BF16), sres=d("sres", [DI, T], BF16),
    )
    ioc = dict(
        xc=ios["xc"], dtm=ios["dtm"], bm=ios["bm"], asb=iop["asb"],
        dpr=d("dpr", [128, NDI], F32, EI), sres=ios["sres"], opT=iop["opT"],
        xn1=iof0["xn1"], s1=d("s1", [T, D], F32),
        y2=d("y2", [DI, T], BF16),
    )
    iof1 = dict(
        xn1fm=iof0["xn1fm"], xhatfm=iof0["xhatfm"],
        wgr=d("wgr", [128, ND * NE], F32, EI),
        gr=d("gr", [128, NE * ND], F32, EI),
        br=d("br", [128, NE * ND], F32, EI),
        w1=iop["w1b_1"].rearrange("(e d f) -> e d f", e=NE, d=D),
        b1r=d("b1r_1", [128, NE * (FF // 128)], F32, EI),
        w2=iop["w2b_1"].rearrange("(e f dd) -> e f dd", e=NE, f=FF),
        b2=d("b2_1", [NE, D], F32, EI),
        xn1=iof0["xn1"], x1=iof0["x1"], s1=ioc["s1"],
        out=d("out", [T, D], F32, EO),
    )
    p.emit_consts()
    g = p.nc.gpsimd
    for emit, io in ((emit_stage_p, iop), (emit_stage_a, ioa),
                     (emit_stage_f0, iof0), (emit_stage_s, ios)):
        p.push_scope()
        p.prefix = emit.__name__.replace("emit_stage_", "") + "_"
        emit(p, io)
        p.pop_scope()
        for sm in ("S", "S2", "L", "L2"):
            p.wait(g, sm)
        p.bar()
    p.push_scope()
    emit_stage_cf1(p, ioc, iof1)
    p.pop_scope()
    return p.done()


_progs = {}


def _get_prog(name, builder):
    if name not in _progs:
        _progs[name] = builder()
    return _progs[name]


def run_stage(name, builder, in_maps):
    nc = _get_prog(name, builder)
    res = run_bass_kernel_spmd(nc, in_maps, core_ids=list(range(8)))
    return res.results


def _host_layouts(inp):
    asc = np.ascontiguousarray
    cw = inp["L1_attn_conv_w"].reshape(DI, KC)

    def pcol(a, c):
        return asc(np.asarray(a).reshape(c, 128).T)

    return dict(
        xpw2=asc(inp["L1_attn_x_proj_w"][NS:, :]),
        wrr=asc(inp["L0_ffn_Wr"].reshape(ND, 128, NE).transpose(1, 0, 2)
                .reshape(128, ND * NE)),
        b1r_0=asc(inp["L0_ffn_b1"].reshape(NE, 32, 128).transpose(2, 0, 1)
                  .reshape(128, NE * 32)),
        cwr=asc(cw.reshape(NDI, 128, KC).transpose(1, 0, 2)
                .reshape(128, NDI * KC)),
        cbr=pcol(inp["L1_attn_conv_b"], NDI),
        dtbr=pcol(inp["L1_attn_dt_b"], NDI),
        dpr=pcol(inp["L1_attn_D_param"], NDI),
        wgr=asc(inp["L1_ffn_Wg"].reshape(ND, 128, NE).transpose(1, 0, 2)
                .reshape(128, ND * NE)),
        gr=asc(inp["L1_ffn_gamma"].reshape(NE, ND, 128).transpose(2, 0, 1)
               .reshape(128, NE * ND)),
        br=asc(inp["L1_ffn_beta"].reshape(NE, ND, 128).transpose(2, 0, 1)
               .reshape(128, NE * ND)),
        b1r_1=asc(inp["L1_ffn_b1"].reshape(NE, 32, 128).transpose(2, 0, 1)
                  .reshape(128, NE * 32)),
        scale128=np.repeat(np.asarray(inp["L0_attn_attn_scale"]), 128),
    )


def kernel(**inp):
    B = 8
    asc = np.ascontiguousarray
    hl = _host_layouts(inp)
    x_all = asc(inp["L0_attn_x"])
    maps = [dict(
        x0=asc(x_all[b]),
        wq=inp["L0_attn_Wq"], wk=inp["L0_attn_Wk"],
        wv=inp["L0_attn_Wv"], wo=inp["L0_attn_Wo"],
        ipw=inp["L1_attn_in_proj_w"], nw=inp["L1_attn_norm_w"],
        dtw=inp["L1_attn_dt_w"], opw=inp["L1_attn_out_proj_w"],
        alog=inp["L1_attn_A_log"],
        bq=inp["L0_attn_bq"], bk=inp["L0_attn_bk"],
        bv=inp["L0_attn_bv"], bo=inp["L0_attn_bo"],
        w1_0=inp["L0_ffn_W1"], w2_0=inp["L0_ffn_W2"], b2_0=inp["L0_ffn_b2"],
        w1_1=inp["L1_ffn_W1"], w2_1=inp["L1_ffn_W2"], b2_1=inp["L1_ffn_b2"],
        **hl,
    ) for b in range(B)]
    res = run_stage("fused", build_fused, maps)
    return np.stack([np.asarray(res[b]["out"], np.float32) for b in range(B)])


# version 09
# speedup vs baseline: 1.0205x; 1.0163x over previous
CF1_INTERLEAVE = True
"""Raw-Bass kernel for mk_recurrent_d2 (depth-2: local-attn+top1-MoE block,
then Mamba-SSM+top2-MoE block), d_model=1024, T=2048, batch 8 -> 8 cores.

Sharding: batch element b -> core b. All weights replicated to every core.

On-chip layout conventions:
  * token-major (tm): [128 tokens (partitions), features (free)]
  * feature-major (fm): [128 feature-dims (partitions), tokens (free)]
  * matmuls contract over the partition dim; activations feed matmuls in fm
    (bf16), the residual stream lives token-major in fp32 (HBM-backed).
  * weights W[out,in] used as x@W.T are pre-transposed once into bf16 HBM
    scratch (prep stage); W[in,out] weights (MoE W1/W2, Wr, Wg) load natural.
"""
import contextlib
import numpy as np
import ml_dtypes
import concourse.bass as bass
import concourse.mybir as mybir
from concourse.bass_utils import run_bass_kernel_spmd

F32 = mybir.dt.float32
BF16 = mybir.dt.bfloat16
AF = mybir.ActivationFunctionType
ALU = mybir.AluOpType
AX = mybir.AxisListType.X

T = 2048
D = 1024
NT = T // 128
ND = D // 128
NH = 16
DH = 64
NE = 8
FF = 4096
DI = 2048
NDI = DI // 128
NS = 64
KC = 4

nbf = ml_dtypes.bfloat16


class Prog:
    def __init__(self, name="prog"):
        self.nc = bass.Bass("TRN2", target_bir_lowering=False)
        self.stack = contextlib.ExitStack()
        self.scopes = [self.stack]
        self.cnt = {}
        self.sems = {}
        for s in ("L", "S", "P", "V", "A", "G", "S2", "L2", "L3", "L4",
                  "L5", "G2"):
            self.sems[s] = self.stack.enter_context(self.nc.semaphore(s))
            self.cnt[s] = 0

    prefix = ""

    def push_scope(self):
        self.scopes.append(contextlib.ExitStack())

    def pop_scope(self):
        self.scopes.pop().close()

    def sbuf(self, name, shape, dt):
        return self.scopes[-1].enter_context(
            self.nc.sbuf_tensor(self.prefix + name, shape, dt))

    def psum(self, name, shape, dt):
        return self.scopes[-1].enter_context(
            self.nc.psum_tensor(self.prefix + name, shape, dt))

    def dram(self, name, shape, dt, kind=None):
        if kind is None:
            return self.nc.dram_tensor(name, shape, dt).ap()
        return self.nc.dram_tensor(name, shape, dt, kind=kind).ap()

    def inc(self, inst, sem, n=1):
        inst.then_inc(self.sems[sem], n)
        self.cnt[sem] += n
        return inst

    def ld(self, out, in_, sem="L", **kw):
        self.inc(self.nc.sync.dma_start(out=out, in_=in_, **kw), sem, 16)
        return self.cnt[sem]

    def st(self, out, in_, sem="S", **kw):
        self.inc(self.nc.gpsimd.dma_start(out=out, in_=in_, **kw), sem, 16)
        return self.cnt[sem]

    def wait(self, eng, sem, val=None):
        if val is None:
            val = self.cnt[sem]
        if val > 0:
            eng.wait_ge(self.sems[sem], val)

    def bar(self):
        self.nc.all_engine_barrier()

    def emit_consts(self):
        c = self.sbuf("consts", [128, 8], F32)
        v = self.nc.vector
        for i, val in enumerate([-1.0, 1.0, -0.5, 1.0 / D, 1e-5, 1e-6,
                                 1.0 / DI, 0.0]):
            v.memset(c[:, i:i + 1], val)
        ident = self.sbuf("identity", [128, 128], BF16)
        g = self.nc.gpsimd
        self.inc(g.iota(ident[:, :], [[1, 128]], channel_multiplier=-1,
                        allow_small_or_imprecise_dtypes=True), "G")
        self.wait(g, "G")
        g.tensor_single_scalar(out=ident[:], in_=ident[:],
                               scalar=0.0, op=ALU.is_equal)
        self.consts = c
        self.ident = ident
        self.bar()

    def C(self, i):
        return self.consts[:, i:i + 1]

    def done(self):
        self.stack.close()
        return self.nc


def emit_w_to_bf16(p, scr, src, dst, nelem):
    nc = p.nc
    v, g = nc.vector, nc.gpsimd
    CH = 128 * 4096
    nch = nelem // CH
    assert nelem % CH == 0
    stgf = scr["wc_stgf"]
    stgb = scr["wc_stgb"]
    sflat = src
    cast_marks = [p.cnt["V"]] * 2
    store_marks = [p.cnt["S"], p.cnt["S2"]]
    for c in range(nch):
        b = c % 2
        lsem = "L" if b == 0 else "L2"
        ssem = "S" if b == 0 else "S2"
        p.wait(nc.sync, "V", cast_marks[b])
        p.ld(stgf[:, b * 4096:(b + 1) * 4096],
             sflat[c * CH:(c + 1) * CH].rearrange("(p f) -> p f", p=128),
             sem=lsem)
        p.wait(v, lsem)
        p.wait(v, ssem, store_marks[b])
        p.inc(v.tensor_copy(out=stgb[:, b * 4096:(b + 1) * 4096],
                            in_=stgf[:, b * 4096:(b + 1) * 4096]), "V")
        cast_marks[b] = p.cnt["V"]
        p.wait(g, "V")
        p.st(dst[c * CH:(c + 1) * CH].rearrange("(p f) -> p f", p=128),
             stgb[:, b * 4096:(b + 1) * 4096], sem=ssem)
        store_marks[b] = p.cnt[ssem]
    p.bar()


def emit_weight_transpose(p, scr, src, dst, R, C, scale_sb=None):
    nc = p.nc
    assert C <= 2048 and R % 128 == 0 and C % 128 == 0
    nr, ncc = R // 128, C // 128
    stgf, otile, pst = scr["stgf"], scr["otile"], scr["pst"]
    v, t, g = nc.vector, nc.tensor, nc.gpsimd
    W = 2048
    cast_marks = [p.cnt["V"]] * 2
    evac_marks = [p.cnt["V"]] * 2
    store_marks = [p.cnt["S"], p.cnt["S2"]]
    for r in range(nr):
        b = r % 2
        lsem = "L" if b == 0 else "L2"
        p.wait(nc.sync, "V", cast_marks[b])
        p.ld(stgf[:, b * W:b * W + C], src[r * 128:(r + 1) * 128, :],
             sem=lsem)
        p.wait(v, lsem)
        p.wait(v, "S" if b == 0 else "S2", store_marks[b])
        p.inc(v.tensor_copy(out=otile[:, b * W:b * W + C],
                            in_=stgf[:, b * W:b * W + C]), "V")
        cast_marks[b] = p.cnt["V"]
        p.wait(t, "V")
        p.wait(t, "V", evac_marks[b])
        for cc in range(ncc):
            inst = t.transpose(pst[b][:, cc * 128:(cc + 1) * 128],
                               otile[:, b * W + cc * 128:b * W + (cc + 1) * 128],
                               p.ident[:, :])
        p.inc(inst, "P")
        p.wait(v, "P")
        if scale_sb is None:
            p.inc(v.tensor_copy(out=otile[:, b * W:b * W + C],
                                in_=pst[b][:, 0:C]), "V")
        else:
            for cc in range(ncc):
                inst = v.tensor_scalar(
                    out=otile[:, b * W + cc * 128:b * W + (cc + 1) * 128],
                    in0=pst[b][:, cc * 128:(cc + 1) * 128],
                    scalar1=scale_sb[:, cc:cc + 1], scalar2=None,
                    op0=ALU.mult)
            p.inc(inst, "V")
        evac_marks[b] = p.cnt["V"]
        p.wait(g, "V")
        ssem = "S" if b == 0 else "S2"
        for cc in range(ncc):
            p.st(dst[cc * 128:(cc + 1) * 128, r * 128:(r + 1) * 128],
                 otile[:, b * W + cc * 128:b * W + (cc + 1) * 128], sem=ssem)
        store_marks[b] = p.cnt[ssem]
    p.bar()


def emit_ln_to_fm(p, x_dram, xfm_dram):
    nc = p.nc
    v, s, t, g = nc.vector, nc.scalar, nc.tensor, nc.gpsimd
    xt = p.sbuf("ln_x", [128, 2 * D], F32)
    xnb = p.sbuf("ln_xnb", [128, 2 * D], BF16)
    trash = p.sbuf("ln_trash", [128, D], BF16)
    stat = p.sbuf("ln_stat", [128, 16], F32)
    pst = [p.psum(f"ln_ps{i}", [128, D], BF16) for i in range(2)]
    xread_marks = [0, 0]
    evac_marks = [0, 0]
    store_marks = [0, 0]
    for tt in range(NT):
        b = tt % 2
        o = 8 * b
        lsem = "L" if b == 0 else "L2"
        p.wait(nc.sync, "V", xread_marks[b])
        ldv = p.ld(xt[:, b * D:(b + 1) * D],
                   x_dram[tt * 128:(tt + 1) * 128, :], sem=lsem)
        xv = xt[:, b * D:(b + 1) * D]
        p.wait(v, lsem, ldv)
        p.inc(v.reduce_sum(stat[:, o:o + 1], xv, axis=AX), "V")
        p.wait(s, lsem, ldv)
        p.wait(s, "V", xread_marks[b])
        p.inc(s.activation(out=trash[:, :], in_=xv, func=AF.Square,
                           accum_out=stat[:, o + 1:o + 2]), "A")
        sq_mark = p.cnt["A"]
        p.wait(v, "A", sq_mark)
        p.wait(v, "V")
        p.inc(v.tensor_scalar(out=stat[:, o + 3:o + 4], in0=stat[:, o:o + 1],
                              scalar1=-1.0 / D, scalar2=None, op0=ALU.mult),
              "V")
        p.wait(v, "V")
        v.tensor_tensor(out=stat[:, o + 6:o + 7], in0=stat[:, o + 3:o + 4],
                        in1=stat[:, o + 3:o + 4], op=ALU.mult)
        p.inc(v.tensor_scalar(out=stat[:, o + 4:o + 5],
                              in0=stat[:, o + 1:o + 2],
                              scalar1=1.0 / D, scalar2=None, op0=ALU.mult),
              "V")
        p.wait(v, "V")
        p.inc(v.tensor_tensor(out=stat[:, o + 5:o + 6],
                              in0=stat[:, o + 4:o + 5],
                              in1=stat[:, o + 6:o + 7], op=ALU.subtract), "V")
        var_mark = p.cnt["V"]
        p.wait(s, "V", var_mark)
        p.inc(s.activation(out=stat[:, o + 2:o + 3], in_=stat[:, o + 5:o + 6],
                           func=AF.Ln, bias=p.C(4)), "A")
        p.wait(s, "A")
        p.inc(s.activation(out=stat[:, o + 2:o + 3], in_=stat[:, o + 2:o + 3],
                           func=AF.Exp, scale=p.C(2)), "A")
        p.wait(v, "A")
        p.wait(v, "S" if b == 0 else "S2", store_marks[b])
        p.inc(v.tensor_scalar(out=xnb[:, b * D:(b + 1) * D], in0=xv,
                              scalar1=stat[:, o + 3:o + 4],
                              scalar2=stat[:, o + 2:o + 3],
                              op0=ALU.add, op1=ALU.mult), "V")
        xread_marks[b] = p.cnt["V"]
        p.wait(t, "V")
        p.wait(t, "V", evac_marks[b])
        for dc in range(ND):
            inst = t.transpose(pst[b][:, dc * 128:(dc + 1) * 128],
                               xnb[:, b * D + dc * 128:b * D + (dc + 1) * 128],
                               p.ident[:, :])
        p.inc(inst, "P")
        p.wait(v, "P")
        p.inc(v.tensor_copy(out=xnb[:, b * D:(b + 1) * D], in_=pst[b][:, :]),
              "V")
        evac_marks[b] = p.cnt["V"]
        p.wait(g, "V")
        ssem = "S" if b == 0 else "S2"
        for dc in range(ND):
            p.st(xfm_dram[dc * 128:(dc + 1) * 128, tt * 128:(tt + 1) * 128],
                 xnb[:, b * D + dc * 128:b * D + (dc + 1) * 128], sem=ssem)
        store_marks[b] = p.cnt[ssem]
    p.bar()


def emit_stage_p(p, io):
    nc = p.nc
    x0 = io["x0"]
    wq = io["wq"]
    wk = io["wk"]
    wv = io["wv"]
    wo = io["wo"]
    ipw = io["ipw"]
    nw = io["nw"]
    dtw = io["dtw"]
    xpw2 = io["xpw2"]
    opw = io["opw"]
    alog = io["alog"]

    wqT = io["wqT"]
    wkT = io["wkT"]
    wvT = io["wvT"]
    woT = io["woT"]
    iprojT = io["iprojT"]
    dtwT = io["dtwT"]
    xpT = io["xpT"]
    opT = io["opT"]
    asb_d = io["asb"]
    xn0fm = io["xn0fm"]


    v, s = nc.vector, nc.scalar
    at = p.sbuf("a_t", [128, NDI * NS], F32)
    p.ld(at[:, :].rearrange("p (c n) -> p c n", n=NS),
         alog.rearrange("(c p) n -> p c n", p=128),
         allow_slow_non_contiguous=True)
    p.wait(s, "L")
    p.inc(s.activation(out=at[:, :], in_=at[:, :], func=AF.Exp), "A")
    p.wait(v, "A")
    p.inc(v.tensor_scalar(out=at[:, :], in0=at[:, :], scalar1=-1.0,
                          scalar2=None, op0=ALU.mult), "V")
    p.wait(nc.gpsimd, "V")
    p.st(asb_d[:, :], at[:, :])
    nwt = p.sbuf("nw_t", [128, ND], F32)
    p.ld(nwt[:, :], nw.rearrange("(c p) -> p c", p=128),
         allow_slow_non_contiguous=True, sem="L2")
    p.wait(v, "L2")
    p.bar()

    scr = dict(
        stgf=p.sbuf("wt_stgf", [128, 2 * 2048], F32),
        otile=p.sbuf("wt_ot", [128, 2 * 2048], BF16),
        pst=[p.psum(f"wt_ps{i}", [128, 2048], BF16) for i in range(2)],
    )
    emit_weight_transpose(p, scr, wq, wqT, D, D)
    emit_weight_transpose(p, scr, wk, wkT, D, D)
    emit_weight_transpose(p, scr, wv, wvT, D, D)
    emit_weight_transpose(p, scr, wo, woT, D, D)
    emit_weight_transpose(p, scr, ipw, iprojT, 2 * DI, D, scale_sb=nwt)
    emit_weight_transpose(p, scr, dtw, dtwT, DI, DI)
    emit_weight_transpose(p, scr, opw, opT, D, DI)

    t, g = nc.tensor, nc.gpsimd
    xp_stgf = scr["stgf"]
    xp_stgb = scr["otile"]
    xp_ot_full = scr["otile"]
    xp_ps = scr["pst"][0]
    p.wait(v, "S")
    p.wait(v, "S2")
    p.inc(v.memset(xp_stgb[:, 0:DI], 0.0), "V")
    p.ld(xp_stgf[0:64, 0:DI], xpw2[:, :])
    p.wait(v, "L")
    p.wait(v, "V")
    p.inc(v.tensor_copy(out=xp_stgb[0:64, 0:DI], in_=xp_stgf[0:64, 0:DI]),
          "V")
    p.wait(t, "V")
    for cc in range(NDI):
        inst = t.transpose(xp_ps[:, cc * 128:(cc + 1) * 128],
                           xp_stgb[:, cc * 128:(cc + 1) * 128],
                           p.ident[:, :])
    p.inc(inst, "P")
    p.wait(v, "P")
    for cc in range(NDI):
        inst = v.tensor_copy(out=xp_ot_full[:, 2048 + cc * NS: 2048 + (cc + 1) * NS],
                             in_=xp_ps[:, cc * 128:cc * 128 + NS])
    p.inc(inst, "V")
    p.wait(g, "V")
    for cc in range(NDI):
        p.st(xpT[cc * 128:(cc + 1) * 128, :],
             xp_ot_full[:, 2048 + cc * NS: 2048 + (cc + 1) * NS])
    p.bar()

    p.push_scope()
    wscr = dict(wc_stgf=p.sbuf("wc_stgf", [128, 2 * 4096], F32),
                wc_stgb=p.sbuf("wc_stgb", [128, 2 * 4096], BF16))
    for wsrc, wdst in (("w1_0", "w1b_0"), ("w2_0", "w2b_0"),
                       ("w1_1", "w1b_1"), ("w2_1", "w2b_1")):
        emit_w_to_bf16(p, wscr,
                       io[wsrc].rearrange("e a b -> (e a b)"),
                       io[wdst], NE * D * FF)
    p.pop_scope()

    emit_ln_to_fm(p, x0, xn0fm)

    g = nc.gpsimd
    p.wait(g, "S")
    p.wait(g, "L")
def build_stage_p():
    p = Prog("p")
    io = {}
    io["x0"] = p.dram("x0", [T, D], F32, "ExternalInput")
    io["xpw2"] = p.dram("xpw2", [NS, DI], F32, "ExternalInput")
    io["wq"] = p.dram("wq", [D, D], F32, "ExternalInput")
    io["wk"] = p.dram("wk", [D, D], F32, "ExternalInput")
    io["wv"] = p.dram("wv", [D, D], F32, "ExternalInput")
    io["wo"] = p.dram("wo", [D, D], F32, "ExternalInput")
    io["ipw"] = p.dram("ipw", [2 * DI, D], F32, "ExternalInput")
    io["nw"] = p.dram("nw", [D], F32, "ExternalInput")
    io["dtw"] = p.dram("dtw", [DI, DI], F32, "ExternalInput")
    io["opw"] = p.dram("opw", [D, DI], F32, "ExternalInput")
    io["alog"] = p.dram("alog", [DI, NS], F32, "ExternalInput")
    io["wqT"] = p.dram("wqT", [D, D], BF16, "ExternalOutput")
    io["wkT"] = p.dram("wkT", [D, D], BF16, "ExternalOutput")
    io["wvT"] = p.dram("wvT", [D, D], BF16, "ExternalOutput")
    io["woT"] = p.dram("woT", [D, D], BF16, "ExternalOutput")
    io["iprojT"] = p.dram("iprojT", [D, 2 * DI], BF16, "ExternalOutput")
    io["dtwT"] = p.dram("dtwT", [DI, DI], BF16, "ExternalOutput")
    io["xpT"] = p.dram("xpT", [DI, NS], BF16, "ExternalOutput")
    io["opT"] = p.dram("opT", [DI, D], BF16, "ExternalOutput")
    io["asb"] = p.dram("asb", [128, NDI * NS], F32, "ExternalOutput")
    io["xn0fm"] = p.dram("xn0fm", [D, T], BF16, "ExternalOutput")
    p.emit_consts()
    p.push_scope()
    emit_stage_p(p, io)
    p.pop_scope()
    return p.done()



def emit_stage_a(p, io):
    nc = p.nc
    v, s, t, g = nc.vector, nc.scalar, nc.tensor, nc.gpsimd
    xn0fm = io["xn0fm"]
    wqT = io["wqT"]
    wkT = io["wkT"]
    wvT = io["wvT"]
    woT = io["woT"]
    bq = io["bq"]
    bk = io["bk"]
    bv = io["bv"]
    bo = io["bo"]
    scale128 = io["scale128"]
    s0_d = io["s0"]


    xn0 = p.sbuf("xn0", [128, ND * T], BF16)
    wbuf = p.sbuf("wbuf", [128, ND * D], BF16)
    wq_sb = wk_sb = wv_sb = wo_sb = wbuf
    q_sb = p.sbuf("q_sb", [128, ND * T], BF16)
    k_sb = p.sbuf("k_sb", [128, ND * T], BF16)
    v_sb = p.sbuf("v_sb", [128, NT * D], BF16)
    bias_sb = p.sbuf("bias_sb", [128, 4 * ND], F32)
    scale_sb = p.sbuf("scale_sb", [128, 1], F32)
    brow = p.sbuf("brow", [1, 2 * D], BF16)
    browf = p.sbuf("browf", [1, 2 * D], F32)
    bvb = p.sbuf("bv_bcast", [128, D], F32)
    bob = p.sbuf("bo_bcast", [128, D], F32)
    ones1 = p.sbuf("ones1", [1, 128], BF16)
    maskc = p.sbuf("maskc", [128, 128], F32)
    mk_t = p.sbuf("mk_t", [128, 3 * 128], F32)
    mk_b = p.sbuf("mk_b", [128, 2], F32)
    probs = p.sbuf("probs", [128, 2 * T], BF16)
    probsT = p.sbuf("probsT", [128, T], BF16)
    ao_sb = p.sbuf("ao_sb", [128, ND * 128], BF16)
    att_st = p.sbuf("att_st", [128, 8], F32)
    s0t = p.sbuf("s0t", [128, 2 * D], F32)

    ps_sc = p.psum("ps_sc", [128, T], F32)
    ps_tr = p.psum("ps_tr", [128, T], BF16)
    ps_av = p.psum("ps_av", [128, 128], F32)
    ps_s0 = p.psum("ps_s0", [128, 512], F32)

    for dc in range(ND):
        p.ld(xn0[:, dc * T:(dc + 1) * T], xn0fm[dc * 128:(dc + 1) * 128, :])
    p.ld(wbuf[:, :].rearrange("p (c o) -> p c o", c=ND),
         wqT.rearrange("(c p) o -> p c o", p=128))
    p.ld(bias_sb[:, 0:ND], bq.rearrange("(c p) -> p c", p=128),
         allow_slow_non_contiguous=True)
    p.ld(bias_sb[:, ND:2 * ND], bk.rearrange("(c p) -> p c", p=128),
         allow_slow_non_contiguous=True)
    p.ld(scale_sb[:, 0:1], scale128[:, None])
    p.ld(browf[0:1, 0:D], bv[None, :])
    p.ld(browf[0:1, D:2 * D], bo[None, :])
    p.inc(g.iota(mk_t[:, 0:128], [[1, 2], [0, 64]], channel_multiplier=0,
                 allow_small_or_imprecise_dtypes=True), "G")
    p.inc(g.iota(mk_t[:, 128:256], [[1, 128]], channel_multiplier=-1,
                 allow_small_or_imprecise_dtypes=True), "G")
    p.inc(g.iota(mk_b[:, 0:1], [[0, 1]], channel_multiplier=1,
                 allow_small_or_imprecise_dtypes=True), "G")
    p.wait(v, "G")
    p.inc(v.tensor_scalar(out=mk_b[:, 1:2], in0=mk_b[:, 0:1], scalar1=63.5,
                          scalar2=None, op0=ALU.is_gt), "V")
    p.wait(v, "V")
    p.inc(v.tensor_scalar(out=mk_t[:, 256:384], in0=mk_t[:, 0:128],
                          scalar1=mk_b[:, 1:2], scalar2=None, op0=ALU.is_gt),
          "V")
    p.wait(v, "V")
    p.inc(v.tensor_scalar(out=mk_t[:, 0:128], in0=mk_t[:, 0:128],
                          scalar1=mk_b[:, 1:2], scalar2=None,
                          op0=ALU.is_equal), "V")
    p.wait(v, "V")
    p.inc(v.tensor_single_scalar(out=mk_t[:, 128:256], in_=mk_t[:, 128:256],
                                 scalar=0.5, op=ALU.is_le), "V")
    p.wait(v, "V")
    p.inc(v.tensor_tensor(out=mk_t[:, 0:128], in0=mk_t[:, 0:128],
                          in1=mk_t[:, 128:256], op=ALU.mult), "V")
    p.wait(v, "V")
    p.inc(v.tensor_tensor(out=mk_t[:, 0:128], in0=mk_t[:, 0:128],
                          in1=mk_t[:, 256:384], op=ALU.max), "V")
    p.wait(v, "V")
    p.inc(v.tensor_scalar(out=maskc[:, :], in0=mk_t[:, 0:128], scalar1=-1.0,
                          scalar2=1e30, op0=ALU.add, op1=ALU.mult), "V")
    p.wait(v, "L")
    p.inc(v.tensor_scalar(out=bias_sb[:, 0:ND], in0=bias_sb[:, 0:ND],
                          scalar1=scale_sb[:, 0:1], scalar2=None,
                          op0=ALU.mult), "V")
    v.memset(ones1[0:1, :], 1.0)
    p.wait(v, "V")
    p.inc(v.tensor_copy(out=brow[0:1, :], in_=browf[0:1, :]), "V")
    p.wait(t, "V")
    for j, (dst, off) in enumerate(((bvb, 0), (bvb, 512), (bob, D),
                                    (bob, D + 512))):
        inst = t.matmul(ps_sc[:, (j % 4) * 512:(j % 4) * 512 + 512],
                        ones1[0:1, :], brow[0:1, off:off + 512],
                        start=True, stop=True)
    p.inc(inst, "P")
    p.wait(v, "P")
    v.tensor_copy(out=bvb[:, :], in_=ps_sc[:, 0:1024])
    p.inc(v.tensor_copy(out=bob[:, :], in_=ps_sc[:, 1024:2048]), "V")
    p.bar()

    def proj_qk(out_sb, bias_off, use_scale):
        evac_marks = [0, 0]
        for och in range(ND):
            for tch in range(4):
                half = (och * 4 + tch) % 2
                pso = ps_s0 if half == 0 else ps_sc
                psv = pso[:, 0:512]
                p.wait(t, "V", evac_marks[half])
                for dch in range(ND):
                    inst = t.matmul(
                        psv,
                        wbuf[:, dch * D + och * 128:dch * D + (och + 1) * 128],
                        xn0[:, dch * T + tch * 512:dch * T + (tch + 1) * 512],
                        start=(dch == 0), stop=(dch == ND - 1))
                p.inc(inst, "P")
                p.wait(v, "P")
                if use_scale:
                    inst = v.tensor_scalar(
                        out=out_sb[:, och * T + tch * 512:
                                   och * T + (tch + 1) * 512],
                        in0=psv, scalar1=scale_sb[:, 0:1],
                        scalar2=bias_sb[:, bias_off + och:bias_off + och + 1],
                        op0=ALU.mult, op1=ALU.add)
                else:
                    inst = v.tensor_scalar(
                        out=out_sb[:, och * T + tch * 512:
                                   och * T + (tch + 1) * 512],
                        in0=psv,
                        scalar1=bias_sb[:, bias_off + och:bias_off + och + 1],
                        scalar2=None, op0=ALU.add)
                p.inc(inst, "V")
                evac_marks[half] = p.cnt["V"]

    proj_qk(q_sb, 0, True)
    p.bar()
    p.ld(wbuf[:, :].rearrange("p (c o) -> p c o", c=ND),
         wkT.rearrange("(c p) o -> p c o", p=128))
    p.wait(t, "L")
    proj_qk(k_sb, ND, False)
    p.bar()
    p.ld(wbuf[:, :].rearrange("p (c o) -> p c o", c=ND),
         wvT.rearrange("(c p) o -> p c o", p=128))
    p.wait(t, "L")
    evac_marks = [0, 0]
    for t128 in range(NT):
        for oh in range(2):
            half = (t128 * 2 + oh) % 2
            pso = ps_s0 if half == 0 else ps_sc
            psv = pso[:, 0:512]
            p.wait(t, "V", evac_marks[half])
            for dch in range(ND):
                inst = t.matmul(
                    psv,
                    xn0[:, dch * T + t128 * 128:dch * T + (t128 + 1) * 128],
                    wbuf[:, dch * D + oh * 512:dch * D + (oh + 1) * 512],
                    start=(dch == 0), stop=(dch == ND - 1))
            p.inc(inst, "P")
            p.wait(v, "P")
            p.inc(v.tensor_tensor(
                out=v_sb[:, t128 * D + oh * 512:t128 * D + (oh + 1) * 512],
                in0=psv, in1=bvb[:, oh * 512:(oh + 1) * 512], op=ALU.add),
                "V")
            evac_marks[half] = p.cnt["V"]
    p.bar()
    p.ld(wbuf[:, :].rearrange("p (c o) -> p c o", c=ND),
         woT.rearrange("(c p) o -> p c o", p=128))
    p.wait(t, "L")
    p.bar()

    exp_mark = 0
    trans_mark = 0
    trev_mark = 0
    aoev_mark = 0
    s0proj_mark = 0
    s0ev_mark = 0
    s0st_marks = [0, 0]
    probs_free = [0, 0]
    for tq in range(NT):
        for h in range(NH):
            hb = h % 2
            po = (h % 2) * 64
            och = h // 2
            koff = 128 * tq
            W = T - koff
            nk1 = W // 128
            qsl = q_sb[po:po + 64,
                       och * T + tq * 128:och * T + (tq + 1) * 128]
            p.wait(t, "A", exp_mark)
            nfull = (W + 511) // 512
            for c5 in range(nfull):
                wmm = min(512, W - c5 * 512)
                inst = t.matmul(
                    ps_sc[:, c5 * 512:c5 * 512 + wmm], qsl,
                    k_sb[po:po + 64,
                         och * T + koff + c5 * 512:
                         och * T + koff + c5 * 512 + wmm],
                    start=True, stop=True)
            p.inc(inst, "P")
            p.wait(v, "P")
            p.inc(v.tensor_tensor(out=ps_sc[:, 0:128], in0=ps_sc[:, 0:128],
                                  in1=maskc[:, :], op=ALU.add), "V")
            p.wait(v, "V")
            p.inc(v.reduce_max(att_st[:, 0:1], ps_sc[:, 0:W], axis=AX), "V")
            p.wait(v, "V")
            p.inc(v.tensor_scalar(out=att_st[:, 1:2], in0=att_st[:, 0:1],
                                  scalar1=-1.0, scalar2=None, op0=ALU.mult),
                  "V")
            p.wait(s, "V")
            p.wait(s, "P", probs_free[hb])
            p.inc(s.activation(out=probs[:, hb * T:hb * T + W],
                               in_=ps_sc[:, 0:W], func=AF.Exp,
                               bias=att_st[:, 1:2],
                               accum_out=att_st[:, 2:3]), "A")
            exp_mark = p.cnt["A"]
            p.wait(v, "A")
            p.inc(v.reciprocal(att_st[:, 3:4], att_st[:, 2:3]), "V")
            p.wait(v, "V")
            p.inc(v.tensor_scalar(out=probs[:, hb * T:hb * T + W],
                                  in0=probs[:, hb * T:hb * T + W],
                                  scalar1=att_st[:, 3:4], scalar2=None,
                                  op0=ALU.mult), "V")
            norm_mark = p.cnt["V"]
            p.wait(t, "V", norm_mark)
            p.wait(t, "V", trev_mark)
            for kb in range(nk1):
                inst = t.transpose(ps_tr[:, kb * 128:(kb + 1) * 128],
                                   probs[:, hb * T + kb * 128:
                                         hb * T + (kb + 1) * 128],
                                   p.ident[:, :])
            p.inc(inst, "P")
            probs_free[hb] = p.cnt["P"]
            trans_mark = p.cnt["P"]
            p.wait(v, "P", trans_mark)
            p.inc(v.tensor_copy(out=probsT[:, 0:W], in_=ps_tr[:, 0:W]), "V")
            trev_mark = p.cnt["V"]
            p.wait(t, "V", trev_mark)
            p.wait(t, "V", aoev_mark)
            for kb in range(nk1):
                inst = t.matmul(ps_av[0:64, :],
                                v_sb[:, (tq + kb) * D + h * 64:
                                     (tq + kb) * D + h * 64 + 64],
                                probsT[:, kb * 128:(kb + 1) * 128],
                                start=(kb == 0), stop=(kb == nk1 - 1))
            p.inc(inst, "P")
            p.wait(v, "P")
            p.wait(v, "P", s0proj_mark)
            p.inc(v.tensor_copy(
                out=ao_sb[po:po + 64, och * 128:(och + 1) * 128],
                in_=ps_av[0:64, :]), "V")
            aoev_mark = p.cnt["V"]
        tb = tq % 2
        p.wait(t, "V")
        for oh in range(2):
            p.wait(t, "V", s0ev_mark)
            for och in range(ND):
                inst = t.matmul(ps_s0[:, :],
                                ao_sb[:, och * 128:(och + 1) * 128],
                                wo_sb[:, och * D + oh * 512:
                                      och * D + (oh + 1) * 512],
                                start=(och == 0), stop=(och == ND - 1))
            p.inc(inst, "P")
            if oh == 1:
                s0proj_mark = p.cnt["P"]
            p.wait(v, "P")
            p.wait(v, "S" if tb == 0 else "S2", s0st_marks[tb])
            p.inc(v.tensor_tensor(
                out=s0t[:, tb * D + oh * 512:tb * D + (oh + 1) * 512],
                in0=ps_s0[:, :], in1=bob[:, oh * 512:(oh + 1) * 512],
                op=ALU.add), "V")
            s0ev_mark = p.cnt["V"]
            p.wait(g, "V")
            ssem = "S" if tb == 0 else "S2"
            p.st(s0_d[tq * 128:(tq + 1) * 128, oh * 512:(oh + 1) * 512],
                 s0t[:, tb * D + oh * 512:tb * D + (oh + 1) * 512], sem=ssem)
            s0st_marks[tb] = p.cnt[ssem]
    p.bar()
    p.wait(g, "S")
def build_stage_a():
    p = Prog("a")
    io = {}
    io["xn0fm"] = p.dram("xn0fm", [D, T], BF16, "ExternalInput")
    io["wqT"] = p.dram("wqT", [D, D], BF16, "ExternalInput")
    io["wkT"] = p.dram("wkT", [D, D], BF16, "ExternalInput")
    io["wvT"] = p.dram("wvT", [D, D], BF16, "ExternalInput")
    io["woT"] = p.dram("woT", [D, D], BF16, "ExternalInput")
    io["bq"] = p.dram("bq", [D], F32, "ExternalInput")
    io["bk"] = p.dram("bk", [D], F32, "ExternalInput")
    io["bv"] = p.dram("bv", [D], F32, "ExternalInput")
    io["bo"] = p.dram("bo", [D], F32, "ExternalInput")
    io["scale128"] = p.dram("scale128", [128], F32, "ExternalInput")
    io["s0"] = p.dram("s0", [T, D], F32, "ExternalOutput")
    p.emit_consts()
    p.push_scope()
    emit_stage_a(p, io)
    p.pop_scope()
    return p.done()



def emit_stage_f0(p, io):
    nc = p.nc
    v, s, t, g = nc.vector, nc.scalar, nc.tensor, nc.gpsimd
    xn0fm = io["xn0fm"]
    wrr = io["wrr"]
    w1 = io["w1"]
    b1r = io["b1r"]
    w2 = io["w2"]
    b2 = io["b2"]
    x0 = io["x0"]
    s0_d = io["s0"]
    x1_d = io["x1"]
    xn1_d = io["xn1"]
    xn1sfm = io["xn1sfm"]
    xhatfm = io["xhatfm"]
    xn1fm = io["xn1fm"]


    xn0 = p.sbuf("xn0", [128, ND * T], BF16)
    acc0 = p.sbuf("acc0", [128, NT * D], BF16)
    acc2 = p.sbuf("acc2", [128, NT * D], BF16)
    stg = p.sbuf("stg", [128, 4096], F32)
    w1bf = p.sbuf("w1bf", [128, 2 * 4096], BF16)
    w2bf = p.sbuf("w2bf", [128, 2 * 4096], BF16)
    h_sb = p.sbuf("h_sb", [128, 4 * T], BF16)
    wr_sb = p.sbuf("wr_sb", [128, ND * NE], BF16)
    b1_sb = p.sbuf("b1_sb", [128, NE * (FF // 128)], F32)
    b2row = p.sbuf("b2row", [1, D], BF16)
    b2rowf = p.sbuf("b2rowf", [1, D], F32)
    b2b = p.sbuf("b2b", [128, D], F32)
    ones1 = p.sbuf("ones1", [1, 128], BF16)
    logit_sb = p.sbuf("logit_sb", [128, NT * NE], F32)
    sc_sb = p.sbuf("sc_sb", [128, NT * NE], F32)
    lst = p.sbuf("lst", [128, 8], F32)
    trash8 = p.sbuf("trash8", [128, NE], F32)
    tmpc = p.sbuf("tmpc", [128, D], F32)

    ps_h = [p.psum(f"ps_h{i}", [128, 512], F32) for i in range(2)]
    ps_o = [p.psum(f"ps_o{i}", [128, 512], F32) for i in range(2)]
    ps_bc = p.psum("ps_bc", [128, D], F32)

    for dc in range(ND):
        p.ld(xn0[:, dc * T:(dc + 1) * T], xn0fm[dc * 128:(dc + 1) * 128, :])
    p.ld(stg[:, 0:ND * NE], wrr[:, :])
    p.ld(b1_sb[:, :], b1r[:, :])
    v.memset(ones1[0:1, :], 1.0)
    p.wait(v, "L")
    p.inc(v.tensor_copy(out=wr_sb[:, :], in_=stg[:, 0:ND * NE]), "V")
    p.bar()

    rev_marks = [0, 0]
    for t128 in range(NT):
        pb = t128 % 2
        p.wait(t, "V", rev_marks[pb])
        for dch in range(ND):
            inst = t.matmul(ps_h[pb][:, 0:NE],
                            xn0[:, dch * T + t128 * 128:
                                dch * T + (t128 + 1) * 128],
                            wr_sb[:, dch * NE:(dch + 1) * NE],
                            start=(dch == 0), stop=(dch == ND - 1))
        p.inc(inst, "P")
        p.wait(v, "P")
        p.inc(v.tensor_copy(out=logit_sb[:, t128 * NE:(t128 + 1) * NE],
                            in_=ps_h[pb][:, 0:NE]), "V")
        rev_marks[pb] = p.cnt["V"]
    for t128 in range(NT):
        lg = logit_sb[:, t128 * NE:(t128 + 1) * NE]
        p.wait(v, "V")
        p.inc(v.reduce_max(lst[:, 0:1], lg, axis=AX), "V")
        p.wait(v, "V")
        p.inc(v.tensor_scalar(out=lst[:, 1:2], in0=lst[:, 0:1], scalar1=-1.0,
                              scalar2=None, op0=ALU.mult), "V")
        p.wait(v, "V")
        p.inc(v.tensor_scalar(out=sc_sb[:, t128 * NE:(t128 + 1) * NE],
                              in0=lg, scalar1=lst[:, 0:1], scalar2=None,
                              op0=ALU.is_equal), "V")
        p.wait(s, "V")
        p.inc(s.activation(out=trash8[:, :], in_=lg, func=AF.Exp,
                           bias=lst[:, 1:2], accum_out=lst[:, 2:3]), "A")
        p.wait(v, "A")
        p.inc(v.reciprocal(lst[:, 3:4], lst[:, 2:3]), "V")
        p.wait(v, "V")
        p.inc(v.tensor_scalar(out=sc_sb[:, t128 * NE:(t128 + 1) * NE],
                              in0=sc_sb[:, t128 * NE:(t128 + 1) * NE],
                              scalar1=lst[:, 3:4], scalar2=None,
                              op0=ALU.mult), "V")
    p.bar()

    cast_marks = [0, 0]
    hps_marks = [0, 0]
    ops_marks = [0, 0]
    w2mm_mark = 0
    accv_mark = 0
    bcev_mark = 0
    bcps_mark = 0
    for e in range(NE):
        p.wait(nc.sync, "V", bcev_mark)
        self_ = p.inc(nc.sync.dma_start(out=b2rowf[0:1, :],
                                        in_=b2[e, :][None, :]), "L3", 16)
        p.wait(v, "L3")
        p.inc(v.tensor_copy(out=b2row[0:1, :], in_=b2rowf[0:1, :]), "V")
        p.wait(t, "V")
        p.wait(t, "V", bcev_mark)
        for j in range(2):
            inst = t.matmul(ps_bc[:, j * 512:(j + 1) * 512], ones1[0:1, :],
                            b2row[0:1, j * 512:(j + 1) * 512],
                            start=True, stop=True)
        p.inc(inst, "P")
        p.wait(v, "P")
        p.inc(v.tensor_copy(out=b2b[:, :], in_=ps_bc[:, :]), "V")
        bcev_mark = p.cnt["V"]
        for f8 in range(8):
            pb = (e * 8 + f8) % 2
            lsem = "L" if pb == 0 else "L2"
            p.wait(nc.sync, "P", cast_marks[pb])
            p.ld(w1bf[:, pb * 4096:(pb + 1) * 4096]
                 .rearrange("p (c f) -> p c f", c=ND),
                 w1[e].rearrange("(c p) f -> p c f", p=128)
                 [:, :, f8 * 512:(f8 + 1) * 512], sem=lsem)
            p.ld(w2bf[:, pb * 4096:(pb + 1) * 4096]
                 .rearrange("p (c d) -> p c d", c=4),
                 w2[e].rearrange("(c p) d -> p c d", p=128)
                 [:, f8 * 4:(f8 + 1) * 4, :], sem=lsem)
            w1v = w1bf[:, pb * 4096:(pb + 1) * 4096]
            w2v = w2bf[:, pb * 4096:(pb + 1) * 4096]
            p.wait(t, lsem)
            p.wait(t, "P", w2mm_mark)
            for ft in range(4):
                ftg = f8 * 4 + ft
                for tch in range(4):
                    hb = (ft * 4 + tch) % 2
                    p.wait(t, "A", hps_marks[hb])
                    for dch in range(ND):
                        inst = t.matmul(
                            ps_h[hb][:, :],
                            w1v[:, dch * 512 + ft * 128:
                                dch * 512 + (ft + 1) * 128],
                            xn0[:, dch * T + tch * 512:
                                dch * T + (tch + 1) * 512],
                            start=(dch == 0), stop=(dch == ND - 1))
                    p.inc(inst, "P")
                    p.wait(s, "P")
                    p.inc(s.activation(
                        out=h_sb[:, ft * T + tch * 512:ft * T + (tch + 1) * 512],
                        in_=ps_h[hb][:, :], func=AF.Gelu,
                        bias=b1_sb[:, e * 32 + ftg:e * 32 + ftg + 1]), "A")
                    hps_marks[hb] = p.cnt["A"]
            hre_mark = p.cnt["A"]
            p.wait(v, "V", accv_mark)
            accv_base = p.cnt["V"]
            for t128 in range(NT):
                for dh in range(2):
                    ob = (t128 * 2 + dh) % 2
                    p.wait(t, "A", hre_mark)
                    p.wait(t, "V", ops_marks[ob])
                    for ft in range(4):
                        inst = t.matmul(
                            ps_o[ob][:, :],
                            h_sb[:, ft * T + t128 * 128:
                                 ft * T + (t128 + 1) * 128],
                            w2v[:, ft * D + dh * 512:ft * D + (dh + 1) * 512],
                            start=(ft == 0), stop=(ft == 3))
                    p.inc(inst, "P")
                    p.wait(v, "P")
                    dst = acc2[:, t128 * D + dh * 512:t128 * D + (dh + 1) * 512]
                    if f8 == 0:
                        inst = v.tensor_copy(out=dst, in_=ps_o[ob][:, :])
                    else:
                        inst = v.tensor_tensor(out=dst, in0=ps_o[ob][:, :],
                                               in1=dst, op=ALU.add)
                    p.inc(inst, "V")
                    ops_marks[ob] = p.cnt["V"]
            accv_mark = p.cnt["V"]
            w2mm_mark = p.cnt["P"]
            cast_marks[pb] = p.cnt["P"]
        p.wait(v, "V", accv_mark)
        for t128 in range(NT):
            a2 = acc2[:, t128 * D:(t128 + 1) * D]
            a0 = acc0[:, t128 * D:(t128 + 1) * D]
            scc = sc_sb[:, t128 * NE + e:t128 * NE + e + 1]
            p.inc(v.tensor_tensor(out=tmpc[:, :], in0=a2, in1=b2b[:, :],
                                  op=ALU.add), "V")
            p.wait(v, "V")
            if e == 0:
                inst = v.tensor_scalar(out=a0, in0=tmpc[:, :], scalar1=scc,
                                       scalar2=None, op0=ALU.mult)
            else:
                inst = v.scalar_tensor_tensor(out=a0, in0=tmpc[:, :],
                                              scalar=scc, in1=a0,
                                              op0=ALU.mult, op1=ALU.add)
            p.inc(inst, "V")
            p.wait(v, "V")
        accv_mark = p.cnt["V"]
    p.bar()

    xt = p.sbuf("c_x", [128, D], F32)
    st_ = p.sbuf("c_s", [128, D], F32)
    gt_ = p.sbuf("c_g", [128, D], F32)
    x1t = p.sbuf("c_x1", [128, D], F32)
    xnt = p.sbuf("c_xn", [128, D], F32)
    xb1 = p.sbuf("c_xb", [128, 3 * D], BF16)
    stat = p.sbuf("c_stat", [128, 16], F32)
    trashd = p.sbuf("c_trash", [128, D], BF16)
    ps_t = [p.psum(f"c_pst{i}", [128, D], BF16) for i in range(2)]
    iter_mark = 0
    evac_marks = [0, 0]
    for tt in range(NT):
        ssem = "S" if tt % 2 == 0 else "S2"
        p.wait(nc.sync, "V", iter_mark)
        p.ld(xt[:, :], x0[tt * 128:(tt + 1) * 128, :])
        p.ld(st_[:, :], s0_d[tt * 128:(tt + 1) * 128, :])
        ldm = p.cnt["L"]
        p.wait(s, "A")
        p.inc(s.activation(out=gt_[:, :], in_=acc0[:, tt * D:(tt + 1) * D],
                           func=AF.Silu), "A")
        p.wait(v, "A")
        p.wait(v, "L", ldm)
        p.inc(v.tensor_tensor(out=gt_[:, :], in0=gt_[:, :],
                              in1=st_[:, :], op=ALU.mult), "V")
        p.wait(v, "V")
        p.wait(v, "S")
        p.wait(v, "S2")
        p.inc(v.tensor_tensor(out=x1t[:, :], in0=xt[:, :], in1=gt_[:, :],
                              op=ALU.add), "V")
        p.wait(v, "V")
        x1v = x1t[:, :]
        p.wait(g, "V")
        p.st(x1_d[tt * 128:(tt + 1) * 128, :], x1v, sem=ssem)
        p.inc(v.reduce_sum(stat[:, 0:1], x1v, axis=AX), "V")
        p.wait(s, "V")
        p.inc(s.activation(out=trashd[:, :], in_=x1v, func=AF.Square,
                           accum_out=stat[:, 1:2]), "A")
        p.wait(v, "A")
        p.wait(v, "V")
        p.inc(v.tensor_scalar(out=stat[:, 3:4], in0=stat[:, 0:1],
                              scalar1=-1.0 / D, scalar2=None, op0=ALU.mult),
              "V")
        p.wait(v, "V")
        v.tensor_tensor(out=stat[:, 6:7], in0=stat[:, 3:4],
                        in1=stat[:, 3:4], op=ALU.mult)
        p.inc(v.tensor_scalar(out=stat[:, 4:5], in0=stat[:, 1:2],
                              scalar1=1.0 / D, scalar2=None, op0=ALU.mult),
              "V")
        p.wait(v, "V")
        p.inc(v.tensor_tensor(out=stat[:, 5:6], in0=stat[:, 4:5],
                              in1=stat[:, 6:7], op=ALU.subtract), "V")
        p.wait(s, "V")
        p.inc(s.activation(out=stat[:, 2:3], in_=stat[:, 5:6], func=AF.Ln,
                           bias=p.C(4)), "A")
        p.wait(s, "A")
        p.inc(s.activation(out=stat[:, 2:3], in_=stat[:, 2:3], func=AF.Exp,
                           scale=p.C(2)), "A")
        p.wait(v, "A")
        p.inc(v.tensor_scalar(out=xnt[:, :], in0=x1v,
                              scalar1=stat[:, 3:4], scalar2=stat[:, 2:3],
                              op0=ALU.add, op1=ALU.mult), "V")
        xnv = xnt[:, :]
        p.wait(v, "V")
        p.wait(g, "V")
        p.st(xn1_d[tt * 128:(tt + 1) * 128, :], xnv, sem=ssem)
        p.inc(v.reduce_sum(stat[:, 8:9], xnv, axis=AX), "V")
        p.wait(s, "V")
        p.inc(s.activation(out=trashd[:, :], in_=xnv, func=AF.Square,
                           accum_out=stat[:, 9:10]), "A")
        p.wait(v, "A")
        p.wait(v, "V")
        p.inc(v.tensor_scalar(out=stat[:, 10:11], in0=stat[:, 9:10],
                              scalar1=1.0 / D, scalar2=None, op0=ALU.mult),
              "V")
        p.wait(s, "V")
        p.inc(s.activation(out=stat[:, 11:12], in_=stat[:, 10:11], func=AF.Ln,
                           bias=p.C(5)), "A")
        p.wait(s, "A")
        p.inc(s.activation(out=stat[:, 11:12], in_=stat[:, 11:12],
                           func=AF.Exp, scale=p.C(2)), "A")
        p.wait(v, "V")
        p.inc(v.tensor_scalar(out=stat[:, 12:13], in0=stat[:, 8:9],
                              scalar1=-1.0 / D, scalar2=None, op0=ALU.mult),
              "V")
        p.wait(v, "V")
        v.tensor_tensor(out=stat[:, 13:14], in0=stat[:, 12:13],
                        in1=stat[:, 12:13], op=ALU.mult)
        p.inc(v.tensor_scalar(out=stat[:, 14:15], in0=stat[:, 9:10],
                              scalar1=1.0 / D, scalar2=None, op0=ALU.mult),
              "V")
        p.wait(v, "V")
        p.inc(v.tensor_tensor(out=stat[:, 14:15], in0=stat[:, 14:15],
                              in1=stat[:, 13:14], op=ALU.subtract), "V")
        p.wait(s, "V")
        p.inc(s.activation(out=stat[:, 15:16], in_=stat[:, 14:15], func=AF.Ln,
                           bias=p.C(5)), "A")
        p.wait(s, "A")
        p.inc(s.activation(out=stat[:, 15:16], in_=stat[:, 15:16],
                           func=AF.Exp, scale=p.C(2)), "A")
        p.wait(v, "A")
        p.inc(v.tensor_scalar(out=xb1[:, 0:D], in0=xnv,
                              scalar1=stat[:, 11:12], scalar2=None,
                              op0=ALU.mult), "V")
        p.inc(v.tensor_scalar(out=xb1[:, D:2 * D], in0=xnv,
                              scalar1=stat[:, 12:13], scalar2=stat[:, 15:16],
                              op0=ALU.add, op1=ALU.mult), "V")
        p.inc(v.tensor_copy(out=xb1[:, 2 * D:3 * D], in_=xnv), "V")
        xb_mark = p.cnt["V"]
        for j, dstd in enumerate((xn1sfm, xhatfm, xn1fm)):
            jb = j % 2
            p.wait(t, "V", xb_mark)
            p.wait(t, "V", evac_marks[jb])
            for dc in range(ND):
                inst = t.transpose(ps_t[jb][:, dc * 128:(dc + 1) * 128],
                                   xb1[:, j * D + dc * 128:
                                       j * D + (dc + 1) * 128],
                                   p.ident[:, :])
            p.inc(inst, "P")
            p.wait(v, "P")
            p.inc(v.tensor_copy(out=xb1[:, j * D:(j + 1) * D],
                                in_=ps_t[jb][:, :]), "V")
            evac_marks[jb] = p.cnt["V"]
            p.wait(g, "V")
            for dc in range(ND):
                p.st(dstd[dc * 128:(dc + 1) * 128, tt * 128:(tt + 1) * 128],
                     xb1[:, j * D + dc * 128:j * D + (dc + 1) * 128],
                     sem=ssem)
        iter_mark = p.cnt["V"]
    p.bar()
    p.wait(g, "S")
    p.wait(g, "S2")
    p.wait(g, "L")
def build_stage_f0():
    p = Prog("f0")
    io = {}
    io["xn0fm"] = p.dram("xn0fm", [D, T], BF16, "ExternalInput")
    io["wrr"] = p.dram("wrr", [128, ND * NE], F32, "ExternalInput")
    io["w1"] = p.dram("w1", [NE, D, FF], F32, "ExternalInput")
    io["b1r"] = p.dram("b1r", [128, NE * (FF // 128)], F32, "ExternalInput")
    io["w2"] = p.dram("w2", [NE, FF, D], F32, "ExternalInput")
    io["b2"] = p.dram("b2", [NE, D], F32, "ExternalInput")
    io["x0"] = p.dram("x0", [T, D], F32, "ExternalInput")
    io["s0"] = p.dram("s0", [T, D], F32, "ExternalInput")
    io["x1"] = p.dram("x1", [T, D], F32, "ExternalOutput")
    io["xn1"] = p.dram("xn1", [T, D], F32, "ExternalOutput")
    io["xn1sfm"] = p.dram("xn1sfm", [D, T], BF16, "ExternalOutput")
    io["xhatfm"] = p.dram("xhatfm", [D, T], BF16, "ExternalOutput")
    io["xn1fm"] = p.dram("xn1fm", [D, T], BF16, "ExternalOutput")
    p.emit_consts()
    p.push_scope()
    emit_stage_f0(p, io)
    p.pop_scope()
    return p.done()



S_UPTO = 4
S_DT_MODE = 2


def emit_stage_s(p, io):
    nc = p.nc
    v, s, t, g = nc.vector, nc.scalar, nc.tensor, nc.gpsimd
    xn1sfm = io["xn1sfm"]
    iprojT = io["iprojT"]
    cwr = io["cwr"]
    cbr = io["cbr"]
    xpT = io["xpT"]
    dtwT = io["dtwT"]
    dtbr = io["dtbr"]
    xc_d = io["xc"]
    dtm_d = io["dtm"]
    bm_d = io["bm"]
    sres_d = io["sres"]


    XW = T + 3
    xn1 = p.sbuf("xn1", [128, ND * T], BF16)
    xi_sb = p.sbuf("xi_sb", [128, NDI * XW], BF16)
    xc_sb = p.sbuf("xc_sb", [128, NDI * T], BF16)
    wcol = p.sbuf("wcol", [128, 2 * ND * 128], BF16)
    dtcol = p.sbuf("dtcol", [128, 2 * NDI * 128], BF16)
    small = p.sbuf("small", [128, NDI * KC + 3 * NDI], F32)
    xp_sb = p.sbuf("xp_sb", [128, NDI * NS], BF16)
    srt = p.sbuf("srt", [128, 2 * 512], BF16)
    cacc = p.sbuf("cacc", [128, 2 * T], F32)
    ut = p.sbuf("ut", [128, 2 * 512], F32)
    at = p.sbuf("at", [128, 2 * 512], F32)
    dtt = p.sbuf("dtt", [128, 2 * 512], F32)
    bt_o = p.sbuf("bt_o", [64, 2 * 512], BF16)
    ps_m = [p.psum(f"s_ps{i}", [128, 512], F32) for i in range(2)]

    cw_v = small[:, 0:NDI * KC]
    cb_v = small[:, NDI * KC:NDI * KC + NDI]
    dtb_v = small[:, NDI * KC + NDI:NDI * KC + 2 * NDI]

    for dc in range(ND):
        p.ld(xn1[:, dc * T:(dc + 1) * T], xn1sfm[dc * 128:(dc + 1) * 128, :])
    p.ld(cw_v, cwr[:, :])
    p.ld(cb_v, cbr[:, :])
    p.ld(dtb_v, dtbr[:, :])
    p.ld(xp_sb[:, :].rearrange("p (c n) -> p c n", c=NDI),
         xpT.rearrange("(c p) n -> p c n", p=128))
    for cc in range(NDI):
        inst = g.memset(xi_sb[:, cc * XW:cc * XW + 3], 0.0)
    p.inc(inst, "G")
    p.bar()

    ev_marks = [0, 0]
    ev_sems = [None, None]
    wrd_marks = [0, 0]
    for och in range(2 * DI // 128):
        pb = och % 2
        lsem = "L" if pb == 0 else "L2"
        p.wait(nc.sync, "P", wrd_marks[pb])
        p.ld(wcol[:, pb * 1024:(pb + 1) * 1024]
             .rearrange("p (c o) -> p c o", c=ND),
             iprojT.rearrange("(c p) o -> p c o", p=128)
             [:, :, och * 128:(och + 1) * 128], sem=lsem)
        p.wait(t, lsem)
        for tch in range(4):
            hb = tch % 2
            if ev_sems[hb] is not None:
                p.wait(t, ev_sems[hb], ev_marks[hb])
            for dch in range(ND):
                inst = t.matmul(
                    ps_m[hb][:, :],
                    wcol[:, pb * 1024 + dch * 128:pb * 1024 + (dch + 1) * 128],
                    xn1[:, dch * T + tch * 512:dch * T + (tch + 1) * 512],
                    start=(dch == 0), stop=(dch == ND - 1))
            p.inc(inst, "P")
            if tch == 3:
                wrd_marks[pb] = p.cnt["P"]
            if och < NDI:
                p.wait(v, "P")
                p.inc(v.tensor_copy(
                    out=xi_sb[:, och * XW + 3 + tch * 512:
                              och * XW + 3 + (tch + 1) * 512],
                    in_=ps_m[hb][:, :]), "V")
                ev_marks[hb] = p.cnt["V"]
                ev_sems[hb] = "V"
            else:
                oi = och - NDI
                sb = (och * 4 + tch) % 2
                p.wait(s, "P")
                p.wait(s, "S" if sb == 0 else "S2")
                p.inc(s.activation(out=srt[:, sb * 512:(sb + 1) * 512],
                                   in_=ps_m[hb][:, :], func=AF.Silu), "A")
                ev_marks[hb] = p.cnt["A"]
                ev_sems[hb] = "A"
                p.wait(g, "A")
                p.st(sres_d[oi * 128:(oi + 1) * 128,
                            tch * 512:(tch + 1) * 512],
                     srt[:, sb * 512:(sb + 1) * 512],
                     sem="S" if sb == 0 else "S2")
    p.bar()
    if S_UPTO < 2:
        p.wait(g, "S")
        p.wait(g, "S2")
    

    for och in range(NDI):
        cb_ = och % 2
        cv = cacc[:, cb_ * T:(cb_ + 1) * T]
        p.wait(v, "A")
        p.inc(v.tensor_scalar(out=cv, in0=xi_sb[:, och * XW:och * XW + T],
                              scalar1=cw_v[:, och * KC:och * KC + 1],
                              scalar2=cb_v[:, och:och + 1],
                              op0=ALU.mult, op1=ALU.add), "V")
        for j in range(1, KC):
            p.wait(v, "V")
            p.inc(v.scalar_tensor_tensor(
                out=cv, in0=xi_sb[:, och * XW + j:och * XW + j + T],
                scalar=cw_v[:, och * KC + j:och * KC + j + 1], in1=cv,
                op0=ALU.mult, op1=ALU.add), "V")
        p.wait(s, "V")
        p.inc(s.activation(out=xc_sb[:, och * T:(och + 1) * T], in_=cv,
                           func=AF.Silu), "A")
        p.wait(g, "A")
        p.st(xc_d[och * 128:(och + 1) * 128, :],
             xc_sb[:, och * T:(och + 1) * T],
             sem="S" if cb_ == 0 else "S2")
    p.bar()
    if S_UPTO < 3:
        p.wait(g, "S")
        p.wait(g, "S2")
    

    bev_marks = [0, 0]
    for tch in range(4):
        hb = tch % 2
        p.wait(t, "V", bev_marks[hb])
        for cch in range(NDI):
            inst = t.matmul(ps_m[hb][0:64, :],
                            xp_sb[:, cch * NS:(cch + 1) * NS],
                            xc_sb[:, cch * T + tch * 512:
                                  cch * T + (tch + 1) * 512],
                            start=(cch == 0), stop=(cch == NDI - 1))
        p.inc(inst, "P")
        p.wait(v, "P")
        p.wait(v, "S" if hb == 0 else "S2")
        p.inc(v.tensor_copy(out=bt_o[0:64, hb * 512:(hb + 1) * 512],
                            in_=ps_m[hb][0:64, :]), "V")
        bev_marks[hb] = p.cnt["V"]
        p.wait(g, "V")
        p.st(bm_d[:, tch * 512:(tch + 1) * 512],
             bt_o[0:64, hb * 512:(hb + 1) * 512],
             sem="S" if hb == 0 else "S2")
    p.bar()
    if S_UPTO < 4:
        p.wait(g, "S")
        p.wait(g, "S2")
    

    wrd_marks = [0, 0]
    spv_marks = [0, 0]
    spa_marks = [0, 0]
    for och in range(NDI):
        pb = och % 2
        lsem = "L" if pb == 0 else "L2"
        p.wait(nc.sync, "P", wrd_marks[pb])
        p.ld(dtcol[:, pb * T:(pb + 1) * T]
             .rearrange("p (c o) -> p c o", c=NDI),
             dtwT.rearrange("(c p) o -> p c o", p=128)
             [:, :, och * 128:(och + 1) * 128], sem=lsem)
        p.wait(t, lsem)
        for tch in range(4):
            hb = tch % 2
            p.wait(t, "V", spv_marks[hb])
            p.wait(t, "A", spa_marks[hb])
            for cch in range(NDI):
                inst = t.matmul(
                    ps_m[hb][:, :],
                    dtcol[:, pb * T + cch * 128:pb * T + (cch + 1) * 128],
                    xc_sb[:, cch * T + tch * 512:cch * T + (tch + 1) * 512],
                    start=(cch == 0), stop=(cch == NDI - 1))
            p.inc(inst, "P")
            if tch == 3:
                wrd_marks[pb] = p.cnt["P"]
            if S_DT_MODE == 0:
                p.wait(v, "P")
                p.wait(v, "V", spv_marks[hb])
                p.wait(v, "S" if hb == 0 else "S2")
                p.inc(v.tensor_copy(out=dtt[:, hb * 512:(hb + 1) * 512],
                                    in_=ps_m[hb][:, :]), "V")
                spv_marks[hb] = p.cnt["V"]
                spa_marks[hb] = 0
                p.wait(g, "V")
                p.st(dtm_d[och * 128:(och + 1) * 128,
                           tch * 512:(tch + 1) * 512],
                     dtt[:, hb * 512:(hb + 1) * 512],
                     sem="S" if hb == 0 else "S2")
                continue
            p.wait(v, "P")
            p.inc(v.tensor_scalar(out=ut[:, hb * 512:(hb + 1) * 512],
                                  in0=ps_m[hb][:, :],
                                  scalar1=dtb_v[:, och:och + 1], scalar2=0.0,
                                  op0=ALU.add, op1=ALU.max), "V")
            p.wait(s, "P")
            p.wait(s, "V")
            p.inc(s.activation(out=at[:, hb * 512:(hb + 1) * 512],
                               in_=ps_m[hb][:, :], func=AF.Abs,
                               bias=dtb_v[:, och:och + 1]), "A")
            spa_marks[hb] = p.cnt["A"]
            p.wait(s, "A")
            p.inc(s.activation(out=at[:, hb * 512:(hb + 1) * 512],
                               in_=at[:, hb * 512:(hb + 1) * 512],
                               func=AF.Exp, scale=p.C(0)), "A")
            p.wait(s, "A")
            p.inc(s.activation(out=at[:, hb * 512:(hb + 1) * 512],
                               in_=at[:, hb * 512:(hb + 1) * 512],
                               func=AF.Ln, bias=p.C(1)), "A")
            p.wait(v, "A")
            p.wait(v, "V")
            p.wait(v, "S" if hb == 0 else "S2")
            p.inc(v.tensor_tensor(out=dtt[:, hb * 512:(hb + 1) * 512],
                                  in0=ut[:, hb * 512:(hb + 1) * 512],
                                  in1=at[:, hb * 512:(hb + 1) * 512],
                                  op=ALU.add), "V")
            spv_marks[hb] = p.cnt["V"]
            p.wait(g, "V")
            p.st(dtm_d[och * 128:(och + 1) * 128, tch * 512:(tch + 1) * 512],
                 dtt[:, hb * 512:(hb + 1) * 512],
                 sem="S" if hb == 0 else "S2")
    p.bar()
    p.wait(g, "S")
    p.wait(g, "S2")
def build_stage_s():
    p = Prog("s")
    io = {}
    io["xn1sfm"] = p.dram("xn1sfm", [D, T], BF16, "ExternalInput")
    io["iprojT"] = p.dram("iprojT", [D, 2 * DI], BF16, "ExternalInput")
    io["cwr"] = p.dram("cwr", [128, NDI * KC], F32, "ExternalInput")
    io["cbr"] = p.dram("cbr", [128, NDI], F32, "ExternalInput")
    io["xpT"] = p.dram("xpT", [DI, NS], BF16, "ExternalInput")
    io["dtwT"] = p.dram("dtwT", [DI, DI], BF16, "ExternalInput")
    io["dtbr"] = p.dram("dtbr", [128, NDI], F32, "ExternalInput")
    io["xc"] = p.dram("xc", [DI, T], BF16, "ExternalOutput")
    io["dtm"] = p.dram("dtm", [DI, T], F32, "ExternalOutput")
    io["bm"] = p.dram("bm", [NS, T], BF16, "ExternalOutput")
    io["sres"] = p.dram("sres", [DI, T], BF16, "ExternalOutput")
    p.emit_consts()
    p.push_scope()
    emit_stage_s(p, io)
    p.pop_scope()
    return p.done()



def gen_c_scan(p, io):
    nc = p.nc
    p.prefix = "c1_"
    v, s, t, g = nc.vector, nc.scalar, nc.tensor, nc.gpsimd
    xc_d, dtm_d, bm_d = io["xc"], io["dtm"], io["bm"]
    asb_d, dpr, sres_d = io["asb"], io["dpr"], io["sres"]
    y2_d = io["y2"]

    TQ = 512
    NB = 8
    NBK = NS // NB
    asb = p.sbuf("asb_s", [128, NDI * NS], F32)
    dp_sb = p.sbuf("dp_sb", [128, NDI], F32)
    ones1 = p.sbuf("ones1", [1, 128], BF16)
    bt_sb = p.sbuf("bt_sb", [64, TQ], BF16)
    brow = p.sbuf("brow", [1, 2 * TQ], BF16)
    bbc = p.sbuf("bbc", [128, NB * TQ], BF16)
    dt_t = p.sbuf("dt_t", [128, 2 * TQ], F32)
    xc_t = p.sbuf("xc_t", [128, 2 * TQ], BF16)
    sr_t = p.sbuf("sr_t", [128, 2 * TQ], BF16)
    ad_t = p.sbuf("ad_t", [128, 4 * TQ], F32)
    bx_t = p.sbuf("bx_t", [128, 4 * TQ], BF16)
    h_t = p.sbuf("h_t", [128, 4 * TQ], BF16)
    carry = p.sbuf("carry", [128, NDI * NS], BF16)
    TQP = 512
    y_acc = p.sbuf("y_acc", [128, NDI * TQ], BF16)
    y2_t = p.sbuf("y2_t", [128, TQ], BF16)
    ps_y = p.psum("ps_y", [128, TQ], F32)
    ps_c = p.psum("ps_c", [128, NS], F32)
    ps_b = p.psum("ps_b", [128, TQ], F32)

    p.ld(asb[:, :], asb_d[:, :], sem="L5")
    p.ld(dp_sb[:, :], dpr[:, :], sem="L5")
    p.wait(v, "L5")
    p.inc(v.memset(ones1[0:1, :], 1.0), "V")
    prol_mark = p.cnt["V"]
    admk = [0, 0, 0, 0]
    hmk = [0, 0, 0, 0]
    bxmk = [0, 0, 0, 0]
    ldg_v = [0, 0]
    ldg_g = [0, 0]
    ldg_a = [0, 0]
    yield
    for tq in range(T // TQ):
        p.wait(nc.sync, "V", prol_mark)
        p.wait(nc.sync, "L5")
        p.ld(bt_sb[0:64, :], bm_d[:, tq * TQ:(tq + 1) * TQ], sem="L5")
        for nb_ in range(NBK):
            bev = 0
            first_ev = True
            for ng in range(NB // 2):
                p.wait(nc.sync, "L5")
                p.wait(nc.sync, "P")
                p.inc(nc.sync.dma_start(
                    out=brow[0:1, :],
                    in_=bt_sb[nb_ * NB + ng * 2:nb_ * NB + (ng + 1) * 2, :]),
                    "L5", 16)
                p.wait(t, "L5")
                for nl in range(2):
                    n_ = ng * 2 + nl
                    p.wait(t, "V", bev)
                    inst = t.matmul(ps_b[:, :], ones1[0:1, :],
                                    brow[0:1, nl * TQ:(nl + 1) * TQ],
                                    start=True, stop=True)
                    p.inc(inst, "P")
                    p.wait(v, "P")
                    if first_ev:
                        p.wait(v, "G")
                        first_ev = False
                    p.inc(v.tensor_copy(out=bbc[:, n_ * TQ:(n_ + 1) * TQ],
                                        in_=ps_b[:, :]), "V")
                    bev = p.cnt["V"]
            bev_all = p.cnt["V"]
            yield
            for dch in range(NDI):
                db_ = dch % 2
                lsem = "L3" if db_ == 0 else "L4"
                p.wait(nc.sync, "V", ldg_v[db_])
                p.wait(nc.sync, "G", ldg_g[db_])
                p.wait(nc.sync, "A", ldg_a[db_])
                p.ld(dt_t[:, db_ * TQ:(db_ + 1) * TQ],
                     dtm_d[dch * 128:(dch + 1) * 128,
                           tq * TQ:(tq + 1) * TQ], sem=lsem)
                p.ld(xc_t[:, db_ * TQ:(db_ + 1) * TQ],
                     xc_d[dch * 128:(dch + 1) * 128,
                          tq * TQ:(tq + 1) * TQ], sem=lsem)
                if nb_ == NBK - 1:
                    p.ld(sr_t[:, db_ * TQ:(db_ + 1) * TQ],
                         sres_d[dch * 128:(dch + 1) * 128,
                                tq * TQ:(tq + 1) * TQ], sem=lsem)
                p.wait(s, lsem)
                p.wait(g, lsem)
                p.wait(v, lsem)
                dtv = dt_t[:, db_ * TQ:(db_ + 1) * TQ]
                xcv = xc_t[:, db_ * TQ:(db_ + 1) * TQ]
                srv = sr_t[:, db_ * TQ:(db_ + 1) * TQ]
                for n_ in range(NB):
                    n = nb_ * NB + n_
                    nbuf = n_ % 4
                    adv = ad_t[:, nbuf * TQ:(nbuf + 1) * TQ]
                    bxv = bx_t[:, nbuf * TQ:(nbuf + 1) * TQ]
                    hv = h_t[:, nbuf * TQ:(nbuf + 1) * TQ]
                    col = dch * NS + n
                    p.wait(s, "V", admk[nbuf])
                    p.inc(s.activation(out=adv, in_=dtv, func=AF.Exp,
                                       scale=asb[:, col:col + 1]), "A")
                    use_pool = True
                    eng = g
                    esem = "G"
                    p.wait(eng, "V", bev_all)
                    p.wait(eng, "V", bxmk[nbuf])
                    p.inc(eng.tensor_tensor(
                        out=bxv, in0=xcv,
                        in1=bbc[:, n_ * TQ:(n_ + 1) * TQ], op=ALU.mult), esem)
                    bx_mark = p.cnt[esem]
                    p.wait(v, "A")
                    p.wait(v, esem, bx_mark)
                    p.wait(v, "P", hmk[nbuf])
                    init = 0.0 if tq == 0 else carry[:, col:col + 1]
                    p.inc(v.tensor_tensor_scan(out=hv, data0=adv, data1=bxv,
                                               initial=init, op0=ALU.mult,
                                               op1=ALU.add), "V")
                    admk[nbuf] = p.cnt["V"]
                    bxmk[nbuf] = p.cnt["V"]
                    p.wait(t, "V")
                    inst = t.matmul(ps_y[:, :], p.ident[:, :], hv,
                                    start=(n_ == 0), stop=(n_ == NB - 1))
                    if tq < (T // TQ) - 1:
                        inst = t.matmul(ps_c[:, n:n + 1], p.ident[:, :],
                                        hv[:, TQ - 1:TQ], start=(n_ == 0),
                                        stop=(n_ == NB - 1))
                    p.inc(inst, "P")
                    hmk[nbuf] = p.cnt["P"]
                p.wait(v, "P")
                yv = y_acc[:, dch * TQP:(dch + 1) * TQP]
                if nb_ == 0:
                    p.inc(v.tensor_copy(out=yv, in_=ps_y[:, :]), "V")
                else:
                    p.inc(v.tensor_tensor(out=yv, in0=ps_y[:, :],
                                          in1=yv, op=ALU.add), "V")
                if tq < (T // TQ) - 1:
                    p.wait(v, "V")
                    p.inc(v.tensor_copy(
                        out=carry[:, dch * NS + nb_ * NB:
                                  dch * NS + (nb_ + 1) * NB],
                        in_=ps_c[:, nb_ * NB:(nb_ + 1) * NB]), "V")
                if nb_ == NBK - 1:
                    p.wait(v, "V")
                    p.inc(v.scalar_tensor_tensor(
                        out=yv, in0=xcv,
                        scalar=dp_sb[:, dch:dch + 1], in1=yv,
                        op0=ALU.mult, op1=ALU.add), "V")
                    p.wait(v, "V")
                    ssem = "S" if db_ == 0 else "S2"
                    p.wait(v, "S")
                    p.wait(v, "S2")
                    p.inc(v.tensor_tensor(out=y2_t[:, :], in0=yv,
                                          in1=srv, op=ALU.mult), "V")
                    p.wait(g, "V")
                    p.st(y2_d[dch * 128:(dch + 1) * 128,
                              tq * TQ:(tq + 1) * TQ], y2_t[:, :], sem=ssem)
                ldg_v[db_] = p.cnt["V"]
                ldg_g[db_] = p.cnt["G"]
                ldg_a[db_] = p.cnt["A"]
                yield
    p.bar()


def emit_c_outproj(p, io):
    nc = p.nc
    p.prefix = "c2_"
    v, s, t, g = nc.vector, nc.scalar, nc.tensor, nc.gpsimd
    opT, xn1_d, s1_d, y2_d = io["opT"], io["xn1"], io["s1"], io["y2"]
    opt_sb = p.sbuf("opt_sb", [128, NDI * D], BF16)
    y2l = p.sbuf("y2l", [128, 2 * NDI * 128], BF16)
    xnt = p.sbuf("xnt", [128, 2 * D], F32)
    s1t = p.sbuf("s1t", [128, 2 * D], F32)
    ps_b2 = [p.psum(f"ps_op{i}", [128, 512], F32) for i in range(2)]
    p.ld(opt_sb[:, :].rearrange("p (c o) -> p c o", c=NDI),
         opT.rearrange("(c p) o -> p c o", p=128))
    p.bar()
    opev = [0, 0]
    ldg = [0, 0]
    for t128 in range(NT):
        pb = t128 % 2
        lsem = "L" if pb == 0 else "L2"
        ssem = "S" if pb == 0 else "S2"
        p.wait(nc.sync, "V", ldg[pb])
        for cch in range(NDI):
            p.ld(y2l[:, pb * DI + cch * 128:pb * DI + (cch + 1) * 128],
                 y2_d[cch * 128:(cch + 1) * 128,
                      t128 * 128:(t128 + 1) * 128], sem=lsem)
        p.ld(xnt[:, pb * D:(pb + 1) * D],
             xn1_d[t128 * 128:(t128 + 1) * 128, :], sem=lsem)
        p.wait(t, lsem)
        for oh in range(2):
            hb = (t128 * 2 + oh) % 2
            p.wait(t, "V", opev[hb])
            for cch in range(NDI):
                inst = t.matmul(
                    ps_b2[hb][:, :],
                    y2l[:, pb * DI + cch * 128:pb * DI + (cch + 1) * 128],
                    opt_sb[:, cch * D + oh * 512:cch * D + (oh + 1) * 512],
                    start=(cch == 0), stop=(cch == NDI - 1))
            p.inc(inst, "P")
            p.wait(v, "P")
            p.wait(v, ssem)
            p.inc(v.tensor_tensor(
                out=s1t[:, pb * D + oh * 512:pb * D + (oh + 1) * 512],
                in0=ps_b2[hb][:, :],
                in1=xnt[:, pb * D + oh * 512:pb * D + (oh + 1) * 512],
                op=ALU.add), "V")
            opev[hb] = p.cnt["V"]
            p.wait(g, "V")
            p.st(s1_d[t128 * 128:(t128 + 1) * 128,
                      oh * 512:(oh + 1) * 512],
                 s1t[:, pb * D + oh * 512:pb * D + (oh + 1) * 512],
                 sem=ssem)
        ldg[pb] = p.cnt["V"]
    p.bar()


def emit_f1_prologue(p, io, bufs):
    nc = p.nc
    p.prefix = "f1_"
    v, s, t, g = nc.vector, nc.scalar, nc.tensor, nc.gpsimd
    xn1fm, wgr = io["xn1fm"], io["wgr"]
    gr, br, b1r = io["gr"], io["br"], io["b1r"]
    normed = bufs["normed"] = p.sbuf("normed", [128, ND * T], BF16)
    acc1 = bufs["acc1"]
    bufs["acc2"] = p.sbuf("acc2", [128, NT * D], BF16)
    bufs["stg"] = bufs["tmpc"]
    bufs["w1bf"] = p.sbuf("w1bf", [128, 2 * 4096], BF16)
    bufs["w2bf"] = p.sbuf("w2bf", [128, 4096], BF16)
    bufs["h_sb"] = p.sbuf("h_sb", [128, 4 * T], BF16)
    wg_sb = bufs["wg_sb"] = p.sbuf("wg_sb", [128, ND * NE], BF16)
    gb_sb = bufs["gb_sb"] = p.sbuf("gb_sb", [128, 2 * NE * ND], F32)
    bufs["b1_sb"] = p.sbuf("b1_sb", [128, NE * (FF // 128)], F32)
    bufs["b2row"] = p.sbuf("b2row", [1, D], BF16)
    bufs["b2b"] = p.sbuf("b2b", [128, D], BF16)
    bufs["ones1"] = p.sbuf("ones1f", [1, 128], BF16)
    logit_sb = bufs["logit_sb"] = p.sbuf("logit_sb", [128, NT * NE], F32)
    cw_sb = bufs["cw_sb"]
    sw_sb = bufs["sw_sb"]
    lst = bufs["lst"] = p.sbuf("lst", [128, 8], F32)
    sp8 = bufs["sp8"] = p.sbuf("sp8", [128, NE], F32)
    ps_h = bufs["ps_h"] = [p.psum(f"ps_h{i}", [128, 512], F32)
                           for i in range(2)]
    bufs["ps_o"] = [p.psum(f"ps_o{i}", [128, 512], F32) for i in range(2)]
    bufs["ps_bc"] = p.psum("ps_bc", [128, 512], F32)

    for dc in range(ND):
        p.ld(normed[:, dc * T:(dc + 1) * T],
             xn1fm[dc * 128:(dc + 1) * 128, :])
    p.ld(bufs["stg"][:, 0:ND * NE], wgr[:, :])
    p.ld(bufs["b1_sb"][:, :], b1r[:, :])
    p.ld(gb_sb[:, 0:NE * ND], gr[:, :])
    p.ld(gb_sb[:, NE * ND:2 * NE * ND], br[:, :])
    v.memset(bufs["ones1"][0:1, :], 1.0)
    p.wait(v, "L")
    p.inc(v.tensor_copy(out=wg_sb[:, :], in_=bufs["stg"][:, 0:ND * NE]),
          "V")
    p.bar()
    rev_marks = [0, 0]
    for t128 in range(NT):
        pb = t128 % 2
        p.wait(t, "V", rev_marks[pb])
        for dch in range(ND):
            inst = t.matmul(ps_h[pb][:, 0:NE],
                            normed[:, dch * T + t128 * 128:
                                   dch * T + (t128 + 1) * 128],
                            wg_sb[:, dch * NE:(dch + 1) * NE],
                            start=(dch == 0), stop=(dch == ND - 1))
        p.inc(inst, "P")
        p.wait(v, "P")
        p.inc(v.tensor_copy(out=logit_sb[:, t128 * NE:(t128 + 1) * NE],
                            in_=ps_h[pb][:, 0:NE]), "V")
        rev_marks[pb] = p.cnt["V"]
    for t128 in range(NT):
        lg = logit_sb[:, t128 * NE:(t128 + 1) * NE]
        cwv = cw_sb[:, t128 * NE:(t128 + 1) * NE]
        p.wait(v, "V")
        p.inc(v.reduce_max(lst[:, 0:1], lg, axis=AX), "V")
        p.wait(v, "V")
        p.inc(v.tensor_scalar(out=cwv, in0=lg, scalar1=lst[:, 0:1],
                              scalar2=None, op0=ALU.is_equal), "V")
        p.wait(v, "V")
        p.inc(v.scalar_tensor_tensor(out=sp8[:, :], in0=cwv, scalar=-1e30,
                                     in1=lg, op0=ALU.mult, op1=ALU.add), "V")
        p.wait(v, "V")
        p.inc(v.reduce_max(lst[:, 1:2], sp8[:, :], axis=AX), "V")
        p.wait(v, "V")
        p.inc(v.tensor_tensor(out=lst[:, 2:3], in0=lst[:, 0:1],
                              in1=lst[:, 1:2], op=ALU.subtract), "V")
        p.wait(s, "V")
        s.activation(out=lst[:, 3:4], in_=lst[:, 2:3], func=AF.Sigmoid)
        p.inc(s.activation(out=lst[:, 4:5], in_=lst[:, 2:3], func=AF.Sigmoid,
                           scale=p.C(0)), "A")
        p.wait(v, "A")
        p.inc(v.tensor_scalar(out=sp8[:, :], in0=sp8[:, :],
                              scalar1=lst[:, 1:2], scalar2=None,
                              op0=ALU.is_equal), "V")
        p.wait(v, "V")
        p.inc(v.tensor_scalar(out=cwv, in0=cwv, scalar1=lst[:, 3:4],
                              scalar2=None, op0=ALU.mult), "V")
        p.wait(v, "V")
        p.inc(v.scalar_tensor_tensor(out=cwv, in0=sp8[:, :],
                                     scalar=lst[:, 4:5], in1=cwv,
                                     op0=ALU.mult, op1=ALU.add), "V")
        p.inc(v.tensor_tensor(out=sw_sb[:, t128:t128 + 1], in0=lst[:, 3:4],
                              in1=lst[:, 4:5], op=ALU.add), "V")
        p.wait(v, "V")
    p.bar()
    return bufs


def gen_f1_experts(p, io, bufs):
    nc = p.nc
    v, s, t, g = nc.vector, nc.scalar, nc.tensor, nc.gpsimd
    xhatfm, w1, w2, b2 = io["xhatfm"], io["w1"], io["w2"], io["b2"]
    normed, acc1, acc2 = bufs["normed"], bufs["acc1"], bufs["acc2"]
    stg, w1bf, w2bf = bufs["stg"], bufs["w1bf"], bufs["w2bf"]
    h_sb = bufs["h_sb"]
    gb_sb, b1_sb = bufs["gb_sb"], bufs["b1_sb"]
    b2row, b2b = bufs["b2row"], bufs["b2b"]
    ones1, cw_sb, tmpc = bufs["ones1"], bufs["cw_sb"], bufs["tmpc"]
    ps_h, ps_o, ps_bc = bufs["ps_h"], bufs["ps_o"], bufs["ps_bc"]
    cast_marks = [0, 0]
    hps_marks = [0, 0]
    ops_marks = [0, 0]
    w2mm_mark = 0
    accv_mark = 0
    bcev_mark = 0
    nrm_mark = 0
    yield
    for e in range(NE):
        p.wait(t, "V")
        p.wait(nc.sync, "P", w2mm_mark)
        for dc in range(ND):
            p.wait(nc.sync, "V", nrm_mark)
            p.ld(normed[:, dc * T:(dc + 1) * T],
                 xhatfm[dc * 128:(dc + 1) * 128, :], sem="L")
            p.wait(v, "L")
            p.wait(v, "P", w2mm_mark)
            p.inc(v.tensor_scalar(
                out=normed[:, dc * T:(dc + 1) * T],
                in0=normed[:, dc * T:(dc + 1) * T],
                scalar1=gb_sb[:, e * ND + dc:e * ND + dc + 1],
                scalar2=gb_sb[:, NE * ND + e * ND + dc:
                              NE * ND + e * ND + dc + 1],
                op0=ALU.mult, op1=ALU.add), "V")
            nrm_mark = p.cnt["V"]
        p.wait(g, "V", bcev_mark)
        p.wait(g, "V", nrm_mark)
        p.inc(g.dma_start(out=b2row[0:1, :], in_=b2[e, :][None, :]),
              "G2", 16)
        for j in range(2):
            p.wait(t, "G2")
            p.wait(t, "V", bcev_mark)
            inst = t.matmul(ps_bc[:, :], ones1[0:1, :],
                            b2row[0:1, j * 512:(j + 1) * 512],
                            start=True, stop=True)
            p.inc(inst, "P")
            p.wait(v, "P")
            p.inc(v.tensor_copy(out=b2b[:, j * 512:(j + 1) * 512],
                                in_=ps_bc[:, :]), "V")
            bcev_mark = p.cnt["V"]
        yield
        for f8 in range(8):
            pb = (e * 8 + f8) % 2
            lsem = "L" if pb == 0 else "L2"
            wv1 = w1bf[:, pb * 4096:(pb + 1) * 4096]
            wv2 = w2bf[:, :]
            p.wait(nc.sync, "P", cast_marks[pb])
            if f8 == 0:
                p.wait(nc.sync, "V", nrm_mark)
            else:
                p.wait(nc.sync, "P", w2mm_mark)
            p.ld(wv1[:, :].rearrange("p (c f) -> p c f", c=ND),
                 w1[e].rearrange("(c p) f -> p c f", p=128)
                 [:, :, f8 * 512:(f8 + 1) * 512], sem=lsem)
            p.ld(wv2[:, :].rearrange("p (c d) -> p c d", c=4),
                 w2[e].rearrange("(c p) d -> p c d", p=128)
                 [:, f8 * 4:(f8 + 1) * 4, :], sem=lsem)
            p.wait(t, lsem)
            p.wait(t, "P", w2mm_mark)
            p.wait(t, "V", nrm_mark)
            for ft in range(4):
                ftg = f8 * 4 + ft
                for tch in range(4):
                    hb = (ft * 4 + tch) % 2
                    p.wait(t, "A", hps_marks[hb])
                    for dch in range(ND):
                        inst = t.matmul(
                            ps_h[hb][:, :],
                            wv1[:, dch * 512 + ft * 128:
                                dch * 512 + (ft + 1) * 128],
                            normed[:, dch * T + tch * 512:
                                   dch * T + (tch + 1) * 512],
                            start=(dch == 0), stop=(dch == ND - 1))
                    p.inc(inst, "P")
                    p.wait(s, "P")
                    p.inc(s.activation(
                        out=h_sb[:, ft * T + tch * 512:
                                 ft * T + (tch + 1) * 512],
                        in_=ps_h[hb][:, :], func=AF.Relu,
                        bias=b1_sb[:, e * 32 + ftg:e * 32 + ftg + 1]), "A")
                    hps_marks[hb] = p.cnt["A"]
                    if tch == 3:
                        yield
            hre_mark = p.cnt["A"]
            for t128 in range(NT):
                if t128 % 4 == 0:
                    yield
                for dh in range(2):
                    ob = (t128 * 2 + dh) % 2
                    p.wait(t, "A", hre_mark)
                    p.wait(t, "A", ops_marks[ob])
                    dst = acc2[:, t128 * D + dh * 512:
                               t128 * D + (dh + 1) * 512]
                    first = True
                    if f8 > 0:
                        inst = t.matmul(ps_o[ob][:, :], p.ident[:, :], dst,
                                        start=True, stop=False)
                        first = False
                    for ft in range(4):
                        inst = t.matmul(
                            ps_o[ob][:, :],
                            h_sb[:, ft * T + t128 * 128:
                                 ft * T + (t128 + 1) * 128],
                            wv2[:, ft * D + dh * 512:ft * D + (dh + 1) * 512],
                            start=(first and ft == 0), stop=(ft == 3))
                    p.inc(inst, "P")
                    p.wait(s, "P")
                    p.inc(s.activation(out=dst, in_=ps_o[ob][:, :],
                                       func=AF.Copy), "A")
                    ops_marks[ob] = p.cnt["A"]
            accv_mark = p.cnt["V"]
            w2mm_mark = p.cnt["P"]
            cast_marks[pb] = p.cnt["P"]
            yield
        p.wait(v, "V", accv_mark)
        p.wait(v, "A")
        for t128 in range(NT):
            a2 = acc2[:, t128 * D:(t128 + 1) * D]
            a0 = acc1[:, t128 * D:(t128 + 1) * D]
            scc = cw_sb[:, t128 * NE + e:t128 * NE + e + 1]
            p.inc(v.tensor_tensor(out=tmpc[:, :], in0=a2, in1=b2b[:, :],
                                  op=ALU.add), "V")
            p.wait(v, "V")
            if e == 0:
                inst = v.tensor_scalar(out=a0, in0=tmpc[:, :], scalar1=scc,
                                       scalar2=None, op0=ALU.mult)
            else:
                inst = v.scalar_tensor_tensor(out=a0, in0=tmpc[:, :],
                                              scalar=scc, in1=a0,
                                              op0=ALU.mult, op1=ALU.add)
            p.inc(inst, "V")
            p.wait(v, "V")
        yield


def emit_f1_final(p, io, bufs):
    nc = p.nc
    p.prefix = "f2_"
    v, s, t, g = nc.vector, nc.scalar, nc.tensor, nc.gpsimd
    xn1_d, x1_d, s1_d, out_d = io["xn1"], io["x1"], io["s1"], io["out"]
    acc1, sw_sb, tmpc = bufs["acc1"], bufs["sw_sb"], bufs["tmpc"]
    xnt = p.sbuf("f_xn", [128, D], F32)
    x1t = p.sbuf("f_x1", [128, D], F32)
    s1t = p.sbuf("f_s1", [128, D], F32)
    gt_ = p.sbuf("f_g", [128, D], F32)
    ot = p.sbuf("f_o", [128, D], F32)
    yt = tmpc
    iter_mark = 0
    for tt in range(NT):
        ssem = "S" if tt % 2 == 0 else "S2"
        p.wait(nc.sync, "V", iter_mark)
        p.ld(xnt[:, :], xn1_d[tt * 128:(tt + 1) * 128, :])
        p.ld(x1t[:, :], x1_d[tt * 128:(tt + 1) * 128, :])
        p.ld(s1t[:, :], s1_d[tt * 128:(tt + 1) * 128, :])
        ldm = p.cnt["L"]
        p.wait(v, "L", ldm)
        p.wait(v, "A")
        p.inc(v.scalar_tensor_tensor(out=yt[:, :], in0=xnt[:, :],
                                     scalar=sw_sb[:, tt:tt + 1],
                                     in1=acc1[:, tt * D:(tt + 1) * D],
                                     op0=ALU.mult, op1=ALU.add), "V")
        p.wait(s, "V")
        p.inc(s.activation(out=gt_[:, :], in_=yt[:, :], func=AF.Silu), "A")
        p.wait(v, "A")
        p.inc(v.tensor_tensor(out=gt_[:, :], in0=gt_[:, :], in1=s1t[:, :],
                              op=ALU.mult), "V")
        p.wait(v, "V")
        p.wait(v, "S")
        p.wait(v, "S2")
        p.inc(v.tensor_tensor(out=ot[:, :], in0=x1t[:, :], in1=gt_[:, :],
                              op=ALU.add), "V")
        iter_mark = p.cnt["V"]
        p.wait(g, "V")
        p.st(out_d[tt * 128:(tt + 1) * 128, :], ot[:, :], sem=ssem)
    p.bar()
    p.wait(g, "S")
    p.wait(g, "S2")


def emit_stage_cf1(p, ioc, iof1):
    nc = p.nc
    g = nc.gpsimd
    p.prefix = "f1p_"
    bufs = dict(
        acc1=p.sbuf("acc1", [128, NT * D], BF16),
        cw_sb=p.sbuf("cw_sb", [128, NT * NE], F32),
        sw_sb=p.sbuf("sw_sb", [128, NT], F32),
        tmpc=p.sbuf("tmpc", [128, D], F32),
    )
    p.push_scope()
    emit_f1_prologue(p, iof1, bufs)
    gc = gen_c_scan(p, ioc)
    gf = gen_f1_experts(p, iof1, bufs)
    next(gc)
    next(gf)
    INTERLEAVE = globals().get("CF1_INTERLEAVE", True)
    done_c = done_f = False
    while not (done_c and done_f):
        if not done_c:
            try:
                next(gc)
            except StopIteration:
                done_c = True
        if not done_f and (INTERLEAVE or done_c):
            try:
                next(gf)
            except StopIteration:
                done_f = True
    for sm in ("S", "S2", "L", "L2", "L3", "L4", "L5"):
        p.wait(g, sm)
    p.bar()
    p.pop_scope()
    emit_c_outproj(p, ioc)
    emit_f1_final(p, iof1, bufs)


def build_fused():
    p = Prog("fused")
    EI, EO = "ExternalInput", "ExternalOutput"
    d = p.dram
    iop = dict(
        x0=d("x0", [T, D], F32, EI), wq=d("wq", [D, D], F32, EI),
        wk=d("wk", [D, D], F32, EI), wv=d("wv", [D, D], F32, EI),
        wo=d("wo", [D, D], F32, EI), ipw=d("ipw", [2 * DI, D], F32, EI),
        nw=d("nw", [D], F32, EI), dtw=d("dtw", [DI, DI], F32, EI),
        xpw2=d("xpw2", [NS, DI], F32, EI), opw=d("opw", [D, DI], F32, EI),
        alog=d("alog", [DI, NS], F32, EI),
        wqT=d("wqT", [D, D], BF16), wkT=d("wkT", [D, D], BF16),
        wvT=d("wvT", [D, D], BF16), woT=d("woT", [D, D], BF16),
        iprojT=d("iprojT", [D, 2 * DI], BF16), dtwT=d("dtwT", [DI, DI], BF16),
        xpT=d("xpT", [DI, NS], BF16), opT=d("opT", [DI, D], BF16),
        asb=d("asb", [128, NDI * NS], F32), xn0fm=d("xn0fm", [D, T], BF16),
        w1_0=d("w1_0", [NE, D, FF], F32, EI),
        w2_0=d("w2_0", [NE, FF, D], F32, EI),
        w1_1=d("w1_1", [NE, D, FF], F32, EI),
        w2_1=d("w2_1", [NE, FF, D], F32, EI),
        w1b_0=d("w1b_0", [NE * D * FF], BF16),
        w2b_0=d("w2b_0", [NE * FF * D], BF16),
        w1b_1=d("w1b_1", [NE * D * FF], BF16),
        w2b_1=d("w2b_1", [NE * FF * D], BF16),
    )
    ioa = dict(
        xn0fm=iop["xn0fm"], wqT=iop["wqT"], wkT=iop["wkT"], wvT=iop["wvT"],
        woT=iop["woT"], bq=d("bq", [D], F32, EI), bk=d("bk", [D], F32, EI),
        bv=d("bv", [D], F32, EI), bo=d("bo", [D], F32, EI),
        scale128=d("scale128", [128], F32, EI), s0=d("s0", [T, D], F32),
    )
    iof0 = dict(
        xn0fm=iop["xn0fm"], wrr=d("wrr", [128, ND * NE], F32, EI),
        w1=iop["w1b_0"].rearrange("(e d f) -> e d f", e=NE, d=D),
        b1r=d("b1r_0", [128, NE * (FF // 128)], F32, EI),
        w2=iop["w2b_0"].rearrange("(e f dd) -> e f dd", e=NE, f=FF),
        b2=d("b2_0", [NE, D], F32, EI),
        x0=iop["x0"], s0=ioa["s0"], x1=d("x1", [T, D], F32),
        xn1=d("xn1", [T, D], F32), xn1sfm=d("xn1sfm", [D, T], BF16),
        xhatfm=d("xhatfm", [D, T], BF16), xn1fm=d("xn1fm", [D, T], BF16),
    )
    ios = dict(
        xn1sfm=iof0["xn1sfm"], iprojT=iop["iprojT"],
        cwr=d("cwr", [128, NDI * KC], F32, EI),
        cbr=d("cbr", [128, NDI], F32, EI), xpT=iop["xpT"],
        dtwT=iop["dtwT"], dtbr=d("dtbr", [128, NDI], F32, EI),
        xc=d("xc", [DI, T], BF16), dtm=d("dtm", [DI, T], F32),
        bm=d("bm", [NS, T], BF16), sres=d("sres", [DI, T], BF16),
    )
    ioc = dict(
        xc=ios["xc"], dtm=ios["dtm"], bm=ios["bm"], asb=iop["asb"],
        dpr=d("dpr", [128, NDI], F32, EI), sres=ios["sres"], opT=iop["opT"],
        xn1=iof0["xn1"], s1=d("s1", [T, D], F32),
        y2=d("y2", [DI, T], BF16),
    )
    iof1 = dict(
        xn1fm=iof0["xn1fm"], xhatfm=iof0["xhatfm"],
        wgr=d("wgr", [128, ND * NE], F32, EI),
        gr=d("gr", [128, NE * ND], F32, EI),
        br=d("br", [128, NE * ND], F32, EI),
        w1=iop["w1b_1"].rearrange("(e d f) -> e d f", e=NE, d=D),
        b1r=d("b1r_1", [128, NE * (FF // 128)], F32, EI),
        w2=iop["w2b_1"].rearrange("(e f dd) -> e f dd", e=NE, f=FF),
        b2=d("b2_1", [NE, D], F32, EI),
        xn1=iof0["xn1"], x1=iof0["x1"], s1=ioc["s1"],
        out=d("out", [T, D], F32, EO),
    )
    p.emit_consts()
    g = p.nc.gpsimd
    for emit, io in ((emit_stage_p, iop), (emit_stage_a, ioa),
                     (emit_stage_f0, iof0), (emit_stage_s, ios)):
        p.push_scope()
        p.prefix = emit.__name__.replace("emit_stage_", "") + "_"
        emit(p, io)
        p.pop_scope()
        for sm in ("S", "S2", "L", "L2"):
            p.wait(g, sm)
        p.bar()
    p.push_scope()
    emit_stage_cf1(p, ioc, iof1)
    p.pop_scope()
    return p.done()


_progs = {}


def _get_prog(name, builder):
    if name not in _progs:
        _progs[name] = builder()
    return _progs[name]


def run_stage(name, builder, in_maps):
    nc = _get_prog(name, builder)
    res = run_bass_kernel_spmd(nc, in_maps, core_ids=list(range(8)))
    return res.results


def _host_layouts(inp):
    asc = np.ascontiguousarray
    cw = inp["L1_attn_conv_w"].reshape(DI, KC)

    def pcol(a, c):
        return asc(np.asarray(a).reshape(c, 128).T)

    return dict(
        xpw2=asc(inp["L1_attn_x_proj_w"][NS:, :]),
        wrr=asc(inp["L0_ffn_Wr"].reshape(ND, 128, NE).transpose(1, 0, 2)
                .reshape(128, ND * NE)),
        b1r_0=asc(inp["L0_ffn_b1"].reshape(NE, 32, 128).transpose(2, 0, 1)
                  .reshape(128, NE * 32)),
        cwr=asc(cw.reshape(NDI, 128, KC).transpose(1, 0, 2)
                .reshape(128, NDI * KC)),
        cbr=pcol(inp["L1_attn_conv_b"], NDI),
        dtbr=pcol(inp["L1_attn_dt_b"], NDI),
        dpr=pcol(inp["L1_attn_D_param"], NDI),
        wgr=asc(inp["L1_ffn_Wg"].reshape(ND, 128, NE).transpose(1, 0, 2)
                .reshape(128, ND * NE)),
        gr=asc(inp["L1_ffn_gamma"].reshape(NE, ND, 128).transpose(2, 0, 1)
               .reshape(128, NE * ND)),
        br=asc(inp["L1_ffn_beta"].reshape(NE, ND, 128).transpose(2, 0, 1)
               .reshape(128, NE * ND)),
        b1r_1=asc(inp["L1_ffn_b1"].reshape(NE, 32, 128).transpose(2, 0, 1)
                  .reshape(128, NE * 32)),
        scale128=np.repeat(np.asarray(inp["L0_attn_attn_scale"]), 128),
    )


def kernel(**inp):
    B = 8
    asc = np.ascontiguousarray
    hl = _host_layouts(inp)
    x_all = asc(inp["L0_attn_x"])
    maps = [dict(
        x0=asc(x_all[b]),
        wq=inp["L0_attn_Wq"], wk=inp["L0_attn_Wk"],
        wv=inp["L0_attn_Wv"], wo=inp["L0_attn_Wo"],
        ipw=inp["L1_attn_in_proj_w"], nw=inp["L1_attn_norm_w"],
        dtw=inp["L1_attn_dt_w"], opw=inp["L1_attn_out_proj_w"],
        alog=inp["L1_attn_A_log"],
        bq=inp["L0_attn_bq"], bk=inp["L0_attn_bk"],
        bv=inp["L0_attn_bv"], bo=inp["L0_attn_bo"],
        w1_0=inp["L0_ffn_W1"], w2_0=inp["L0_ffn_W2"], b2_0=inp["L0_ffn_b2"],
        w1_1=inp["L1_ffn_W1"], w2_1=inp["L1_ffn_W2"], b2_1=inp["L1_ffn_b2"],
        **hl,
    ) for b in range(B)]
    res = run_stage("fused", build_fused, maps)
    return np.stack([np.asarray(res[b]["out"], np.float32) for b in range(B)])


# version 10 (final)
# speedup vs baseline: 1.1713x; 1.1478x over previous
CF1_INTERLEAVE = True
"""Raw-Bass kernel for mk_recurrent_d2 (depth-2: local-attn+top1-MoE block,
then Mamba-SSM+top2-MoE block), d_model=1024, T=2048, batch 8 -> 8 cores.

Sharding: batch element b -> core b. All weights replicated to every core.

On-chip layout conventions:
  * token-major (tm): [128 tokens (partitions), features (free)]
  * feature-major (fm): [128 feature-dims (partitions), tokens (free)]
  * matmuls contract over the partition dim; activations feed matmuls in fm
    (bf16), the residual stream lives token-major in fp32 (HBM-backed).
  * weights W[out,in] used as x@W.T are pre-transposed once into bf16 HBM
    scratch (prep stage); W[in,out] weights (MoE W1/W2, Wr, Wg) load natural.
"""
import contextlib
import numpy as np
import ml_dtypes
import concourse.bass as bass
import concourse.mybir as mybir
from concourse.bass_utils import run_bass_kernel_spmd

F32 = mybir.dt.float32
BF16 = mybir.dt.bfloat16
AF = mybir.ActivationFunctionType
ALU = mybir.AluOpType
AX = mybir.AxisListType.X

T = 2048
D = 1024
NT = T // 128
ND = D // 128
NH = 16
DH = 64
NE = 8
FF = 4096
DI = 2048
NDI = DI // 128
NS = 64
KC = 4

nbf = ml_dtypes.bfloat16


class Prog:
    def __init__(self, name="prog"):
        self.nc = bass.Bass("TRN2", target_bir_lowering=False)
        self.stack = contextlib.ExitStack()
        self.scopes = [self.stack]
        self.cnt = {}
        self.sems = {}
        for s in ("L", "S", "P", "V", "A", "G", "S2", "L2", "L3", "L4",
                  "L5", "G2"):
            self.sems[s] = self.stack.enter_context(self.nc.semaphore(s))
            self.cnt[s] = 0

    prefix = ""

    def push_scope(self):
        self.scopes.append(contextlib.ExitStack())

    def pop_scope(self):
        self.scopes.pop().close()

    def sbuf(self, name, shape, dt):
        return self.scopes[-1].enter_context(
            self.nc.sbuf_tensor(self.prefix + name, shape, dt))

    def psum(self, name, shape, dt):
        return self.scopes[-1].enter_context(
            self.nc.psum_tensor(self.prefix + name, shape, dt))

    def dram(self, name, shape, dt, kind=None):
        if kind is None:
            return self.nc.dram_tensor(name, shape, dt).ap()
        return self.nc.dram_tensor(name, shape, dt, kind=kind).ap()

    def inc(self, inst, sem, n=1):
        inst.then_inc(self.sems[sem], n)
        self.cnt[sem] += n
        return inst

    def ld(self, out, in_, sem="L", **kw):
        self.inc(self.nc.sync.dma_start(out=out, in_=in_, **kw), sem, 16)
        return self.cnt[sem]

    def st(self, out, in_, sem="S", **kw):
        self.inc(self.nc.gpsimd.dma_start(out=out, in_=in_, **kw), sem, 16)
        return self.cnt[sem]

    def wait(self, eng, sem, val=None):
        if val is None:
            val = self.cnt[sem]
        if val > 0:
            eng.wait_ge(self.sems[sem], val)

    def bar(self):
        self.nc.all_engine_barrier()

    def emit_consts(self):
        c = self.sbuf("consts", [128, 8], F32)
        v = self.nc.vector
        for i, val in enumerate([-1.0, 1.0, -0.5, 1.0 / D, 1e-5, 1e-6,
                                 1.0 / DI, 0.0]):
            v.memset(c[:, i:i + 1], val)
        ident = self.sbuf("identity", [128, 128], BF16)
        g = self.nc.gpsimd
        self.inc(g.iota(ident[:, :], [[1, 128]], channel_multiplier=-1,
                        allow_small_or_imprecise_dtypes=True), "G")
        self.wait(g, "G")
        g.tensor_single_scalar(out=ident[:], in_=ident[:],
                               scalar=0.0, op=ALU.is_equal)
        self.consts = c
        self.ident = ident
        self.bar()

    def C(self, i):
        return self.consts[:, i:i + 1]

    def done(self):
        self.stack.close()
        return self.nc


def emit_w_to_bf16(p, scr, src, dst, nelem):
    nc = p.nc
    v, g = nc.vector, nc.gpsimd
    CH = 128 * 4096
    nch = nelem // CH
    assert nelem % CH == 0
    stgf = scr["wc_stgf"]
    stgb = scr["wc_stgb"]
    sflat = src
    cast_marks = [p.cnt["V"]] * 2
    store_marks = [p.cnt["S"], p.cnt["S2"]]
    for c in range(nch):
        b = c % 2
        lsem = "L" if b == 0 else "L2"
        ssem = "S" if b == 0 else "S2"
        p.wait(nc.sync, "V", cast_marks[b])
        p.ld(stgf[:, b * 4096:(b + 1) * 4096],
             sflat[c * CH:(c + 1) * CH].rearrange("(p f) -> p f", p=128),
             sem=lsem)
        p.wait(v, lsem)
        p.wait(v, ssem, store_marks[b])
        p.inc(v.tensor_copy(out=stgb[:, b * 4096:(b + 1) * 4096],
                            in_=stgf[:, b * 4096:(b + 1) * 4096]), "V")
        cast_marks[b] = p.cnt["V"]
        p.wait(g, "V")
        p.st(dst[c * CH:(c + 1) * CH].rearrange("(p f) -> p f", p=128),
             stgb[:, b * 4096:(b + 1) * 4096], sem=ssem)
        store_marks[b] = p.cnt[ssem]
    p.bar()


def emit_weight_transpose(p, scr, src, dst, R, C, scale_sb=None):
    nc = p.nc
    assert C <= 2048 and R % 128 == 0 and C % 128 == 0
    nr, ncc = R // 128, C // 128
    stgf, otile, pst = scr["stgf"], scr["otile"], scr["pst"]
    v, t, g = nc.vector, nc.tensor, nc.gpsimd
    W = 2048
    cast_marks = [p.cnt["V"]] * 2
    evac_marks = [p.cnt["V"]] * 2
    store_marks = [p.cnt["S"], p.cnt["S2"]]
    for r in range(nr):
        b = r % 2
        lsem = "L" if b == 0 else "L2"
        p.wait(nc.sync, "V", cast_marks[b])
        p.ld(stgf[:, b * W:b * W + C], src[r * 128:(r + 1) * 128, :],
             sem=lsem)
        p.wait(v, lsem)
        p.wait(v, "S" if b == 0 else "S2", store_marks[b])
        p.inc(v.tensor_copy(out=otile[:, b * W:b * W + C],
                            in_=stgf[:, b * W:b * W + C]), "V")
        cast_marks[b] = p.cnt["V"]
        p.wait(t, "V")
        p.wait(t, "V", evac_marks[b])
        for cc in range(ncc):
            inst = t.transpose(pst[b][:, cc * 128:(cc + 1) * 128],
                               otile[:, b * W + cc * 128:b * W + (cc + 1) * 128],
                               p.ident[:, :])
        p.inc(inst, "P")
        p.wait(v, "P")
        if scale_sb is None:
            p.inc(v.tensor_copy(out=otile[:, b * W:b * W + C],
                                in_=pst[b][:, 0:C]), "V")
        else:
            for cc in range(ncc):
                inst = v.tensor_scalar(
                    out=otile[:, b * W + cc * 128:b * W + (cc + 1) * 128],
                    in0=pst[b][:, cc * 128:(cc + 1) * 128],
                    scalar1=scale_sb[:, cc:cc + 1], scalar2=None,
                    op0=ALU.mult)
            p.inc(inst, "V")
        evac_marks[b] = p.cnt["V"]
        p.wait(g, "V")
        ssem = "S" if b == 0 else "S2"
        for cc in range(ncc):
            p.st(dst[cc * 128:(cc + 1) * 128, r * 128:(r + 1) * 128],
                 otile[:, b * W + cc * 128:b * W + (cc + 1) * 128], sem=ssem)
        store_marks[b] = p.cnt[ssem]
    p.bar()


def emit_ln_to_fm(p, x_dram, xfm_dram):
    nc = p.nc
    v, s, t, g = nc.vector, nc.scalar, nc.tensor, nc.gpsimd
    xt = p.sbuf("ln_x", [128, 2 * D], F32)
    xnb = p.sbuf("ln_xnb", [128, 2 * D], BF16)
    trash = p.sbuf("ln_trash", [128, D], BF16)
    stat = p.sbuf("ln_stat", [128, 16], F32)
    pst = [p.psum(f"ln_ps{i}", [128, D], BF16) for i in range(2)]
    xread_marks = [0, 0]
    evac_marks = [0, 0]
    store_marks = [0, 0]
    for tt in range(NT):
        b = tt % 2
        o = 8 * b
        lsem = "L" if b == 0 else "L2"
        p.wait(nc.sync, "V", xread_marks[b])
        ldv = p.ld(xt[:, b * D:(b + 1) * D],
                   x_dram[tt * 128:(tt + 1) * 128, :], sem=lsem)
        xv = xt[:, b * D:(b + 1) * D]
        p.wait(v, lsem, ldv)
        p.inc(v.reduce_sum(stat[:, o:o + 1], xv, axis=AX), "V")
        p.wait(s, lsem, ldv)
        p.wait(s, "V", xread_marks[b])
        p.inc(s.activation(out=trash[:, :], in_=xv, func=AF.Square,
                           accum_out=stat[:, o + 1:o + 2]), "A")
        sq_mark = p.cnt["A"]
        p.wait(v, "A", sq_mark)
        p.wait(v, "V")
        p.inc(v.tensor_scalar(out=stat[:, o + 3:o + 4], in0=stat[:, o:o + 1],
                              scalar1=-1.0 / D, scalar2=None, op0=ALU.mult),
              "V")
        p.wait(v, "V")
        v.tensor_tensor(out=stat[:, o + 6:o + 7], in0=stat[:, o + 3:o + 4],
                        in1=stat[:, o + 3:o + 4], op=ALU.mult)
        p.inc(v.tensor_scalar(out=stat[:, o + 4:o + 5],
                              in0=stat[:, o + 1:o + 2],
                              scalar1=1.0 / D, scalar2=None, op0=ALU.mult),
              "V")
        p.wait(v, "V")
        p.inc(v.tensor_tensor(out=stat[:, o + 5:o + 6],
                              in0=stat[:, o + 4:o + 5],
                              in1=stat[:, o + 6:o + 7], op=ALU.subtract), "V")
        var_mark = p.cnt["V"]
        p.wait(s, "V", var_mark)
        p.inc(s.activation(out=stat[:, o + 2:o + 3], in_=stat[:, o + 5:o + 6],
                           func=AF.Ln, bias=p.C(4)), "A")
        p.wait(s, "A")
        p.inc(s.activation(out=stat[:, o + 2:o + 3], in_=stat[:, o + 2:o + 3],
                           func=AF.Exp, scale=p.C(2)), "A")
        p.wait(v, "A")
        p.wait(v, "S" if b == 0 else "S2", store_marks[b])
        p.inc(v.tensor_scalar(out=xnb[:, b * D:(b + 1) * D], in0=xv,
                              scalar1=stat[:, o + 3:o + 4],
                              scalar2=stat[:, o + 2:o + 3],
                              op0=ALU.add, op1=ALU.mult), "V")
        xread_marks[b] = p.cnt["V"]
        p.wait(t, "V")
        p.wait(t, "V", evac_marks[b])
        for dc in range(ND):
            inst = t.transpose(pst[b][:, dc * 128:(dc + 1) * 128],
                               xnb[:, b * D + dc * 128:b * D + (dc + 1) * 128],
                               p.ident[:, :])
        p.inc(inst, "P")
        p.wait(v, "P")
        p.inc(v.tensor_copy(out=xnb[:, b * D:(b + 1) * D], in_=pst[b][:, :]),
              "V")
        evac_marks[b] = p.cnt["V"]
        p.wait(g, "V")
        ssem = "S" if b == 0 else "S2"
        for dc in range(ND):
            p.st(xfm_dram[dc * 128:(dc + 1) * 128, tt * 128:(tt + 1) * 128],
                 xnb[:, b * D + dc * 128:b * D + (dc + 1) * 128], sem=ssem)
        store_marks[b] = p.cnt[ssem]
    p.bar()


def emit_stage_p(p, io):
    nc = p.nc
    x0 = io["x0"]
    wq = io["wq"]
    wk = io["wk"]
    wv = io["wv"]
    wo = io["wo"]
    ipw = io["ipw"]
    nw = io["nw"]
    dtw = io["dtw"]
    xpw2 = io["xpw2"]
    opw = io["opw"]
    alog = io["alog"]

    wqT = io["wqT"]
    wkT = io["wkT"]
    wvT = io["wvT"]
    woT = io["woT"]
    iprojT = io["iprojT"]
    dtwT = io["dtwT"]
    xpT = io["xpT"]
    opT = io["opT"]
    asb_d = io["asb"]
    xn0fm = io["xn0fm"]


    v, s = nc.vector, nc.scalar
    at = p.sbuf("a_t", [128, NDI * NS], F32)
    p.ld(at[:, :].rearrange("p (c n) -> p c n", n=NS),
         alog.rearrange("(c p) n -> p c n", p=128),
         allow_slow_non_contiguous=True)
    p.wait(s, "L")
    p.inc(s.activation(out=at[:, :], in_=at[:, :], func=AF.Exp), "A")
    p.wait(v, "A")
    p.inc(v.tensor_scalar(out=at[:, :], in0=at[:, :], scalar1=-1.0,
                          scalar2=None, op0=ALU.mult), "V")
    p.wait(nc.gpsimd, "V")
    p.st(asb_d[:, :], at[:, :])
    nwt = p.sbuf("nw_t", [128, ND], F32)
    p.ld(nwt[:, :], nw.rearrange("(c p) -> p c", p=128),
         allow_slow_non_contiguous=True, sem="L2")
    p.wait(v, "L2")
    p.bar()

    scr = dict(
        stgf=p.sbuf("wt_stgf", [128, 2 * 2048], F32),
        otile=p.sbuf("wt_ot", [128, 2 * 2048], BF16),
        pst=[p.psum(f"wt_ps{i}", [128, 2048], BF16) for i in range(2)],
    )
    emit_weight_transpose(p, scr, wq, wqT, D, D)
    emit_weight_transpose(p, scr, wk, wkT, D, D)
    emit_weight_transpose(p, scr, wv, wvT, D, D)
    emit_weight_transpose(p, scr, wo, woT, D, D)
    emit_weight_transpose(p, scr, ipw, iprojT, 2 * DI, D, scale_sb=nwt)
    emit_weight_transpose(p, scr, dtw, dtwT, DI, DI)
    emit_weight_transpose(p, scr, opw, opT, D, DI)

    t, g = nc.tensor, nc.gpsimd
    xp_stgf = scr["stgf"]
    xp_stgb = scr["otile"]
    xp_ot_full = scr["otile"]
    xp_ps = scr["pst"][0]
    p.wait(v, "S")
    p.wait(v, "S2")
    p.inc(v.memset(xp_stgb[:, 0:DI], 0.0), "V")
    p.ld(xp_stgf[0:64, 0:DI], xpw2[:, :])
    p.wait(v, "L")
    p.wait(v, "V")
    p.inc(v.tensor_copy(out=xp_stgb[0:64, 0:DI], in_=xp_stgf[0:64, 0:DI]),
          "V")
    p.wait(t, "V")
    for cc in range(NDI):
        inst = t.transpose(xp_ps[:, cc * 128:(cc + 1) * 128],
                           xp_stgb[:, cc * 128:(cc + 1) * 128],
                           p.ident[:, :])
    p.inc(inst, "P")
    p.wait(v, "P")
    for cc in range(NDI):
        inst = v.tensor_copy(out=xp_ot_full[:, 2048 + cc * NS: 2048 + (cc + 1) * NS],
                             in_=xp_ps[:, cc * 128:cc * 128 + NS])
    p.inc(inst, "V")
    p.wait(g, "V")
    for cc in range(NDI):
        p.st(xpT[cc * 128:(cc + 1) * 128, :],
             xp_ot_full[:, 2048 + cc * NS: 2048 + (cc + 1) * NS])
    p.bar()

    p.push_scope()
    wscr = dict(wc_stgf=p.sbuf("wc_stgf", [128, 2 * 4096], F32),
                wc_stgb=p.sbuf("wc_stgb", [128, 2 * 4096], BF16))
    for wsrc, wdst in (("w1_0", "w1b_0"), ("w2_0", "w2b_0"),
                       ("w1_1", "w1b_1"), ("w2_1", "w2b_1")):
        emit_w_to_bf16(p, wscr,
                       io[wsrc].rearrange("e a b -> (e a b)"),
                       io[wdst], NE * D * FF)
    p.pop_scope()

    emit_ln_to_fm(p, x0, xn0fm)

    g = nc.gpsimd
    p.wait(g, "S")
    p.wait(g, "L")
def build_stage_p():
    p = Prog("p")
    io = {}
    io["x0"] = p.dram("x0", [T, D], F32, "ExternalInput")
    io["xpw2"] = p.dram("xpw2", [NS, DI], F32, "ExternalInput")
    io["wq"] = p.dram("wq", [D, D], F32, "ExternalInput")
    io["wk"] = p.dram("wk", [D, D], F32, "ExternalInput")
    io["wv"] = p.dram("wv", [D, D], F32, "ExternalInput")
    io["wo"] = p.dram("wo", [D, D], F32, "ExternalInput")
    io["ipw"] = p.dram("ipw", [2 * DI, D], F32, "ExternalInput")
    io["nw"] = p.dram("nw", [D], F32, "ExternalInput")
    io["dtw"] = p.dram("dtw", [DI, DI], F32, "ExternalInput")
    io["opw"] = p.dram("opw", [D, DI], F32, "ExternalInput")
    io["alog"] = p.dram("alog", [DI, NS], F32, "ExternalInput")
    io["wqT"] = p.dram("wqT", [D, D], BF16, "ExternalOutput")
    io["wkT"] = p.dram("wkT", [D, D], BF16, "ExternalOutput")
    io["wvT"] = p.dram("wvT", [D, D], BF16, "ExternalOutput")
    io["woT"] = p.dram("woT", [D, D], BF16, "ExternalOutput")
    io["iprojT"] = p.dram("iprojT", [D, 2 * DI], BF16, "ExternalOutput")
    io["dtwT"] = p.dram("dtwT", [DI, DI], BF16, "ExternalOutput")
    io["xpT"] = p.dram("xpT", [DI, NS], BF16, "ExternalOutput")
    io["opT"] = p.dram("opT", [DI, D], BF16, "ExternalOutput")
    io["asb"] = p.dram("asb", [128, NDI * NS], F32, "ExternalOutput")
    io["xn0fm"] = p.dram("xn0fm", [D, T], BF16, "ExternalOutput")
    p.emit_consts()
    p.push_scope()
    emit_stage_p(p, io)
    p.pop_scope()
    return p.done()



def emit_stage_a(p, io):
    nc = p.nc
    v, s, t, g = nc.vector, nc.scalar, nc.tensor, nc.gpsimd
    xn0fm = io["xn0fm"]
    wqT = io["wqT"]
    wkT = io["wkT"]
    wvT = io["wvT"]
    woT = io["woT"]
    bq = io["bq"]
    bk = io["bk"]
    bv = io["bv"]
    bo = io["bo"]
    scale128 = io["scale128"]
    s0_d = io["s0"]


    xn0 = p.sbuf("xn0", [128, ND * T], BF16)
    wbuf = p.sbuf("wbuf", [128, ND * D], BF16)
    wq_sb = wk_sb = wv_sb = wo_sb = wbuf
    q_sb = p.sbuf("q_sb", [128, ND * T], BF16)
    k_sb = p.sbuf("k_sb", [128, ND * T], BF16)
    v_sb = p.sbuf("v_sb", [128, NT * D], BF16)
    bias_sb = p.sbuf("bias_sb", [128, 4 * ND], F32)
    scale_sb = p.sbuf("scale_sb", [128, 1], F32)
    brow = p.sbuf("brow", [1, 2 * D], BF16)
    browf = p.sbuf("browf", [1, 2 * D], F32)
    bvb = p.sbuf("bv_bcast", [128, D], F32)
    bob = p.sbuf("bo_bcast", [128, D], F32)
    ones1 = p.sbuf("ones1", [1, 128], BF16)
    maskc = p.sbuf("maskc", [128, 128], F32)
    mk_t = p.sbuf("mk_t", [128, 3 * 128], F32)
    mk_b = p.sbuf("mk_b", [128, 2], F32)
    probs = p.sbuf("probs", [128, 2 * T], BF16)
    probsT = p.sbuf("probsT", [128, T], BF16)
    ao_sb = p.sbuf("ao_sb", [128, ND * 128], BF16)
    att_st = p.sbuf("att_st", [128, 8], F32)
    s0t = p.sbuf("s0t", [128, 2 * D], F32)

    ps_sc = p.psum("ps_sc", [128, T], F32)
    ps_tr = p.psum("ps_tr", [128, T], BF16)
    ps_av = p.psum("ps_av", [128, 128], F32)
    ps_s0 = p.psum("ps_s0", [128, 512], F32)

    for dc in range(ND):
        p.ld(xn0[:, dc * T:(dc + 1) * T], xn0fm[dc * 128:(dc + 1) * 128, :])
    p.ld(wbuf[:, :].rearrange("p (c o) -> p c o", c=ND),
         wqT.rearrange("(c p) o -> p c o", p=128))
    p.ld(bias_sb[:, 0:ND], bq.rearrange("(c p) -> p c", p=128),
         allow_slow_non_contiguous=True)
    p.ld(bias_sb[:, ND:2 * ND], bk.rearrange("(c p) -> p c", p=128),
         allow_slow_non_contiguous=True)
    p.ld(scale_sb[:, 0:1], scale128[:, None])
    p.ld(browf[0:1, 0:D], bv[None, :])
    p.ld(browf[0:1, D:2 * D], bo[None, :])
    p.inc(g.iota(mk_t[:, 0:128], [[1, 2], [0, 64]], channel_multiplier=0,
                 allow_small_or_imprecise_dtypes=True), "G")
    p.inc(g.iota(mk_t[:, 128:256], [[1, 128]], channel_multiplier=-1,
                 allow_small_or_imprecise_dtypes=True), "G")
    p.inc(g.iota(mk_b[:, 0:1], [[0, 1]], channel_multiplier=1,
                 allow_small_or_imprecise_dtypes=True), "G")
    p.wait(v, "G")
    p.inc(v.tensor_scalar(out=mk_b[:, 1:2], in0=mk_b[:, 0:1], scalar1=63.5,
                          scalar2=None, op0=ALU.is_gt), "V")
    p.wait(v, "V")
    p.inc(v.tensor_scalar(out=mk_t[:, 256:384], in0=mk_t[:, 0:128],
                          scalar1=mk_b[:, 1:2], scalar2=None, op0=ALU.is_gt),
          "V")
    p.wait(v, "V")
    p.inc(v.tensor_scalar(out=mk_t[:, 0:128], in0=mk_t[:, 0:128],
                          scalar1=mk_b[:, 1:2], scalar2=None,
                          op0=ALU.is_equal), "V")
    p.wait(v, "V")
    p.inc(v.tensor_single_scalar(out=mk_t[:, 128:256], in_=mk_t[:, 128:256],
                                 scalar=0.5, op=ALU.is_le), "V")
    p.wait(v, "V")
    p.inc(v.tensor_tensor(out=mk_t[:, 0:128], in0=mk_t[:, 0:128],
                          in1=mk_t[:, 128:256], op=ALU.mult), "V")
    p.wait(v, "V")
    p.inc(v.tensor_tensor(out=mk_t[:, 0:128], in0=mk_t[:, 0:128],
                          in1=mk_t[:, 256:384], op=ALU.max), "V")
    p.wait(v, "V")
    p.inc(v.tensor_scalar(out=maskc[:, :], in0=mk_t[:, 0:128], scalar1=-1.0,
                          scalar2=1e30, op0=ALU.add, op1=ALU.mult), "V")
    p.wait(v, "L")
    p.inc(v.tensor_scalar(out=bias_sb[:, 0:ND], in0=bias_sb[:, 0:ND],
                          scalar1=scale_sb[:, 0:1], scalar2=None,
                          op0=ALU.mult), "V")
    v.memset(ones1[0:1, :], 1.0)
    p.wait(v, "V")
    p.inc(v.tensor_copy(out=brow[0:1, :], in_=browf[0:1, :]), "V")
    p.wait(t, "V")
    for j, (dst, off) in enumerate(((bvb, 0), (bvb, 512), (bob, D),
                                    (bob, D + 512))):
        inst = t.matmul(ps_sc[:, (j % 4) * 512:(j % 4) * 512 + 512],
                        ones1[0:1, :], brow[0:1, off:off + 512],
                        start=True, stop=True)
    p.inc(inst, "P")
    p.wait(v, "P")
    v.tensor_copy(out=bvb[:, :], in_=ps_sc[:, 0:1024])
    p.inc(v.tensor_copy(out=bob[:, :], in_=ps_sc[:, 1024:2048]), "V")
    p.bar()

    def proj_qk(out_sb, bias_off, use_scale):
        evac_marks = [0, 0]
        for och in range(ND):
            for tch in range(4):
                half = (och * 4 + tch) % 2
                pso = ps_s0 if half == 0 else ps_sc
                psv = pso[:, 0:512]
                p.wait(t, "V", evac_marks[half])
                for dch in range(ND):
                    inst = t.matmul(
                        psv,
                        wbuf[:, dch * D + och * 128:dch * D + (och + 1) * 128],
                        xn0[:, dch * T + tch * 512:dch * T + (tch + 1) * 512],
                        start=(dch == 0), stop=(dch == ND - 1))
                p.inc(inst, "P")
                p.wait(v, "P")
                if use_scale:
                    inst = v.tensor_scalar(
                        out=out_sb[:, och * T + tch * 512:
                                   och * T + (tch + 1) * 512],
                        in0=psv, scalar1=scale_sb[:, 0:1],
                        scalar2=bias_sb[:, bias_off + och:bias_off + och + 1],
                        op0=ALU.mult, op1=ALU.add)
                else:
                    inst = v.tensor_scalar(
                        out=out_sb[:, och * T + tch * 512:
                                   och * T + (tch + 1) * 512],
                        in0=psv,
                        scalar1=bias_sb[:, bias_off + och:bias_off + och + 1],
                        scalar2=None, op0=ALU.add)
                p.inc(inst, "V")
                evac_marks[half] = p.cnt["V"]

    proj_qk(q_sb, 0, True)
    p.bar()
    p.ld(wbuf[:, :].rearrange("p (c o) -> p c o", c=ND),
         wkT.rearrange("(c p) o -> p c o", p=128))
    p.wait(t, "L")
    proj_qk(k_sb, ND, False)
    p.bar()
    p.ld(wbuf[:, :].rearrange("p (c o) -> p c o", c=ND),
         wvT.rearrange("(c p) o -> p c o", p=128))
    p.wait(t, "L")
    evac_marks = [0, 0]
    for t128 in range(NT):
        for oh in range(2):
            half = (t128 * 2 + oh) % 2
            pso = ps_s0 if half == 0 else ps_sc
            psv = pso[:, 0:512]
            p.wait(t, "V", evac_marks[half])
            for dch in range(ND):
                inst = t.matmul(
                    psv,
                    xn0[:, dch * T + t128 * 128:dch * T + (t128 + 1) * 128],
                    wbuf[:, dch * D + oh * 512:dch * D + (oh + 1) * 512],
                    start=(dch == 0), stop=(dch == ND - 1))
            p.inc(inst, "P")
            p.wait(v, "P")
            p.inc(v.tensor_tensor(
                out=v_sb[:, t128 * D + oh * 512:t128 * D + (oh + 1) * 512],
                in0=psv, in1=bvb[:, oh * 512:(oh + 1) * 512], op=ALU.add),
                "V")
            evac_marks[half] = p.cnt["V"]
    p.bar()
    p.ld(wbuf[:, :].rearrange("p (c o) -> p c o", c=ND),
         woT.rearrange("(c p) o -> p c o", p=128))
    p.wait(t, "L")
    p.bar()

    exp_mark = 0
    trans_mark = 0
    trev_mark = 0
    aoev_mark = 0
    s0proj_mark = 0
    s0ev_mark = 0
    s0st_marks = [0, 0]
    probs_free = [0, 0]
    for tq in range(NT):
        for h in range(NH):
            hb = h % 2
            po = (h % 2) * 64
            och = h // 2
            koff = 128 * tq
            W = T - koff
            nk1 = W // 128
            qsl = q_sb[po:po + 64,
                       och * T + tq * 128:och * T + (tq + 1) * 128]
            p.wait(t, "A", exp_mark)
            nfull = (W + 511) // 512
            for c5 in range(nfull):
                wmm = min(512, W - c5 * 512)
                inst = t.matmul(
                    ps_sc[:, c5 * 512:c5 * 512 + wmm], qsl,
                    k_sb[po:po + 64,
                         och * T + koff + c5 * 512:
                         och * T + koff + c5 * 512 + wmm],
                    start=True, stop=True)
            p.inc(inst, "P")
            p.wait(v, "P")
            p.inc(v.tensor_tensor(out=ps_sc[:, 0:128], in0=ps_sc[:, 0:128],
                                  in1=maskc[:, :], op=ALU.add), "V")
            p.wait(v, "V")
            p.inc(v.reduce_max(att_st[:, 0:1], ps_sc[:, 0:W], axis=AX), "V")
            p.wait(v, "V")
            p.inc(v.tensor_scalar(out=att_st[:, 1:2], in0=att_st[:, 0:1],
                                  scalar1=-1.0, scalar2=None, op0=ALU.mult),
                  "V")
            p.wait(s, "V")
            p.wait(s, "P", probs_free[hb])
            p.inc(s.activation(out=probs[:, hb * T:hb * T + W],
                               in_=ps_sc[:, 0:W], func=AF.Exp,
                               bias=att_st[:, 1:2],
                               accum_out=att_st[:, 2:3]), "A")
            exp_mark = p.cnt["A"]
            p.wait(v, "A")
            p.inc(v.reciprocal(att_st[:, 3:4], att_st[:, 2:3]), "V")
            p.wait(v, "V")
            p.inc(v.tensor_scalar(out=probs[:, hb * T:hb * T + W],
                                  in0=probs[:, hb * T:hb * T + W],
                                  scalar1=att_st[:, 3:4], scalar2=None,
                                  op0=ALU.mult), "V")
            norm_mark = p.cnt["V"]
            p.wait(t, "V", norm_mark)
            p.wait(t, "V", trev_mark)
            for kb in range(nk1):
                inst = t.transpose(ps_tr[:, kb * 128:(kb + 1) * 128],
                                   probs[:, hb * T + kb * 128:
                                         hb * T + (kb + 1) * 128],
                                   p.ident[:, :])
            p.inc(inst, "P")
            probs_free[hb] = p.cnt["P"]
            trans_mark = p.cnt["P"]
            p.wait(v, "P", trans_mark)
            p.inc(v.tensor_copy(out=probsT[:, 0:W], in_=ps_tr[:, 0:W]), "V")
            trev_mark = p.cnt["V"]
            p.wait(t, "V", trev_mark)
            p.wait(t, "V", aoev_mark)
            for kb in range(nk1):
                inst = t.matmul(ps_av[0:64, :],
                                v_sb[:, (tq + kb) * D + h * 64:
                                     (tq + kb) * D + h * 64 + 64],
                                probsT[:, kb * 128:(kb + 1) * 128],
                                start=(kb == 0), stop=(kb == nk1 - 1))
            p.inc(inst, "P")
            p.wait(v, "P")
            p.wait(v, "P", s0proj_mark)
            p.inc(v.tensor_copy(
                out=ao_sb[po:po + 64, och * 128:(och + 1) * 128],
                in_=ps_av[0:64, :]), "V")
            aoev_mark = p.cnt["V"]
        tb = tq % 2
        p.wait(t, "V")
        for oh in range(2):
            p.wait(t, "V", s0ev_mark)
            for och in range(ND):
                inst = t.matmul(ps_s0[:, :],
                                ao_sb[:, och * 128:(och + 1) * 128],
                                wo_sb[:, och * D + oh * 512:
                                      och * D + (oh + 1) * 512],
                                start=(och == 0), stop=(och == ND - 1))
            p.inc(inst, "P")
            if oh == 1:
                s0proj_mark = p.cnt["P"]
            p.wait(v, "P")
            p.wait(v, "S" if tb == 0 else "S2", s0st_marks[tb])
            p.inc(v.tensor_tensor(
                out=s0t[:, tb * D + oh * 512:tb * D + (oh + 1) * 512],
                in0=ps_s0[:, :], in1=bob[:, oh * 512:(oh + 1) * 512],
                op=ALU.add), "V")
            s0ev_mark = p.cnt["V"]
            p.wait(g, "V")
            ssem = "S" if tb == 0 else "S2"
            p.st(s0_d[tq * 128:(tq + 1) * 128, oh * 512:(oh + 1) * 512],
                 s0t[:, tb * D + oh * 512:tb * D + (oh + 1) * 512], sem=ssem)
            s0st_marks[tb] = p.cnt[ssem]
    p.bar()
    p.wait(g, "S")
def build_stage_a():
    p = Prog("a")
    io = {}
    io["xn0fm"] = p.dram("xn0fm", [D, T], BF16, "ExternalInput")
    io["wqT"] = p.dram("wqT", [D, D], BF16, "ExternalInput")
    io["wkT"] = p.dram("wkT", [D, D], BF16, "ExternalInput")
    io["wvT"] = p.dram("wvT", [D, D], BF16, "ExternalInput")
    io["woT"] = p.dram("woT", [D, D], BF16, "ExternalInput")
    io["bq"] = p.dram("bq", [D], F32, "ExternalInput")
    io["bk"] = p.dram("bk", [D], F32, "ExternalInput")
    io["bv"] = p.dram("bv", [D], F32, "ExternalInput")
    io["bo"] = p.dram("bo", [D], F32, "ExternalInput")
    io["scale128"] = p.dram("scale128", [128], F32, "ExternalInput")
    io["s0"] = p.dram("s0", [T, D], F32, "ExternalOutput")
    p.emit_consts()
    p.push_scope()
    emit_stage_a(p, io)
    p.pop_scope()
    return p.done()



def emit_stage_f0(p, io):
    nc = p.nc
    v, s, t, g = nc.vector, nc.scalar, nc.tensor, nc.gpsimd
    xn0fm = io["xn0fm"]
    wrr = io["wrr"]
    w1 = io["w1"]
    b1r = io["b1r"]
    w2 = io["w2"]
    b2 = io["b2"]
    x0 = io["x0"]
    s0_d = io["s0"]
    x1_d = io["x1"]
    xn1_d = io["xn1"]
    xn1sfm = io["xn1sfm"]
    xhatfm = io["xhatfm"]
    xn1fm = io["xn1fm"]


    xn0 = p.sbuf("xn0", [128, ND * T], BF16)
    acc0 = p.sbuf("acc0", [128, NT * D], BF16)
    acc2 = p.sbuf("acc2", [128, NT * D], BF16)
    stg = p.sbuf("stg", [128, 4096], F32)
    w1bf = p.sbuf("w1bf", [128, 2 * 4096], BF16)
    w2bf = p.sbuf("w2bf", [128, 2 * 4096], BF16)
    h_sb = p.sbuf("h_sb", [128, 4 * T], BF16)
    wr_sb = p.sbuf("wr_sb", [128, ND * NE], BF16)
    b1_sb = p.sbuf("b1_sb", [128, NE * (FF // 128)], F32)
    b2row = p.sbuf("b2row", [1, D], BF16)
    b2rowf = p.sbuf("b2rowf", [1, D], F32)
    b2b = p.sbuf("b2b", [128, D], F32)
    ones1 = p.sbuf("ones1", [1, 128], BF16)
    logit_sb = p.sbuf("logit_sb", [128, NT * NE], F32)
    sc_sb = p.sbuf("sc_sb", [128, NT * NE], F32)
    lst = p.sbuf("lst", [128, 8], F32)
    trash8 = p.sbuf("trash8", [128, NE], F32)
    tmpc = p.sbuf("tmpc", [128, D], F32)

    ps_h = [p.psum(f"ps_h{i}", [128, 512], F32) for i in range(2)]
    ps_o = [p.psum(f"ps_o{i}", [128, 512], F32) for i in range(2)]
    ps_bc = p.psum("ps_bc", [128, D], F32)

    for dc in range(ND):
        p.ld(xn0[:, dc * T:(dc + 1) * T], xn0fm[dc * 128:(dc + 1) * 128, :])
    p.ld(stg[:, 0:ND * NE], wrr[:, :])
    p.ld(b1_sb[:, :], b1r[:, :])
    v.memset(ones1[0:1, :], 1.0)
    p.wait(v, "L")
    p.inc(v.tensor_copy(out=wr_sb[:, :], in_=stg[:, 0:ND * NE]), "V")
    p.bar()

    rev_marks = [0, 0]
    for t128 in range(NT):
        pb = t128 % 2
        p.wait(t, "V", rev_marks[pb])
        for dch in range(ND):
            inst = t.matmul(ps_h[pb][:, 0:NE],
                            xn0[:, dch * T + t128 * 128:
                                dch * T + (t128 + 1) * 128],
                            wr_sb[:, dch * NE:(dch + 1) * NE],
                            start=(dch == 0), stop=(dch == ND - 1))
        p.inc(inst, "P")
        p.wait(v, "P")
        p.inc(v.tensor_copy(out=logit_sb[:, t128 * NE:(t128 + 1) * NE],
                            in_=ps_h[pb][:, 0:NE]), "V")
        rev_marks[pb] = p.cnt["V"]
    for t128 in range(NT):
        lg = logit_sb[:, t128 * NE:(t128 + 1) * NE]
        p.wait(v, "V")
        p.inc(v.reduce_max(lst[:, 0:1], lg, axis=AX), "V")
        p.wait(v, "V")
        p.inc(v.tensor_scalar(out=lst[:, 1:2], in0=lst[:, 0:1], scalar1=-1.0,
                              scalar2=None, op0=ALU.mult), "V")
        p.wait(v, "V")
        p.inc(v.tensor_scalar(out=sc_sb[:, t128 * NE:(t128 + 1) * NE],
                              in0=lg, scalar1=lst[:, 0:1], scalar2=None,
                              op0=ALU.is_equal), "V")
        p.wait(s, "V")
        p.inc(s.activation(out=trash8[:, :], in_=lg, func=AF.Exp,
                           bias=lst[:, 1:2], accum_out=lst[:, 2:3]), "A")
        p.wait(v, "A")
        p.inc(v.reciprocal(lst[:, 3:4], lst[:, 2:3]), "V")
        p.wait(v, "V")
        p.inc(v.tensor_scalar(out=sc_sb[:, t128 * NE:(t128 + 1) * NE],
                              in0=sc_sb[:, t128 * NE:(t128 + 1) * NE],
                              scalar1=lst[:, 3:4], scalar2=None,
                              op0=ALU.mult), "V")
    p.bar()

    cast_marks = [0, 0]
    hps_marks = [0, 0]
    ops_marks = [0, 0]
    w2mm_mark = 0
    accv_mark = 0
    bcev_mark = 0
    bcps_mark = 0
    for e in range(NE):
        p.wait(nc.sync, "V", bcev_mark)
        self_ = p.inc(nc.sync.dma_start(out=b2rowf[0:1, :],
                                        in_=b2[e, :][None, :]), "L3", 16)
        p.wait(v, "L3")
        p.inc(v.tensor_copy(out=b2row[0:1, :], in_=b2rowf[0:1, :]), "V")
        p.wait(t, "V")
        p.wait(t, "V", bcev_mark)
        for j in range(2):
            inst = t.matmul(ps_bc[:, j * 512:(j + 1) * 512], ones1[0:1, :],
                            b2row[0:1, j * 512:(j + 1) * 512],
                            start=True, stop=True)
        p.inc(inst, "P")
        p.wait(v, "P")
        p.inc(v.tensor_copy(out=b2b[:, :], in_=ps_bc[:, :]), "V")
        bcev_mark = p.cnt["V"]
        for f8 in range(8):
            pb = (e * 8 + f8) % 2
            lsem = "L" if pb == 0 else "L2"
            p.wait(nc.sync, "P", cast_marks[pb])
            p.ld(w1bf[:, pb * 4096:(pb + 1) * 4096]
                 .rearrange("p (c f) -> p c f", c=ND),
                 w1[e].rearrange("(c p) f -> p c f", p=128)
                 [:, :, f8 * 512:(f8 + 1) * 512], sem=lsem)
            p.ld(w2bf[:, pb * 4096:(pb + 1) * 4096]
                 .rearrange("p (c d) -> p c d", c=4),
                 w2[e].rearrange("(c p) d -> p c d", p=128)
                 [:, f8 * 4:(f8 + 1) * 4, :], sem=lsem)
            w1v = w1bf[:, pb * 4096:(pb + 1) * 4096]
            w2v = w2bf[:, pb * 4096:(pb + 1) * 4096]
            p.wait(t, lsem)
            p.wait(t, "P", w2mm_mark)
            for ft in range(4):
                ftg = f8 * 4 + ft
                for tch in range(4):
                    hb = (ft * 4 + tch) % 2
                    p.wait(t, "A", hps_marks[hb])
                    for dch in range(ND):
                        inst = t.matmul(
                            ps_h[hb][:, :],
                            w1v[:, dch * 512 + ft * 128:
                                dch * 512 + (ft + 1) * 128],
                            xn0[:, dch * T + tch * 512:
                                dch * T + (tch + 1) * 512],
                            start=(dch == 0), stop=(dch == ND - 1))
                    p.inc(inst, "P")
                    p.wait(s, "P")
                    p.inc(s.activation(
                        out=h_sb[:, ft * T + tch * 512:ft * T + (tch + 1) * 512],
                        in_=ps_h[hb][:, :], func=AF.Gelu,
                        bias=b1_sb[:, e * 32 + ftg:e * 32 + ftg + 1]), "A")
                    hps_marks[hb] = p.cnt["A"]
            hre_mark = p.cnt["A"]
            p.wait(v, "V", accv_mark)
            accv_base = p.cnt["V"]
            for t128 in range(NT):
                for dh in range(2):
                    ob = (t128 * 2 + dh) % 2
                    p.wait(t, "A", hre_mark)
                    p.wait(t, "V", ops_marks[ob])
                    for ft in range(4):
                        inst = t.matmul(
                            ps_o[ob][:, :],
                            h_sb[:, ft * T + t128 * 128:
                                 ft * T + (t128 + 1) * 128],
                            w2v[:, ft * D + dh * 512:ft * D + (dh + 1) * 512],
                            start=(ft == 0), stop=(ft == 3))
                    p.inc(inst, "P")
                    p.wait(v, "P")
                    dst = acc2[:, t128 * D + dh * 512:t128 * D + (dh + 1) * 512]
                    if f8 == 0:
                        inst = v.tensor_copy(out=dst, in_=ps_o[ob][:, :])
                    else:
                        inst = v.tensor_tensor(out=dst, in0=ps_o[ob][:, :],
                                               in1=dst, op=ALU.add)
                    p.inc(inst, "V")
                    ops_marks[ob] = p.cnt["V"]
            accv_mark = p.cnt["V"]
            w2mm_mark = p.cnt["P"]
            cast_marks[pb] = p.cnt["P"]
        p.wait(v, "V", accv_mark)
        for t128 in range(NT):
            a2 = acc2[:, t128 * D:(t128 + 1) * D]
            a0 = acc0[:, t128 * D:(t128 + 1) * D]
            scc = sc_sb[:, t128 * NE + e:t128 * NE + e + 1]
            p.inc(v.tensor_tensor(out=tmpc[:, :], in0=a2, in1=b2b[:, :],
                                  op=ALU.add), "V")
            p.wait(v, "V")
            if e == 0:
                inst = v.tensor_scalar(out=a0, in0=tmpc[:, :], scalar1=scc,
                                       scalar2=None, op0=ALU.mult)
            else:
                inst = v.scalar_tensor_tensor(out=a0, in0=tmpc[:, :],
                                              scalar=scc, in1=a0,
                                              op0=ALU.mult, op1=ALU.add)
            p.inc(inst, "V")
            p.wait(v, "V")
        accv_mark = p.cnt["V"]
    p.bar()

    xt = p.sbuf("c_x", [128, D], F32)
    st_ = p.sbuf("c_s", [128, D], F32)
    gt_ = p.sbuf("c_g", [128, D], F32)
    x1t = p.sbuf("c_x1", [128, D], F32)
    xnt = p.sbuf("c_xn", [128, D], F32)
    xb1 = p.sbuf("c_xb", [128, 3 * D], BF16)
    stat = p.sbuf("c_stat", [128, 16], F32)
    trashd = p.sbuf("c_trash", [128, D], BF16)
    ps_t = [p.psum(f"c_pst{i}", [128, D], BF16) for i in range(2)]
    iter_mark = 0
    evac_marks = [0, 0]
    for tt in range(NT):
        ssem = "S" if tt % 2 == 0 else "S2"
        p.wait(nc.sync, "V", iter_mark)
        p.ld(xt[:, :], x0[tt * 128:(tt + 1) * 128, :])
        p.ld(st_[:, :], s0_d[tt * 128:(tt + 1) * 128, :])
        ldm = p.cnt["L"]
        p.wait(s, "A")
        p.inc(s.activation(out=gt_[:, :], in_=acc0[:, tt * D:(tt + 1) * D],
                           func=AF.Silu), "A")
        p.wait(v, "A")
        p.wait(v, "L", ldm)
        p.inc(v.tensor_tensor(out=gt_[:, :], in0=gt_[:, :],
                              in1=st_[:, :], op=ALU.mult), "V")
        p.wait(v, "V")
        p.wait(v, "S")
        p.wait(v, "S2")
        p.inc(v.tensor_tensor(out=x1t[:, :], in0=xt[:, :], in1=gt_[:, :],
                              op=ALU.add), "V")
        p.wait(v, "V")
        x1v = x1t[:, :]
        p.wait(g, "V")
        p.st(x1_d[tt * 128:(tt + 1) * 128, :], x1v, sem=ssem)
        p.inc(v.reduce_sum(stat[:, 0:1], x1v, axis=AX), "V")
        p.wait(s, "V")
        p.inc(s.activation(out=trashd[:, :], in_=x1v, func=AF.Square,
                           accum_out=stat[:, 1:2]), "A")
        p.wait(v, "A")
        p.wait(v, "V")
        p.inc(v.tensor_scalar(out=stat[:, 3:4], in0=stat[:, 0:1],
                              scalar1=-1.0 / D, scalar2=None, op0=ALU.mult),
              "V")
        p.wait(v, "V")
        v.tensor_tensor(out=stat[:, 6:7], in0=stat[:, 3:4],
                        in1=stat[:, 3:4], op=ALU.mult)
        p.inc(v.tensor_scalar(out=stat[:, 4:5], in0=stat[:, 1:2],
                              scalar1=1.0 / D, scalar2=None, op0=ALU.mult),
              "V")
        p.wait(v, "V")
        p.inc(v.tensor_tensor(out=stat[:, 5:6], in0=stat[:, 4:5],
                              in1=stat[:, 6:7], op=ALU.subtract), "V")
        p.wait(s, "V")
        p.inc(s.activation(out=stat[:, 2:3], in_=stat[:, 5:6], func=AF.Ln,
                           bias=p.C(4)), "A")
        p.wait(s, "A")
        p.inc(s.activation(out=stat[:, 2:3], in_=stat[:, 2:3], func=AF.Exp,
                           scale=p.C(2)), "A")
        p.wait(v, "A")
        p.inc(v.tensor_scalar(out=xnt[:, :], in0=x1v,
                              scalar1=stat[:, 3:4], scalar2=stat[:, 2:3],
                              op0=ALU.add, op1=ALU.mult), "V")
        xnv = xnt[:, :]
        p.wait(v, "V")
        p.wait(g, "V")
        p.st(xn1_d[tt * 128:(tt + 1) * 128, :], xnv, sem=ssem)
        p.inc(v.reduce_sum(stat[:, 8:9], xnv, axis=AX), "V")
        p.wait(s, "V")
        p.inc(s.activation(out=trashd[:, :], in_=xnv, func=AF.Square,
                           accum_out=stat[:, 9:10]), "A")
        p.wait(v, "A")
        p.wait(v, "V")
        p.inc(v.tensor_scalar(out=stat[:, 10:11], in0=stat[:, 9:10],
                              scalar1=1.0 / D, scalar2=None, op0=ALU.mult),
              "V")
        p.wait(s, "V")
        p.inc(s.activation(out=stat[:, 11:12], in_=stat[:, 10:11], func=AF.Ln,
                           bias=p.C(5)), "A")
        p.wait(s, "A")
        p.inc(s.activation(out=stat[:, 11:12], in_=stat[:, 11:12],
                           func=AF.Exp, scale=p.C(2)), "A")
        p.wait(v, "V")
        p.inc(v.tensor_scalar(out=stat[:, 12:13], in0=stat[:, 8:9],
                              scalar1=-1.0 / D, scalar2=None, op0=ALU.mult),
              "V")
        p.wait(v, "V")
        v.tensor_tensor(out=stat[:, 13:14], in0=stat[:, 12:13],
                        in1=stat[:, 12:13], op=ALU.mult)
        p.inc(v.tensor_scalar(out=stat[:, 14:15], in0=stat[:, 9:10],
                              scalar1=1.0 / D, scalar2=None, op0=ALU.mult),
              "V")
        p.wait(v, "V")
        p.inc(v.tensor_tensor(out=stat[:, 14:15], in0=stat[:, 14:15],
                              in1=stat[:, 13:14], op=ALU.subtract), "V")
        p.wait(s, "V")
        p.inc(s.activation(out=stat[:, 15:16], in_=stat[:, 14:15], func=AF.Ln,
                           bias=p.C(5)), "A")
        p.wait(s, "A")
        p.inc(s.activation(out=stat[:, 15:16], in_=stat[:, 15:16],
                           func=AF.Exp, scale=p.C(2)), "A")
        p.wait(v, "A")
        p.inc(v.tensor_scalar(out=xb1[:, 0:D], in0=xnv,
                              scalar1=stat[:, 11:12], scalar2=None,
                              op0=ALU.mult), "V")
        p.inc(v.tensor_scalar(out=xb1[:, D:2 * D], in0=xnv,
                              scalar1=stat[:, 12:13], scalar2=stat[:, 15:16],
                              op0=ALU.add, op1=ALU.mult), "V")
        p.inc(v.tensor_copy(out=xb1[:, 2 * D:3 * D], in_=xnv), "V")
        xb_mark = p.cnt["V"]
        for j, dstd in enumerate((xn1sfm, xhatfm, xn1fm)):
            jb = j % 2
            p.wait(t, "V", xb_mark)
            p.wait(t, "V", evac_marks[jb])
            for dc in range(ND):
                inst = t.transpose(ps_t[jb][:, dc * 128:(dc + 1) * 128],
                                   xb1[:, j * D + dc * 128:
                                       j * D + (dc + 1) * 128],
                                   p.ident[:, :])
            p.inc(inst, "P")
            p.wait(v, "P")
            p.inc(v.tensor_copy(out=xb1[:, j * D:(j + 1) * D],
                                in_=ps_t[jb][:, :]), "V")
            evac_marks[jb] = p.cnt["V"]
            p.wait(g, "V")
            for dc in range(ND):
                p.st(dstd[dc * 128:(dc + 1) * 128, tt * 128:(tt + 1) * 128],
                     xb1[:, j * D + dc * 128:j * D + (dc + 1) * 128],
                     sem=ssem)
        iter_mark = p.cnt["V"]
    p.bar()
    p.wait(g, "S")
    p.wait(g, "S2")
    p.wait(g, "L")
def build_stage_f0():
    p = Prog("f0")
    io = {}
    io["xn0fm"] = p.dram("xn0fm", [D, T], BF16, "ExternalInput")
    io["wrr"] = p.dram("wrr", [128, ND * NE], F32, "ExternalInput")
    io["w1"] = p.dram("w1", [NE, D, FF], F32, "ExternalInput")
    io["b1r"] = p.dram("b1r", [128, NE * (FF // 128)], F32, "ExternalInput")
    io["w2"] = p.dram("w2", [NE, FF, D], F32, "ExternalInput")
    io["b2"] = p.dram("b2", [NE, D], F32, "ExternalInput")
    io["x0"] = p.dram("x0", [T, D], F32, "ExternalInput")
    io["s0"] = p.dram("s0", [T, D], F32, "ExternalInput")
    io["x1"] = p.dram("x1", [T, D], F32, "ExternalOutput")
    io["xn1"] = p.dram("xn1", [T, D], F32, "ExternalOutput")
    io["xn1sfm"] = p.dram("xn1sfm", [D, T], BF16, "ExternalOutput")
    io["xhatfm"] = p.dram("xhatfm", [D, T], BF16, "ExternalOutput")
    io["xn1fm"] = p.dram("xn1fm", [D, T], BF16, "ExternalOutput")
    p.emit_consts()
    p.push_scope()
    emit_stage_f0(p, io)
    p.pop_scope()
    return p.done()



S_UPTO = 4
S_DT_MODE = 2


def emit_stage_s(p, io):
    nc = p.nc
    v, s, t, g = nc.vector, nc.scalar, nc.tensor, nc.gpsimd
    xn1sfm = io["xn1sfm"]
    iprojT = io["iprojT"]
    cwr = io["cwr"]
    cbr = io["cbr"]
    xpT = io["xpT"]
    dtwT = io["dtwT"]
    dtbr = io["dtbr"]
    xc_d = io["xc"]
    dtm_d = io["dtm"]
    bm_d = io["bm"]
    sres_d = io["sres"]


    XW = T + 3
    xn1 = p.sbuf("xn1", [128, ND * T], BF16)
    xi_sb = p.sbuf("xi_sb", [128, NDI * XW], BF16)
    xc_sb = p.sbuf("xc_sb", [128, NDI * T], BF16)
    wcol = p.sbuf("wcol", [128, 2 * ND * 128], BF16)
    dtcol = p.sbuf("dtcol", [128, 2 * NDI * 128], BF16)
    small = p.sbuf("small", [128, NDI * KC + 3 * NDI], F32)
    xp_sb = p.sbuf("xp_sb", [128, NDI * NS], BF16)
    srt = p.sbuf("srt", [128, 2 * 512], BF16)
    cacc = p.sbuf("cacc", [128, 2 * T], F32)
    ut = p.sbuf("ut", [128, 2 * 512], F32)
    at = p.sbuf("at", [128, 2 * 512], F32)
    dtt = p.sbuf("dtt", [128, 2 * 512], F32)
    bt_o = p.sbuf("bt_o", [64, 2 * 512], BF16)
    ps_m = [p.psum(f"s_ps{i}", [128, 512], F32) for i in range(2)]

    cw_v = small[:, 0:NDI * KC]
    cb_v = small[:, NDI * KC:NDI * KC + NDI]
    dtb_v = small[:, NDI * KC + NDI:NDI * KC + 2 * NDI]

    for dc in range(ND):
        p.ld(xn1[:, dc * T:(dc + 1) * T], xn1sfm[dc * 128:(dc + 1) * 128, :])
    p.ld(cw_v, cwr[:, :])
    p.ld(cb_v, cbr[:, :])
    p.ld(dtb_v, dtbr[:, :])
    p.ld(xp_sb[:, :].rearrange("p (c n) -> p c n", c=NDI),
         xpT.rearrange("(c p) n -> p c n", p=128))
    for cc in range(NDI):
        inst = g.memset(xi_sb[:, cc * XW:cc * XW + 3], 0.0)
    p.inc(inst, "G")
    p.bar()

    ev_marks = [0, 0]
    ev_sems = [None, None]
    wrd_marks = [0, 0]
    for och in range(2 * DI // 128):
        pb = och % 2
        lsem = "L" if pb == 0 else "L2"
        p.wait(nc.sync, "P", wrd_marks[pb])
        p.ld(wcol[:, pb * 1024:(pb + 1) * 1024]
             .rearrange("p (c o) -> p c o", c=ND),
             iprojT.rearrange("(c p) o -> p c o", p=128)
             [:, :, och * 128:(och + 1) * 128], sem=lsem)
        p.wait(t, lsem)
        for tch in range(4):
            hb = tch % 2
            if ev_sems[hb] is not None:
                p.wait(t, ev_sems[hb], ev_marks[hb])
            for dch in range(ND):
                inst = t.matmul(
                    ps_m[hb][:, :],
                    wcol[:, pb * 1024 + dch * 128:pb * 1024 + (dch + 1) * 128],
                    xn1[:, dch * T + tch * 512:dch * T + (tch + 1) * 512],
                    start=(dch == 0), stop=(dch == ND - 1))
            p.inc(inst, "P")
            if tch == 3:
                wrd_marks[pb] = p.cnt["P"]
            if och < NDI:
                p.wait(v, "P")
                p.inc(v.tensor_copy(
                    out=xi_sb[:, och * XW + 3 + tch * 512:
                              och * XW + 3 + (tch + 1) * 512],
                    in_=ps_m[hb][:, :]), "V")
                ev_marks[hb] = p.cnt["V"]
                ev_sems[hb] = "V"
            else:
                oi = och - NDI
                sb = (och * 4 + tch) % 2
                p.wait(s, "P")
                p.wait(s, "S" if sb == 0 else "S2")
                p.inc(s.activation(out=srt[:, sb * 512:(sb + 1) * 512],
                                   in_=ps_m[hb][:, :], func=AF.Silu), "A")
                ev_marks[hb] = p.cnt["A"]
                ev_sems[hb] = "A"
                p.wait(g, "A")
                p.st(sres_d[oi * 128:(oi + 1) * 128,
                            tch * 512:(tch + 1) * 512],
                     srt[:, sb * 512:(sb + 1) * 512],
                     sem="S" if sb == 0 else "S2")
    p.bar()
    if S_UPTO < 2:
        p.wait(g, "S")
        p.wait(g, "S2")
    

    for och in range(NDI):
        cb_ = och % 2
        cv = cacc[:, cb_ * T:(cb_ + 1) * T]
        p.wait(v, "A")
        p.inc(v.tensor_scalar(out=cv, in0=xi_sb[:, och * XW:och * XW + T],
                              scalar1=cw_v[:, och * KC:och * KC + 1],
                              scalar2=cb_v[:, och:och + 1],
                              op0=ALU.mult, op1=ALU.add), "V")
        for j in range(1, KC):
            p.wait(v, "V")
            p.inc(v.scalar_tensor_tensor(
                out=cv, in0=xi_sb[:, och * XW + j:och * XW + j + T],
                scalar=cw_v[:, och * KC + j:och * KC + j + 1], in1=cv,
                op0=ALU.mult, op1=ALU.add), "V")
        p.wait(s, "V")
        p.inc(s.activation(out=xc_sb[:, och * T:(och + 1) * T], in_=cv,
                           func=AF.Silu), "A")
        p.wait(g, "A")
        p.st(xc_d[och * 128:(och + 1) * 128, :],
             xc_sb[:, och * T:(och + 1) * T],
             sem="S" if cb_ == 0 else "S2")
    p.bar()
    if S_UPTO < 3:
        p.wait(g, "S")
        p.wait(g, "S2")
    

    bev_marks = [0, 0]
    for tch in range(4):
        hb = tch % 2
        p.wait(t, "V", bev_marks[hb])
        for cch in range(NDI):
            inst = t.matmul(ps_m[hb][0:64, :],
                            xp_sb[:, cch * NS:(cch + 1) * NS],
                            xc_sb[:, cch * T + tch * 512:
                                  cch * T + (tch + 1) * 512],
                            start=(cch == 0), stop=(cch == NDI - 1))
        p.inc(inst, "P")
        p.wait(v, "P")
        p.wait(v, "S" if hb == 0 else "S2")
        p.inc(v.tensor_copy(out=bt_o[0:64, hb * 512:(hb + 1) * 512],
                            in_=ps_m[hb][0:64, :]), "V")
        bev_marks[hb] = p.cnt["V"]
        p.wait(g, "V")
        p.st(bm_d[:, tch * 512:(tch + 1) * 512],
             bt_o[0:64, hb * 512:(hb + 1) * 512],
             sem="S" if hb == 0 else "S2")
    p.bar()
    if S_UPTO < 4:
        p.wait(g, "S")
        p.wait(g, "S2")
    

    wrd_marks = [0, 0]
    spv_marks = [0, 0]
    spa_marks = [0, 0]
    for och in range(NDI):
        pb = och % 2
        lsem = "L" if pb == 0 else "L2"
        p.wait(nc.sync, "P", wrd_marks[pb])
        p.ld(dtcol[:, pb * T:(pb + 1) * T]
             .rearrange("p (c o) -> p c o", c=NDI),
             dtwT.rearrange("(c p) o -> p c o", p=128)
             [:, :, och * 128:(och + 1) * 128], sem=lsem)
        p.wait(t, lsem)
        for tch in range(4):
            hb = tch % 2
            p.wait(t, "V", spv_marks[hb])
            p.wait(t, "A", spa_marks[hb])
            for cch in range(NDI):
                inst = t.matmul(
                    ps_m[hb][:, :],
                    dtcol[:, pb * T + cch * 128:pb * T + (cch + 1) * 128],
                    xc_sb[:, cch * T + tch * 512:cch * T + (tch + 1) * 512],
                    start=(cch == 0), stop=(cch == NDI - 1))
            p.inc(inst, "P")
            if tch == 3:
                wrd_marks[pb] = p.cnt["P"]
            if S_DT_MODE == 0:
                p.wait(v, "P")
                p.wait(v, "V", spv_marks[hb])
                p.wait(v, "S" if hb == 0 else "S2")
                p.inc(v.tensor_copy(out=dtt[:, hb * 512:(hb + 1) * 512],
                                    in_=ps_m[hb][:, :]), "V")
                spv_marks[hb] = p.cnt["V"]
                spa_marks[hb] = 0
                p.wait(g, "V")
                p.st(dtm_d[och * 128:(och + 1) * 128,
                           tch * 512:(tch + 1) * 512],
                     dtt[:, hb * 512:(hb + 1) * 512],
                     sem="S" if hb == 0 else "S2")
                continue
            p.wait(v, "P")
            p.inc(v.tensor_scalar(out=ut[:, hb * 512:(hb + 1) * 512],
                                  in0=ps_m[hb][:, :],
                                  scalar1=dtb_v[:, och:och + 1], scalar2=0.0,
                                  op0=ALU.add, op1=ALU.max), "V")
            p.wait(s, "P")
            p.wait(s, "V")
            p.inc(s.activation(out=at[:, hb * 512:(hb + 1) * 512],
                               in_=ps_m[hb][:, :], func=AF.Abs,
                               bias=dtb_v[:, och:och + 1]), "A")
            spa_marks[hb] = p.cnt["A"]
            p.wait(s, "A")
            p.inc(s.activation(out=at[:, hb * 512:(hb + 1) * 512],
                               in_=at[:, hb * 512:(hb + 1) * 512],
                               func=AF.Exp, scale=p.C(0)), "A")
            p.wait(s, "A")
            p.inc(s.activation(out=at[:, hb * 512:(hb + 1) * 512],
                               in_=at[:, hb * 512:(hb + 1) * 512],
                               func=AF.Ln, bias=p.C(1)), "A")
            p.wait(v, "A")
            p.wait(v, "V")
            p.wait(v, "S" if hb == 0 else "S2")
            p.inc(v.tensor_tensor(out=dtt[:, hb * 512:(hb + 1) * 512],
                                  in0=ut[:, hb * 512:(hb + 1) * 512],
                                  in1=at[:, hb * 512:(hb + 1) * 512],
                                  op=ALU.add), "V")
            spv_marks[hb] = p.cnt["V"]
            p.wait(g, "V")
            p.st(dtm_d[och * 128:(och + 1) * 128, tch * 512:(tch + 1) * 512],
                 dtt[:, hb * 512:(hb + 1) * 512],
                 sem="S" if hb == 0 else "S2")
    p.bar()
    p.wait(g, "S")
    p.wait(g, "S2")
def build_stage_s():
    p = Prog("s")
    io = {}
    io["xn1sfm"] = p.dram("xn1sfm", [D, T], BF16, "ExternalInput")
    io["iprojT"] = p.dram("iprojT", [D, 2 * DI], BF16, "ExternalInput")
    io["cwr"] = p.dram("cwr", [128, NDI * KC], F32, "ExternalInput")
    io["cbr"] = p.dram("cbr", [128, NDI], F32, "ExternalInput")
    io["xpT"] = p.dram("xpT", [DI, NS], BF16, "ExternalInput")
    io["dtwT"] = p.dram("dtwT", [DI, DI], BF16, "ExternalInput")
    io["dtbr"] = p.dram("dtbr", [128, NDI], F32, "ExternalInput")
    io["xc"] = p.dram("xc", [DI, T], BF16, "ExternalOutput")
    io["dtm"] = p.dram("dtm", [DI, T], F32, "ExternalOutput")
    io["bm"] = p.dram("bm", [NS, T], BF16, "ExternalOutput")
    io["sres"] = p.dram("sres", [DI, T], BF16, "ExternalOutput")
    p.emit_consts()
    p.push_scope()
    emit_stage_s(p, io)
    p.pop_scope()
    return p.done()



def gen_c_scan(p, io):
    nc = p.nc
    p.prefix = "c1_"
    v, s, t, g = nc.vector, nc.scalar, nc.tensor, nc.gpsimd
    xc_d, dtm_d, bm_d = io["xc"], io["dtm"], io["bm"]
    asb_d, dpr, sres_d = io["asb"], io["dpr"], io["sres"]
    y2_d = io["y2"]

    TQ = 512
    NB = 8
    NBK = NS // NB
    NTQ = T // TQ
    asb = p.sbuf("asb_s", [128, NDI * NS], F32)
    dp_sb = p.sbuf("dp_sb", [128, NDI], F32)
    ones1 = p.sbuf("ones1", [1, 128], BF16)
    bt_sb = p.sbuf("bt_sb", [64, TQ], BF16)
    brow = p.sbuf("brow", [1, TQ], BF16)
    bbc = p.sbuf("bbc", [128, NB * TQ], BF16)
    dt_t = p.sbuf("dt_t", [128, 2 * TQ], F32)
    xc_t = p.sbuf("xc_t", [128, 2 * TQ], BF16)
    sr_t = p.sbuf("sr_t", [128, 2 * TQ], BF16)
    ad_t = p.sbuf("ad_t", [128, 3 * TQ], F32)
    bx_t = p.sbuf("bx_t", [128, 4 * TQ], BF16)
    h_t = p.sbuf("h_t", [128, 2 * NB * TQ], BF16)
    carry = p.sbuf("carry", [128, NDI * NS], BF16)
    y_acc = p.sbuf("y_acc", [128, NDI * TQ], BF16)
    y2_t = p.sbuf("y2_t", [128, TQ], BF16)
    ps_y = p.psum("ps_y", [128, TQ], F32)
    ps_c = p.psum("ps_c", [128, NS], F32)
    ps_b = p.psum("ps_b", [128, TQ], F32)

    p.ld(asb[:, :], asb_d[:, :], sem="L5")
    p.ld(dp_sb[:, :], dpr[:, :], sem="L5")
    p.wait(v, "L5")
    p.inc(v.memset(ones1[0:1, :], 1.0), "V")
    prol_mark = p.cnt["V"]
    admk = [0, 0, 0]
    amk = [0, 0, 0]
    bxmk = [0, 0, 0, 0]
    bxw = [0, 0, 0, 0]
    hfree = [0, 0]
    gidx = 0
    ldg_v = [0, 0]
    ldg_g = [0, 0]
    ldg_a = [0, 0]
    ldg_a_self = [0, 0]
    yev_mark = [0]
    deferred = []

    def emit_deferred():
        if not deferred:
            return
        info = deferred.pop(0)
        tq, nb_, dch, db_, smark, hbase = (
            info["tq"], info["nb"], info["dch"], info["db"], info["smark"],
            info["hbase"])
        p.wait(t, "V", smark)
        p.wait(t, "V", yev_mark[0])
        for n_ in range(NB):
            n = nb_ * NB + n_
            hv = h_t[:, (hbase + n_) * TQ:(hbase + n_ + 1) * TQ]
            inst = t.matmul(ps_y[:, :], p.ident[:, :], hv,
                            start=(n_ == 0), stop=(n_ == NB - 1))
            if tq < NTQ - 1:
                inst = t.matmul(ps_c[:, n:n + 1], p.ident[:, :],
                                hv[:, TQ - 1:TQ], start=(n_ == 0),
                                stop=(n_ == NB - 1))
        p.inc(inst, "P")
        pmark = p.cnt["P"]
        p.wait(v, "P", pmark)
        yv = y_acc[:, dch * TQ:(dch + 1) * TQ]
        if nb_ == 0:
            p.inc(v.tensor_copy(out=yv, in_=ps_y[:, :]), "V")
        else:
            p.inc(v.tensor_tensor(out=yv, in0=ps_y[:, :], in1=yv,
                                  op=ALU.add), "V")
        if tq < NTQ - 1:
            p.wait(v, "V")
            p.inc(v.tensor_copy(
                out=carry[:, dch * NS + nb_ * NB:dch * NS + (nb_ + 1) * NB],
                in_=ps_c[:, nb_ * NB:(nb_ + 1) * NB]), "V")
        yev_mark[0] = p.cnt["V"]
        if nb_ == NBK - 1:
            xcv = info["xcv"]
            srv = info["srv"]
            yscr = dt_t[:, db_ * TQ:(db_ + 1) * TQ]
            p.wait(v, "V")
            p.wait(v, "A", ldg_a_self[db_])
            p.inc(v.scalar_tensor_tensor(
                out=yscr, in0=xcv, scalar=dp_sb[:, dch:dch + 1],
                in1=yv, op0=ALU.mult, op1=ALU.add), "V")
            p.wait(v, "V")
            ssem = "S" if db_ == 0 else "S2"
            p.wait(v, "S")
            p.wait(v, "S2")
            p.inc(v.tensor_tensor(out=y2_t[:, :], in0=yscr,
                                  in1=srv, op=ALU.mult), "V")
            p.wait(g, "V")
            p.st(y2_d[dch * 128:(dch + 1) * 128,
                      tq * TQ:(tq + 1) * TQ], y2_t[:, :], sem=ssem)
        ldg_v[db_] = p.cnt["V"]
        ldg_g[db_] = p.cnt["G"]
        ldg_a[db_] = p.cnt["A"]
        hfree[info["hpar"]] = p.cnt["P"]

    yield
    for tq in range(NTQ):
        p.wait(nc.sync, "V", prol_mark)
        p.wait(nc.sync, "L5")
        p.ld(bt_sb[0:64, :], bm_d[:, tq * TQ:(tq + 1) * TQ], sem="L5")
        for nb_ in range(NBK):
            bev = 0
            first_ev = True
            for n_ in range(NB):
                p.wait(nc.sync, "L5")
                p.wait(nc.sync, "P")
                p.inc(nc.sync.dma_start(
                    out=brow[0:1, :],
                    in_=bt_sb[nb_ * NB + n_:nb_ * NB + n_ + 1, :]),
                    "L5", 16)
                p.wait(t, "L5")
                p.wait(t, "V", bev)
                inst = t.matmul(ps_b[:, :], ones1[0:1, :], brow[0:1, :],
                                start=True, stop=True)
                p.inc(inst, "P")
                p.wait(v, "P")
                if first_ev:
                    p.wait(v, "V", p.cnt["V"])
                    first_ev = False
                p.inc(v.tensor_copy(out=bbc[:, n_ * TQ:(n_ + 1) * TQ],
                                    in_=ps_b[:, :]), "V")
                bev = p.cnt["V"]
            emit_deferred()
            yield
            for dch in range(NDI):
                db_ = dch % 2
                lsem = "L3" if db_ == 0 else "L4"
                p.wait(nc.sync, "V", ldg_v[db_])
                p.wait(nc.sync, "G", ldg_g[db_])
                p.wait(nc.sync, "A", ldg_a[db_])
                p.ld(dt_t[:, db_ * TQ:(db_ + 1) * TQ],
                     dtm_d[dch * 128:(dch + 1) * 128,
                           tq * TQ:(tq + 1) * TQ], sem=lsem)
                p.ld(xc_t[:, db_ * TQ:(db_ + 1) * TQ],
                     xc_d[dch * 128:(dch + 1) * 128,
                          tq * TQ:(tq + 1) * TQ], sem=lsem)
                if nb_ == NBK - 1:
                    p.ld(sr_t[:, db_ * TQ:(db_ + 1) * TQ],
                         sres_d[dch * 128:(dch + 1) * 128,
                                tq * TQ:(tq + 1) * TQ], sem=lsem)
                p.wait(s, lsem)
                p.wait(v, lsem)
                dtv = dt_t[:, db_ * TQ:(db_ + 1) * TQ]
                xcv = xc_t[:, db_ * TQ:(db_ + 1) * TQ]
                srv = sr_t[:, db_ * TQ:(db_ + 1) * TQ]
                hpar = gidx % 2
                gidx += 1
                hbase = hpar * NB
                p.wait(v, "P", hfree[hpar])
                for n_ in range(NB + 2):
                    if n_ < NB:
                        bxb = n_ % 4
                        bxv = bx_t[:, bxb * TQ:(bxb + 1) * TQ]
                        p.wait(v, "V", bxmk[bxb])
                        p.inc(v.tensor_tensor(
                            out=bxv, in0=xcv,
                            in1=bbc[:, n_ * TQ:(n_ + 1) * TQ],
                            op=ALU.mult), "V")
                        bxw[bxb] = p.cnt["V"]
                        adb = n_ % 3
                        adv = ad_t[:, adb * TQ:(adb + 1) * TQ]
                        col = dch * NS + nb_ * NB + n_
                        p.wait(s, "V", admk[adb])
                        p.inc(s.activation(out=adv, in_=dtv, func=AF.Exp,
                                           scale=asb[:, col:col + 1]), "A")
                        amk[adb] = p.cnt["A"]
                    if n_ >= 2:
                        m_ = n_ - 2
                        bxb = m_ % 4
                        adb = m_ % 3
                        hv = h_t[:, (hbase + m_) * TQ:
                                 (hbase + m_ + 1) * TQ]
                        col = dch * NS + nb_ * NB + m_
                        p.wait(v, "A", amk[adb])
                        p.wait(v, "V", bxw[bxb])
                        init = 0.0 if tq == 0 else carry[:, col:col + 1]
                        p.inc(v.tensor_tensor_scan(
                            out=hv,
                            data0=ad_t[:, adb * TQ:(adb + 1) * TQ],
                            data1=bx_t[:, bxb * TQ:(bxb + 1) * TQ],
                            initial=init, op0=ALU.mult, op1=ALU.add), "V")
                        admk[adb] = p.cnt["V"]
                        bxmk[bxb] = p.cnt["V"]
                ldg_a_self[db_] = p.cnt["A"]
                deferred.append(dict(tq=tq, nb=nb_, dch=dch, db=db_,
                                     smark=p.cnt["V"], hbase=hbase,
                                     hpar=hpar, xcv=xcv, srv=srv))
                emit_deferred() if len(deferred) > 1 else None
                yield
    emit_deferred()
    p.bar()


def emit_c_outproj(p, io):
    nc = p.nc
    p.prefix = "c2_"
    v, s, t, g = nc.vector, nc.scalar, nc.tensor, nc.gpsimd
    opT, xn1_d, s1_d, y2_d = io["opT"], io["xn1"], io["s1"], io["y2"]
    opt_sb = p.sbuf("opt_sb", [128, NDI * D], BF16)
    y2l = p.sbuf("y2l", [128, 2 * NDI * 128], BF16)
    xnt = p.sbuf("xnt", [128, 2 * D], F32)
    s1t = p.sbuf("s1t", [128, 2 * D], F32)
    ps_b2 = [p.psum(f"ps_op{i}", [128, 512], F32) for i in range(2)]
    p.ld(opt_sb[:, :].rearrange("p (c o) -> p c o", c=NDI),
         opT.rearrange("(c p) o -> p c o", p=128))
    p.bar()
    opev = [0, 0]
    ldg = [0, 0]
    for t128 in range(NT):
        pb = t128 % 2
        lsem = "L" if pb == 0 else "L2"
        ssem = "S" if pb == 0 else "S2"
        p.wait(nc.sync, "V", ldg[pb])
        for cch in range(NDI):
            p.ld(y2l[:, pb * DI + cch * 128:pb * DI + (cch + 1) * 128],
                 y2_d[cch * 128:(cch + 1) * 128,
                      t128 * 128:(t128 + 1) * 128], sem=lsem)
        p.ld(xnt[:, pb * D:(pb + 1) * D],
             xn1_d[t128 * 128:(t128 + 1) * 128, :], sem=lsem)
        p.wait(t, lsem)
        for oh in range(2):
            hb = (t128 * 2 + oh) % 2
            p.wait(t, "V", opev[hb])
            for cch in range(NDI):
                inst = t.matmul(
                    ps_b2[hb][:, :],
                    y2l[:, pb * DI + cch * 128:pb * DI + (cch + 1) * 128],
                    opt_sb[:, cch * D + oh * 512:cch * D + (oh + 1) * 512],
                    start=(cch == 0), stop=(cch == NDI - 1))
            p.inc(inst, "P")
            p.wait(v, "P")
            p.wait(v, ssem)
            p.inc(v.tensor_tensor(
                out=s1t[:, pb * D + oh * 512:pb * D + (oh + 1) * 512],
                in0=ps_b2[hb][:, :],
                in1=xnt[:, pb * D + oh * 512:pb * D + (oh + 1) * 512],
                op=ALU.add), "V")
            opev[hb] = p.cnt["V"]
            p.wait(g, "V")
            p.st(s1_d[t128 * 128:(t128 + 1) * 128,
                      oh * 512:(oh + 1) * 512],
                 s1t[:, pb * D + oh * 512:pb * D + (oh + 1) * 512],
                 sem=ssem)
        ldg[pb] = p.cnt["V"]
    p.bar()


def emit_f1_prologue(p, io, bufs):
    nc = p.nc
    p.prefix = "f1_"
    v, s, t, g = nc.vector, nc.scalar, nc.tensor, nc.gpsimd
    xn1fm, wgr = io["xn1fm"], io["wgr"]
    gr, br, b1r = io["gr"], io["br"], io["b1r"]
    normed = bufs["normed"] = p.sbuf("normed", [128, ND * T], BF16)
    acc1 = bufs["acc1"]
    bufs["acc2"] = p.sbuf("acc2", [128, NT * D], BF16)
    bufs["stg"] = bufs["tmpc"]
    bufs["w1bf"] = p.sbuf("w1bf", [128, 4096], BF16)
    bufs["w2bf"] = p.sbuf("w2bf", [128, 4096], BF16)
    bufs["h_sb"] = p.sbuf("h_sb", [128, 4 * T], BF16)
    wg_sb = bufs["wg_sb"] = p.sbuf("wg_sb", [128, ND * NE], BF16)
    gb_sb = bufs["gb_sb"] = p.sbuf("gb_sb", [128, 2 * NE * ND], F32)
    bufs["b1_sb"] = p.sbuf("b1_sb", [128, NE * (FF // 128)], F32)
    bufs["b2row"] = p.sbuf("b2row", [1, D], BF16)
    bufs["b2b"] = p.sbuf("b2b", [128, D], BF16)
    bufs["ones1"] = p.sbuf("ones1f", [1, 128], BF16)
    logit_sb = bufs["logit_sb"] = p.sbuf("logit_sb", [128, NT * NE], F32)
    cw_sb = bufs["cw_sb"]
    sw_sb = bufs["sw_sb"]
    lst = bufs["lst"] = p.sbuf("lst", [128, 8], F32)
    sp8 = bufs["sp8"] = p.sbuf("sp8", [128, NE], F32)
    ps_h = bufs["ps_h"] = [p.psum(f"ps_h{i}", [128, 512], F32)
                           for i in range(2)]
    bufs["ps_o"] = [p.psum(f"ps_o{i}", [128, 512], F32) for i in range(2)]
    bufs["ps_bc"] = p.psum("ps_bc", [128, 512], F32)

    for dc in range(ND):
        p.ld(normed[:, dc * T:(dc + 1) * T],
             xn1fm[dc * 128:(dc + 1) * 128, :])
    p.ld(bufs["stg"][:, 0:ND * NE], wgr[:, :])
    p.ld(bufs["b1_sb"][:, :], b1r[:, :])
    p.ld(gb_sb[:, 0:NE * ND], gr[:, :])
    p.ld(gb_sb[:, NE * ND:2 * NE * ND], br[:, :])
    v.memset(bufs["ones1"][0:1, :], 1.0)
    p.wait(v, "L")
    p.inc(v.tensor_copy(out=wg_sb[:, :], in_=bufs["stg"][:, 0:ND * NE]),
          "V")
    p.bar()
    rev_marks = [0, 0]
    for t128 in range(NT):
        pb = t128 % 2
        p.wait(t, "V", rev_marks[pb])
        for dch in range(ND):
            inst = t.matmul(ps_h[pb][:, 0:NE],
                            normed[:, dch * T + t128 * 128:
                                   dch * T + (t128 + 1) * 128],
                            wg_sb[:, dch * NE:(dch + 1) * NE],
                            start=(dch == 0), stop=(dch == ND - 1))
        p.inc(inst, "P")
        p.wait(v, "P")
        p.inc(v.tensor_copy(out=logit_sb[:, t128 * NE:(t128 + 1) * NE],
                            in_=ps_h[pb][:, 0:NE]), "V")
        rev_marks[pb] = p.cnt["V"]
    for t128 in range(NT):
        lg = logit_sb[:, t128 * NE:(t128 + 1) * NE]
        cwv = cw_sb[:, t128 * NE:(t128 + 1) * NE]
        p.wait(v, "V")
        p.inc(v.reduce_max(lst[:, 0:1], lg, axis=AX), "V")
        p.wait(v, "V")
        p.inc(v.tensor_scalar(out=cwv, in0=lg, scalar1=lst[:, 0:1],
                              scalar2=None, op0=ALU.is_equal), "V")
        p.wait(v, "V")
        p.inc(v.scalar_tensor_tensor(out=sp8[:, :], in0=cwv, scalar=-1e30,
                                     in1=lg, op0=ALU.mult, op1=ALU.add), "V")
        p.wait(v, "V")
        p.inc(v.reduce_max(lst[:, 1:2], sp8[:, :], axis=AX), "V")
        p.wait(v, "V")
        p.inc(v.tensor_tensor(out=lst[:, 2:3], in0=lst[:, 0:1],
                              in1=lst[:, 1:2], op=ALU.subtract), "V")
        p.wait(s, "V")
        s.activation(out=lst[:, 3:4], in_=lst[:, 2:3], func=AF.Sigmoid)
        p.inc(s.activation(out=lst[:, 4:5], in_=lst[:, 2:3], func=AF.Sigmoid,
                           scale=p.C(0)), "A")
        p.wait(v, "A")
        p.inc(v.tensor_scalar(out=sp8[:, :], in0=sp8[:, :],
                              scalar1=lst[:, 1:2], scalar2=None,
                              op0=ALU.is_equal), "V")
        p.wait(v, "V")
        p.inc(v.tensor_scalar(out=cwv, in0=cwv, scalar1=lst[:, 3:4],
                              scalar2=None, op0=ALU.mult), "V")
        p.wait(v, "V")
        p.inc(v.scalar_tensor_tensor(out=cwv, in0=sp8[:, :],
                                     scalar=lst[:, 4:5], in1=cwv,
                                     op0=ALU.mult, op1=ALU.add), "V")
        p.inc(v.tensor_tensor(out=sw_sb[:, t128:t128 + 1], in0=lst[:, 3:4],
                              in1=lst[:, 4:5], op=ALU.add), "V")
        p.wait(v, "V")
    p.bar()
    return bufs


def gen_f1_experts(p, io, bufs):
    nc = p.nc
    v, s, t, g = nc.vector, nc.scalar, nc.tensor, nc.gpsimd
    xhatfm, w1, w2, b2 = io["xhatfm"], io["w1"], io["w2"], io["b2"]
    normed, acc1, acc2 = bufs["normed"], bufs["acc1"], bufs["acc2"]
    stg, w1bf, w2bf = bufs["stg"], bufs["w1bf"], bufs["w2bf"]
    h_sb = bufs["h_sb"]
    gb_sb, b1_sb = bufs["gb_sb"], bufs["b1_sb"]
    b2row, b2b = bufs["b2row"], bufs["b2b"]
    ones1, cw_sb, tmpc = bufs["ones1"], bufs["cw_sb"], bufs["tmpc"]
    ps_h, ps_o, ps_bc = bufs["ps_h"], bufs["ps_o"], bufs["ps_bc"]
    cast_marks = [0, 0]
    hps_marks = [0, 0]
    ops_marks = [0, 0]
    w2mm_mark = 0
    hmm_mark = 0
    accv_mark = 0
    bcev_mark = 0
    nrm_mark = 0
    yield
    for e in range(NE):
        p.wait(t, "V")
        p.wait(nc.sync, "P", w2mm_mark)
        for dc in range(ND):
            p.wait(nc.sync, "V", nrm_mark)
            p.ld(normed[:, dc * T:(dc + 1) * T],
                 xhatfm[dc * 128:(dc + 1) * 128, :], sem="L")
            p.wait(v, "L")
            p.wait(v, "P", w2mm_mark)
            p.inc(v.tensor_scalar(
                out=normed[:, dc * T:(dc + 1) * T],
                in0=normed[:, dc * T:(dc + 1) * T],
                scalar1=gb_sb[:, e * ND + dc:e * ND + dc + 1],
                scalar2=gb_sb[:, NE * ND + e * ND + dc:
                              NE * ND + e * ND + dc + 1],
                op0=ALU.mult, op1=ALU.add), "V")
            nrm_mark = p.cnt["V"]
        p.wait(g, "V", bcev_mark)
        p.wait(g, "V", nrm_mark)
        p.inc(g.dma_start(out=b2row[0:1, :], in_=b2[e, :][None, :]),
              "G2", 16)
        for j in range(2):
            p.wait(t, "G2")
            p.wait(t, "V", bcev_mark)
            inst = t.matmul(ps_bc[:, :], ones1[0:1, :],
                            b2row[0:1, j * 512:(j + 1) * 512],
                            start=True, stop=True)
            p.inc(inst, "P")
            p.wait(v, "P")
            p.inc(v.tensor_copy(out=b2b[:, j * 512:(j + 1) * 512],
                                in_=ps_bc[:, :]), "V")
            bcev_mark = p.cnt["V"]
        yield
        for f8 in range(8):
            pb = (e * 8 + f8) % 2
            lsem = "L" if pb == 0 else "L2"
            wv1 = w1bf[:, :]
            wv2 = w2bf[:, :]
            p.wait(nc.sync, "P", hmm_mark)
            if f8 == 0:
                p.wait(nc.sync, "V", nrm_mark)
            else:
                p.wait(nc.sync, "P", w2mm_mark)
            p.ld(wv1[:, :].rearrange("p (c f) -> p c f", c=ND),
                 w1[e].rearrange("(c p) f -> p c f", p=128)
                 [:, :, f8 * 512:(f8 + 1) * 512], sem=lsem)
            p.ld(wv2[:, :].rearrange("p (c d) -> p c d", c=4),
                 w2[e].rearrange("(c p) d -> p c d", p=128)
                 [:, f8 * 4:(f8 + 1) * 4, :], sem=lsem)
            p.wait(t, lsem)
            p.wait(t, "P", w2mm_mark)
            p.wait(t, "V", nrm_mark)
            for ft in range(4):
                ftg = f8 * 4 + ft
                for tch in range(4):
                    hb = (ft * 4 + tch) % 2
                    p.wait(t, "A", hps_marks[hb])
                    for dch in range(ND):
                        inst = t.matmul(
                            ps_h[hb][:, :],
                            wv1[:, dch * 512 + ft * 128:
                                dch * 512 + (ft + 1) * 128],
                            normed[:, dch * T + tch * 512:
                                   dch * T + (tch + 1) * 512],
                            start=(dch == 0), stop=(dch == ND - 1))
                    p.inc(inst, "P")
                    p.wait(s, "P")
                    p.inc(s.activation(
                        out=h_sb[:, ft * T + tch * 512:
                                 ft * T + (tch + 1) * 512],
                        in_=ps_h[hb][:, :], func=AF.Relu,
                        bias=b1_sb[:, e * 32 + ftg:e * 32 + ftg + 1]), "A")
                    hps_marks[hb] = p.cnt["A"]
                    if tch == 3:
                        yield
            hre_mark = p.cnt["A"]
            hmm_mark = p.cnt["P"]
            for t128 in range(NT):
                if t128 % 4 == 0:
                    yield
                for dh in range(2):
                    ob = (t128 * 2 + dh) % 2
                    p.wait(t, "A", hre_mark)
                    p.wait(t, "A", ops_marks[ob])
                    dst = acc2[:, t128 * D + dh * 512:
                               t128 * D + (dh + 1) * 512]
                    first = True
                    if f8 > 0:
                        inst = t.matmul(ps_o[ob][:, :], p.ident[:, :], dst,
                                        start=True, stop=False)
                        first = False
                    for ft in range(4):
                        inst = t.matmul(
                            ps_o[ob][:, :],
                            h_sb[:, ft * T + t128 * 128:
                                 ft * T + (t128 + 1) * 128],
                            wv2[:, ft * D + dh * 512:ft * D + (dh + 1) * 512],
                            start=(first and ft == 0), stop=(ft == 3))
                    p.inc(inst, "P")
                    p.wait(s, "P")
                    p.inc(s.activation(out=dst, in_=ps_o[ob][:, :],
                                       func=AF.Copy), "A")
                    ops_marks[ob] = p.cnt["A"]
            accv_mark = p.cnt["V"]
            w2mm_mark = p.cnt["P"]
            cast_marks[pb] = p.cnt["P"]
            yield
        p.wait(v, "V", accv_mark)
        p.wait(v, "A")
        for t128 in range(NT):
            a2 = acc2[:, t128 * D:(t128 + 1) * D]
            a0 = acc1[:, t128 * D:(t128 + 1) * D]
            scc = cw_sb[:, t128 * NE + e:t128 * NE + e + 1]
            p.inc(v.tensor_tensor(out=tmpc[:, :], in0=a2, in1=b2b[:, :],
                                  op=ALU.add), "V")
            p.wait(v, "V")
            if e == 0:
                inst = v.tensor_scalar(out=a0, in0=tmpc[:, :], scalar1=scc,
                                       scalar2=None, op0=ALU.mult)
            else:
                inst = v.scalar_tensor_tensor(out=a0, in0=tmpc[:, :],
                                              scalar=scc, in1=a0,
                                              op0=ALU.mult, op1=ALU.add)
            p.inc(inst, "V")
            p.wait(v, "V")
        yield


def emit_f1_final(p, io, bufs):
    nc = p.nc
    p.prefix = "f2_"
    v, s, t, g = nc.vector, nc.scalar, nc.tensor, nc.gpsimd
    xn1_d, x1_d, s1_d, out_d = io["xn1"], io["x1"], io["s1"], io["out"]
    acc1, sw_sb, tmpc = bufs["acc1"], bufs["sw_sb"], bufs["tmpc"]
    xnt = p.sbuf("f_xn", [128, D], F32)
    x1t = p.sbuf("f_x1", [128, D], F32)
    s1t = p.sbuf("f_s1", [128, D], F32)
    gt_ = p.sbuf("f_g", [128, D], F32)
    ot = p.sbuf("f_o", [128, D], F32)
    yt = tmpc
    iter_mark = 0
    for tt in range(NT):
        ssem = "S" if tt % 2 == 0 else "S2"
        p.wait(nc.sync, "V", iter_mark)
        p.ld(xnt[:, :], xn1_d[tt * 128:(tt + 1) * 128, :])
        p.ld(x1t[:, :], x1_d[tt * 128:(tt + 1) * 128, :])
        p.ld(s1t[:, :], s1_d[tt * 128:(tt + 1) * 128, :])
        ldm = p.cnt["L"]
        p.wait(v, "L", ldm)
        p.wait(v, "A")
        p.inc(v.scalar_tensor_tensor(out=yt[:, :], in0=xnt[:, :],
                                     scalar=sw_sb[:, tt:tt + 1],
                                     in1=acc1[:, tt * D:(tt + 1) * D],
                                     op0=ALU.mult, op1=ALU.add), "V")
        p.wait(s, "V")
        p.inc(s.activation(out=gt_[:, :], in_=yt[:, :], func=AF.Silu), "A")
        p.wait(v, "A")
        p.inc(v.tensor_tensor(out=gt_[:, :], in0=gt_[:, :], in1=s1t[:, :],
                              op=ALU.mult), "V")
        p.wait(v, "V")
        p.wait(v, "S")
        p.wait(v, "S2")
        p.inc(v.tensor_tensor(out=ot[:, :], in0=x1t[:, :], in1=gt_[:, :],
                              op=ALU.add), "V")
        iter_mark = p.cnt["V"]
        p.wait(g, "V")
        p.st(out_d[tt * 128:(tt + 1) * 128, :], ot[:, :], sem=ssem)
    p.bar()
    p.wait(g, "S")
    p.wait(g, "S2")


def emit_stage_cf1(p, ioc, iof1):
    nc = p.nc
    g = nc.gpsimd
    p.prefix = "f1p_"
    bufs = dict(
        acc1=p.sbuf("acc1", [128, NT * D], BF16),
        cw_sb=p.sbuf("cw_sb", [128, NT * NE], F32),
        sw_sb=p.sbuf("sw_sb", [128, NT], F32),
        tmpc=p.sbuf("tmpc", [128, D], F32),
    )
    p.push_scope()
    emit_f1_prologue(p, iof1, bufs)
    gc = gen_c_scan(p, ioc)
    gf = gen_f1_experts(p, iof1, bufs)
    next(gc)
    next(gf)
    INTERLEAVE = globals().get("CF1_INTERLEAVE", True)
    done_c = done_f = False
    while not (done_c and done_f):
        if not done_c:
            try:
                next(gc)
            except StopIteration:
                done_c = True
        if not done_f and (INTERLEAVE or done_c):
            try:
                next(gf)
            except StopIteration:
                done_f = True
    for sm in ("S", "S2", "L", "L2", "L3", "L4", "L5"):
        p.wait(g, sm)
    p.bar()
    p.pop_scope()
    emit_c_outproj(p, ioc)
    emit_f1_final(p, iof1, bufs)


def build_fused():
    p = Prog("fused")
    EI, EO = "ExternalInput", "ExternalOutput"
    d = p.dram
    iop = dict(
        x0=d("x0", [T, D], F32, EI), wq=d("wq", [D, D], F32, EI),
        wk=d("wk", [D, D], F32, EI), wv=d("wv", [D, D], F32, EI),
        wo=d("wo", [D, D], F32, EI), ipw=d("ipw", [2 * DI, D], F32, EI),
        nw=d("nw", [D], F32, EI), dtw=d("dtw", [DI, DI], F32, EI),
        xpw2=d("xpw2", [NS, DI], F32, EI), opw=d("opw", [D, DI], F32, EI),
        alog=d("alog", [DI, NS], F32, EI),
        wqT=d("wqT", [D, D], BF16), wkT=d("wkT", [D, D], BF16),
        wvT=d("wvT", [D, D], BF16), woT=d("woT", [D, D], BF16),
        iprojT=d("iprojT", [D, 2 * DI], BF16), dtwT=d("dtwT", [DI, DI], BF16),
        xpT=d("xpT", [DI, NS], BF16), opT=d("opT", [DI, D], BF16),
        asb=d("asb", [128, NDI * NS], F32), xn0fm=d("xn0fm", [D, T], BF16),
        w1_0=d("w1_0", [NE, D, FF], F32, EI),
        w2_0=d("w2_0", [NE, FF, D], F32, EI),
        w1_1=d("w1_1", [NE, D, FF], F32, EI),
        w2_1=d("w2_1", [NE, FF, D], F32, EI),
        w1b_0=d("w1b_0", [NE * D * FF], BF16),
        w2b_0=d("w2b_0", [NE * FF * D], BF16),
        w1b_1=d("w1b_1", [NE * D * FF], BF16),
        w2b_1=d("w2b_1", [NE * FF * D], BF16),
    )
    ioa = dict(
        xn0fm=iop["xn0fm"], wqT=iop["wqT"], wkT=iop["wkT"], wvT=iop["wvT"],
        woT=iop["woT"], bq=d("bq", [D], F32, EI), bk=d("bk", [D], F32, EI),
        bv=d("bv", [D], F32, EI), bo=d("bo", [D], F32, EI),
        scale128=d("scale128", [128], F32, EI), s0=d("s0", [T, D], F32),
    )
    iof0 = dict(
        xn0fm=iop["xn0fm"], wrr=d("wrr", [128, ND * NE], F32, EI),
        w1=iop["w1b_0"].rearrange("(e d f) -> e d f", e=NE, d=D),
        b1r=d("b1r_0", [128, NE * (FF // 128)], F32, EI),
        w2=iop["w2b_0"].rearrange("(e f dd) -> e f dd", e=NE, f=FF),
        b2=d("b2_0", [NE, D], F32, EI),
        x0=iop["x0"], s0=ioa["s0"], x1=d("x1", [T, D], F32),
        xn1=d("xn1", [T, D], F32), xn1sfm=d("xn1sfm", [D, T], BF16),
        xhatfm=d("xhatfm", [D, T], BF16), xn1fm=d("xn1fm", [D, T], BF16),
    )
    ios = dict(
        xn1sfm=iof0["xn1sfm"], iprojT=iop["iprojT"],
        cwr=d("cwr", [128, NDI * KC], F32, EI),
        cbr=d("cbr", [128, NDI], F32, EI), xpT=iop["xpT"],
        dtwT=iop["dtwT"], dtbr=d("dtbr", [128, NDI], F32, EI),
        xc=d("xc", [DI, T], BF16), dtm=d("dtm", [DI, T], F32),
        bm=d("bm", [NS, T], BF16), sres=d("sres", [DI, T], BF16),
    )
    ioc = dict(
        xc=ios["xc"], dtm=ios["dtm"], bm=ios["bm"], asb=iop["asb"],
        dpr=d("dpr", [128, NDI], F32, EI), sres=ios["sres"], opT=iop["opT"],
        xn1=iof0["xn1"], s1=d("s1", [T, D], F32),
        y2=d("y2", [DI, T], BF16),
    )
    iof1 = dict(
        xn1fm=iof0["xn1fm"], xhatfm=iof0["xhatfm"],
        wgr=d("wgr", [128, ND * NE], F32, EI),
        gr=d("gr", [128, NE * ND], F32, EI),
        br=d("br", [128, NE * ND], F32, EI),
        w1=iop["w1b_1"].rearrange("(e d f) -> e d f", e=NE, d=D),
        b1r=d("b1r_1", [128, NE * (FF // 128)], F32, EI),
        w2=iop["w2b_1"].rearrange("(e f dd) -> e f dd", e=NE, f=FF),
        b2=d("b2_1", [NE, D], F32, EI),
        xn1=iof0["xn1"], x1=iof0["x1"], s1=ioc["s1"],
        out=d("out", [T, D], F32, EO),
    )
    p.emit_consts()
    g = p.nc.gpsimd
    for emit, io in ((emit_stage_p, iop), (emit_stage_a, ioa),
                     (emit_stage_f0, iof0), (emit_stage_s, ios)):
        p.push_scope()
        p.prefix = emit.__name__.replace("emit_stage_", "") + "_"
        emit(p, io)
        p.pop_scope()
        for sm in ("S", "S2", "L", "L2"):
            p.wait(g, sm)
        p.bar()
    p.push_scope()
    emit_stage_cf1(p, ioc, iof1)
    p.pop_scope()
    return p.done()


_progs = {}


def _get_prog(name, builder):
    if name not in _progs:
        _progs[name] = builder()
    return _progs[name]


def run_stage(name, builder, in_maps):
    nc = _get_prog(name, builder)
    res = run_bass_kernel_spmd(nc, in_maps, core_ids=list(range(8)))
    return res.results


def _host_layouts(inp):
    asc = np.ascontiguousarray
    cw = inp["L1_attn_conv_w"].reshape(DI, KC)

    def pcol(a, c):
        return asc(np.asarray(a).reshape(c, 128).T)

    return dict(
        xpw2=asc(inp["L1_attn_x_proj_w"][NS:, :]),
        wrr=asc(inp["L0_ffn_Wr"].reshape(ND, 128, NE).transpose(1, 0, 2)
                .reshape(128, ND * NE)),
        b1r_0=asc(inp["L0_ffn_b1"].reshape(NE, 32, 128).transpose(2, 0, 1)
                  .reshape(128, NE * 32)),
        cwr=asc(cw.reshape(NDI, 128, KC).transpose(1, 0, 2)
                .reshape(128, NDI * KC)),
        cbr=pcol(inp["L1_attn_conv_b"], NDI),
        dtbr=pcol(inp["L1_attn_dt_b"], NDI),
        dpr=pcol(inp["L1_attn_D_param"], NDI),
        wgr=asc(inp["L1_ffn_Wg"].reshape(ND, 128, NE).transpose(1, 0, 2)
                .reshape(128, ND * NE)),
        gr=asc(inp["L1_ffn_gamma"].reshape(NE, ND, 128).transpose(2, 0, 1)
               .reshape(128, NE * ND)),
        br=asc(inp["L1_ffn_beta"].reshape(NE, ND, 128).transpose(2, 0, 1)
               .reshape(128, NE * ND)),
        b1r_1=asc(inp["L1_ffn_b1"].reshape(NE, 32, 128).transpose(2, 0, 1)
                  .reshape(128, NE * 32)),
        scale128=np.repeat(np.asarray(inp["L0_attn_attn_scale"]), 128),
    )


def kernel(**inp):
    B = 8
    asc = np.ascontiguousarray
    hl = _host_layouts(inp)
    x_all = asc(inp["L0_attn_x"])
    maps = [dict(
        x0=asc(x_all[b]),
        wq=inp["L0_attn_Wq"], wk=inp["L0_attn_Wk"],
        wv=inp["L0_attn_Wv"], wo=inp["L0_attn_Wo"],
        ipw=inp["L1_attn_in_proj_w"], nw=inp["L1_attn_norm_w"],
        dtw=inp["L1_attn_dt_w"], opw=inp["L1_attn_out_proj_w"],
        alog=inp["L1_attn_A_log"],
        bq=inp["L0_attn_bq"], bk=inp["L0_attn_bk"],
        bv=inp["L0_attn_bv"], bo=inp["L0_attn_bo"],
        w1_0=inp["L0_ffn_W1"], w2_0=inp["L0_ffn_W2"], b2_0=inp["L0_ffn_b2"],
        w1_1=inp["L1_ffn_W1"], w2_1=inp["L1_ffn_W2"], b2_1=inp["L1_ffn_b2"],
        **hl,
    ) for b in range(B)]
    res = run_stage("fused", build_fused, maps)
    return np.stack([np.asarray(res[b]["out"], np.float32) for b in range(B)])
